# version 06
# baseline (speedup 1.0000x reference)
import math
import os
import contextlib
import numpy as np
import ml_dtypes

import concourse.bass as bass
import concourse.mybir as mybir
from concourse.bass_utils import run_bass_kernel_spmd
from concourse.bass_core_rust import IndirectOffsetOnAxis

F32 = mybir.dt.float32
BF16 = mybir.dt.bfloat16
I32 = mybir.dt.int32
AF = mybir.ActivationFunctionType
ALU = mybir.AluOpType
AX = mybir.AxisListType

D = 1024
SC = 2048
NT = SC // 128
NH = 16
HD = 64
NE = 8
HM = 2048
DH = 4096
NCH = D // 128
KROWS = NH * 65
VBASE = 2 * KROWS
STATROW = VBASE + SC
KVROWS = 4224
SK = 2 * SC
NSK = SK // 128
RSQ = float((1.0 / 8.0) ** 0.5)
EPS = 1e-5

DEBUG = ()
KPH = int(os.environ.get("KPH", "99"))
_PHC = [0]


def _ph_on():
    _PHC[0] += 1
    return _PHC[0] <= KPH

DEBUG_SHAPES = {
    "xn": (SC, D),
    "gate": (128, NT * NE),
    "attnout": (128, NCH * SC),
}


class Plan:
    ENGS = ("sync", "tensor", "scalar", "vector", "gpsimd")
    QUEUES = {"dS": 6, "dC": 4, "dW": 4, "dG": 6}

    def __init__(self, nc, block, es):
        self.nc = nc
        self.block = block
        self.cnt = {}
        self.sems = {}
        for name in ("cc", "te", "sc", "ve", "gp"):
            self.sems[name] = es.enter_context(nc.semaphore(name))
            self.cnt[name] = 0
        for eng in self.ENGS:
            self.sems["bar_" + eng] = es.enter_context(
                nc.semaphore("bar_" + eng))
        self.dlane = {}
        self.dcnt = {}
        self.dmap = {}
        for q, nl in self.QUEUES.items():
            self.cnt[q] = 0
            self.dlane[q] = [es.enter_context(nc.semaphore("%s%d" % (q, i)))
                             for i in range(nl)]
            self.dcnt[q] = [0] * nl
            self.dmap[q] = {}
        self.nbar = 0

    def inc(self, name, n=1):
        self.cnt[name] += n
        return self.sems[name]

    def v(self, name):
        return self.cnt[name]

    def fence(self, sec):
        for q in self.QUEUES:
            sec.w(q, self.cnt[q])
        for s in ("te", "sc", "ve"):
            sec.w(s, self.cnt[s])

    def queue_wait_ops(self, q, pseudo):
        if pseudo <= 0:
            return []
        snap = self.dmap[q].get(pseudo)
        if snap is None:
            keys = [k for k in self.dmap[q] if k >= pseudo]
            if keys:
                snap = self.dmap[q][min(keys)]
            else:
                snap = tuple(self.dcnt[q])
        return [(self.dlane[q][i], v) for i, v in enumerate(snap) if v > 0]

    def barrier(self):
        self.nbar += 1
        tgt = self.nbar
        dwaits = []
        for q in self.QUEUES:
            dwaits.extend((h, v) for h, v in
                          [(self.dlane[q][i], self.dcnt[q][i])
                           for i in range(len(self.dlane[q]))] if v > 0)
        for eng in self.ENGS:
            own = self.sems["bar_" + eng]
            others = [self.sems["bar_" + x] for x in self.ENGS if x != eng]
            seq_only = eng == "sync"
            def _(e, _t=tgt, _own=own, _o=tuple(others), _dw=tuple(dwaits),
                  _sq=seq_only):
                for h, v in _dw:
                    e.wait_ge(h, v)
                if _sq:
                    e.sem_inc(_own, 1)
                else:
                    e.drain().then_inc(_own, 1)
                for h in _o:
                    e.wait_ge(h, _t)
            getattr(self.block, eng)(_)
class Sec:
    def __init__(self, plan, eng):
        self.p = plan
        self.eng = eng
        self.ops = []

    def __enter__(self):
        return self

    def __exit__(self, et, ev, tb):
        if et is not None:
            return False
        ops = self.ops
        getattr(self.p.block, self.eng)(lambda e: [f(e) for f in ops])
        return False

    def w(self, sem, val):
        if not val or val <= 0:
            return
        if sem in self.p.QUEUES:
            for h, v in self.p.queue_wait_ops(sem, val):
                self.ops.append(lambda e, s=h, vv=v: e.wait_ge(s, vv))
        else:
            self.ops.append(
                lambda e, s=self.p.sems[sem], v=val: e.wait_ge(s, v))

    def o(self, fn, inc=None, n=1):
        if inc is None:
            self.ops.append(fn)
            return None
        if inc in self.p.QUEUES:
            return self._dma_o(fn, inc)
        sem = self.p.inc(inc, n)
        tgt = self.p.v(inc)

        def g(e, _f=fn, _s=sem, _n=n):
            _f(e).then_inc(_s, _n)
        self.ops.append(g)
        return tgt

    def _dma_o(self, fn, q):
        p = self.p
        li = (p.cnt[q] // 16) % len(p.dlane[q])
        prev = p.dcnt[q][li]
        if prev > 0:
            self.ops.append(
                lambda e, s=p.dlane[q][li], v=prev: e.wait_ge(s, v))
        p.dcnt[q][li] += 16
        p.cnt[q] += 16
        p.dmap[q][p.cnt[q]] = tuple(p.dcnt[q])
        sem = p.dlane[q][li]

        def g(e, _f=fn, _s=sem):
            _f(e).then_inc(_s, 16)
        self.ops.append(g)
        return p.cnt[q]

    def dma(self, out, in_, inc="dS", **kw):
        return self.o(lambda e, o=out, i=in_, k=kw:
                      e.dma_start(out=o, in_=i, **k), inc=inc, n=16)

    def drain(self):
        self.ops.append(lambda e: e.drain())



def _butterfly_coefs(w):
    ns = w.shape[0]
    i = np.arange(DH)
    wdiag = np.empty((ns, DH), np.float32)
    woff = np.empty((ns, DH), np.float32)
    for s in range(ns):
        s2 = 1 << s
        k = (i // (2 * s2)) * s2 + (i % s2)
        is_b = ((i // s2) % 2).astype(bool)
        wdiag[s] = np.where(is_b, w[s, k, 1, 1], w[s, k, 0, 0])
        woff[s] = np.where(is_b, w[s, k, 1, 0], w[s, k, 0, 1])
    woffx = np.empty_like(woff)
    for s in range(ns):
        woffx[s] = woff[s][i ^ (1 << s)]

    def jc(a):
        return np.ascontiguousarray(
            a.reshape(ns, 32, 128).transpose(0, 2, 1)).astype(np.float32)
    return jc(wdiag), jc(woff), jc(woffx)


def _host_prepare(inp):
    T = lambda a: np.ascontiguousarray(np.asarray(a, np.float32).T)
    C = lambda a: np.ascontiguousarray(np.asarray(a, np.float32))
    chunks = lambda a, n: np.ascontiguousarray(C(a).reshape(n, 128).T)
    rep = lambda a: np.ascontiguousarray(
        np.broadcast_to(C(a)[None, :], (128, np.asarray(a).shape[-1])))

    sh = {}
    sh["wqT"] = T(inp["L0_attn_Wq"]); sh["wkT"] = T(inp["L0_attn_Wk"])
    sh["wvT"] = T(inp["L0_attn_Wv"]); sh["woT"] = T(inp["L0_attn_Wo"])
    sh["bq_r"] = chunks(inp["L0_attn_bq"], NCH)
    sh["bk_r"] = chunks(inp["L0_attn_bk"], NCH)
    sh["bv_rep"] = rep(inp["L0_attn_bv"])
    sh["bo_rep"] = rep(inp["L0_attn_bo"])
    sh["w1"] = C(inp["L0_ffn_w1"]); sh["w2"] = C(inp["L0_ffn_w2"])
    sh["w3"] = C(inp["L0_ffn_w3"])
    sh["b1_r"] = np.stack([chunks(inp["L0_ffn_b1"][e], HM // 128)
                           for e in range(NE)])
    sh["b2_r"] = np.stack([chunks(inp["L0_ffn_b2"][e], HM // 128)
                           for e in range(NE)])
    sh["b3"] = C(inp["L0_ffn_b3"])
    sh["gate_w"] = C(inp["L0_ffn_gate_w"])
    sh["gate_b_rep"] = rep(inp["L0_ffn_gate_b"])
    sh["qw"] = C(inp["L1_attn_qw"]); sh["kw"] = C(inp["L1_attn_kw"])
    sh["vw"] = C(inp["L1_attn_vw"]); sh["ow"] = C(inp["L1_attn_ow"])
    sh["wexp_sw"] = np.ascontiguousarray(
        C(inp["L1_ffn_W_exp"]).reshape(8, 128, 8, 512)
        .transpose(2, 1, 0, 3).reshape(8, 128, 4096))
    sh["bexp_r"] = chunks(inp["L1_ffn_b_exp"], DH // 128)
    sh["upb_r"] = chunks(inp["L1_ffn_up_bias"], DH // 128)
    sh["wcon_sw"] = np.ascontiguousarray(
        C(inp["L1_ffn_W_con"]).reshape(32, 128, 4, 256)
        .transpose(2, 1, 0, 3).reshape(4, 128, 8192))
    sh["bcon_rep"] = rep(inp["L1_ffn_b_con"])
    sh["dbias_rep"] = rep(inp["L1_ffn_down_bias"])
    for pre, w in (("up", inp["L1_ffn_up_weights"]),
                   ("dn", inp["L1_ffn_down_weights"])):
        wd, wo, wox = _butterfly_coefs(C(w))
        sh[pre + "_wd"], sh[pre + "_wo"], sh[pre + "_wox"] = wd, wo, wox

    sh["eye"] = np.eye(128, dtype=np.float32)
    mx = np.zeros((7, 128, 128), np.float32)
    j = np.arange(128)
    for s in range(7):
        mx[s, j, j ^ (1 << s)] = 1
    sh["maskxor"] = mx
    dm = np.zeros((4, 128, 512), np.float32)
    pc = np.arange(128)[:, None]
    qc = np.arange(512)[None, :]
    for v in range(4):
        dm[v] = ((v * 128 + pc) <= qc)
    sh["diagmask"] = dm

    x_full = np.asarray(inp["L0_attn_x"], np.float32)
    inv = (10000.0 ** (-np.arange(0, HD, 2, dtype=np.float32) / HD)
           ).astype(np.float32)
    per_core = []
    for c in range(8):
        b, hf = c // 2, c % 2
        d = {}
        d["x0"] = np.ascontiguousarray(x_full[b, hf * SC:(hf + 1) * SC, :])
        d["kill"] = np.full((128, 1), 0.0 if hf == 1 else -1e9, np.float32)
        d["pidx"] = np.ascontiguousarray(
            (((c ^ 1) % 2) * KVROWS + np.arange(KVROWS))
            .astype(np.int32).reshape(KVROWS // 128, 128).T)
        pos = np.arange(hf * SC, (hf + 1) * SC, dtype=np.float32)
        f = pos[:, None] * inv[None, :]
        d["cos8"] = np.ascontiguousarray(
            np.tile(np.cos(f) * RSQ, (1, NH)).astype(np.float32))
        d["sin8"] = np.ascontiguousarray(
            np.tile(np.sin(f) * RSQ, (1, NH)).astype(np.float32))
        per_core.append(d)
    return sh, per_core



def _build(debug=()):
    nc = bass.Bass("TRN2", target_bir_lowering=False)
    A = {}

    def di(name, shape, dtype=F32):
        A[name] = nc.dram_tensor(name, list(shape), dtype,
                                 kind="ExternalInput").ap()

    di("x0", (SC, D))
    for n in ("wqT", "wkT", "wvT", "woT", "qw", "kw", "vw", "ow"):
        di(n, (D, D))
    di("bq_r", (128, NCH)); di("bk_r", (128, NCH))
    di("bv_rep", (128, D)); di("bo_rep", (128, D))
    di("w1", (NE, D, HM)); di("w2", (NE, HM, HM)); di("w3", (NE, HM, D))
    di("b1_r", (NE, 128, HM // 128)); di("b2_r", (NE, 128, HM // 128))
    di("b3", (NE, D))
    di("gate_w", (D, NE)); di("gate_b_rep", (128, NE))
    di("wexp_sw", (8, 128, 4096)); di("bexp_r", (128, DH // 128))
    di("upb_r", (128, DH // 128))
    di("wcon_sw", (4, 128, 8192))
    di("bcon_rep", (128, D)); di("dbias_rep", (128, D))
    for pre in ("up", "dn"):
        di(pre + "_wd", (12, 128, 32)); di(pre + "_wo", (12, 128, 32))
        di(pre + "_wox", (12, 128, 32))
    di("eye", (128, 128))
    di("maskxor", (7, 128, 128))
    di("diagmask", (4, 128, 512))
    di("kill", (128, 1))
    di("pidx", (128, KVROWS // 128), I32)
    di("cos8", (SC, NH * 32)); di("sin8", (SC, NH * 32))

    out_x = nc.dram_tensor("out_x", [SC, D], F32, kind="ExternalOutput").ap()
    dbg = {}
    for name in debug:
        dbg[name] = nc.dram_tensor("dbg_" + name, list(DEBUG_SHAPES[name]),
                                   F32, kind="ExternalOutput").ap()

    _kd = os.environ.get("KDBG") == "1"
    _kind = dict(kind="ExternalOutput") if _kd else {}
    x1_dram = nc.dram_tensor("x1_dram", [SC, D], F32, **_kind).ap()
    moe_dram = nc.dram_tensor("moe_dram", [SC, D], F32, **_kind).ap()
    qfm_dram = nc.dram_tensor("qfm_dram", [D, SC], BF16).ap()
    wexp_bf = nc.dram_tensor("wexp_bf", [8, 128, 4096], BF16).ap()
    wcon_bf = nc.dram_tensor("wcon_bf", [4, 128, 8192], BF16).ap()
    kv_in_t = nc.dram_tensor("kv_in", [KVROWS, 1024], BF16)
    kv_ga_t = nc.dram_tensor("kv_gather", [2, KVROWS, 1024], BF16)
    kv_pa_t = nc.dram_tensor("kv_partner", [KVROWS, 1024], BF16)
    kvi = kv_in_t.ap()
    kvp = kv_pa_t.ap()
    kviK = kvi.rearrange("(r two) w -> r (two w)", two=2)[0:KROWS, :]
    kvpK = kvp.rearrange("(r two) w -> r (two w)", two=2)[0:KROWS, :]
    kvgaflat = kv_ga_t.ap().rearrange("s r w -> (s r) w")

    es = contextlib.ExitStack()
    _uid = [0]

    def sb(name, shape, dtype=F32, stack=None):
        _uid[0] += 1
        return (stack or es).enter_context(
            nc.sbuf_tensor("%s_u%d" % (name, _uid[0]), list(shape), dtype))

    with nc.Block() as block:
        p = Plan(nc, block, es)

        G = {}
        G["ones_col"] = sb("ones_col", (1, 64), BF16)
        G["ones64"] = sb("ones64", (64, 1), BF16)
        G["eye_t"] = sb("eye_t", (128, 128), BF16)
        G["stats"] = sb("stats", (128, 112), F32)
        G["scr_bf"] = sb("scr_bf", (128, 1024), BF16)
        G["kill_t"] = sb("kill_t", (128, 1), F32)
        G["gate_all"] = sb("gate_all", (128, NT, NE), F32)
        G["gate_eq"] = sb("gate_eq", (128, NT, NE), F32)
        G["gateb_t"] = sb("gateb_t", (128, NE), F32)
        G["wT_sb"] = sb("wT_sb", (8, NT * 128), BF16)
        G["b3_sb"] = sb("b3_sb", (8, D), BF16)
        G["pidx_t"] = sb("pidx_t", (128, KVROWS // 128), I32)
        G["Mup"] = sb("Mup", (128, 32 * 128), BF16)
        G["Mdn"] = sb("Mdn", (128, 32 * 128), BF16)
        G["kmax"] = sb("kmax", (2, 8), F32)
        G["kmax_bf"] = sb("kmax_bf", (2, 8), BF16)
        G["krow"] = sb("krow", (2, 8), F32)

        G["psA"] = es.enter_context(nc.psum_tensor("psA", [128, 512], F32))
        G["psB"] = es.enter_context(nc.psum_tensor("psB", [128, 512], F32))
        G["psC"] = es.enter_context(nc.psum_tensor("psC", [128, 512], F32))
        G["psD"] = es.enter_context(nc.psum_tensor("psD", [128, 512], F32))
        G["psQ"] = es.enter_context(nc.psum_tensor("psQ", [128, 2048], F32))

        with Sec(p, "sync") as s:
            s.dma(G["kill_t"][:], A["kill"])
            s.dma(G["gateb_t"][:], A["gate_b_rep"])
            s.dma(G["pidx_t"][:], A["pidx"])
        with Sec(p, "gpsimd") as g:
            g.dma(G["b3_sb"][:], A["b3"], "dG")
            g.dma(G["eye_t"][:], A["eye"], "dG")
        with Sec(p, "vector") as v:
            v.o(lambda e: e.memset(G["ones_col"][:], 1.0))
            v.o(lambda e: e.memset(G["ones64"][:], 1.0))
            zq = v.o(lambda e: e.memset(G["scr_bf"][:], 0.0), inc="ve")
        with Sec(p, "sync") as s:
            s.w("ve", zq)
            s.dma(kvi[STATROW:KVROWS, :], G["scr_bf"][0:KVROWS - STATROW, :])
        p.barrier()

        rp_sem = es.enter_context(nc.semaphore("rp"))
        with Sec(p, "gpsimd") as g:
            for i in range(8):
                g.ops.append(lambda e, i2=i: e.dma_start(
                    out=wexp_bf[i2], in_=A["wexp_sw"][i2]).then_inc(
                        rp_sem, 16))
            for i in range(4):
                g.ops.append(lambda e, i2=i: e.dma_start(
                    out=wcon_bf[i2], in_=A["wcon_sw"][i2]).then_inc(
                        rp_sem, 16))

        ctx = dict(A=A, G=G, kvi=kvi, kvp=kvp, kviK=kviK, kvpK=kvpK,
                   kvgaflat=kvgaflat, kv_in_t=kv_in_t, kv_ga_t=kv_ga_t,
                   moe_dram=moe_dram, qfm_dram=qfm_dram, sb=sb, dbg=dbg,
                   wexp_bf=wexp_bf, wcon_bf=wcon_bf, rp_sem=rp_sem)
        for L in range(2):
            _emit_layer(p, ctx, L,
                        x_in=(A["x0"] if L == 0 else x1_dram),
                        x_out=(x1_dram if L == 0 else out_x))

        with Sec(p, "sync") as s:
            s.w("dS", p.v("dS"))
            s.w("dG", p.v("dG"))
            s.w("dW", p.v("dW"))
            s.w("dC", p.v("dC"))

    es.close()
    return nc


def _emit_layer(p, ctx, L, x_in, x_out):
    sb = ctx["sb"]
    les = contextlib.ExitStack()
    xnT = sb("xnT%d" % L, (128, NCH, SC), BF16, les)

    if _ph_on():
        _p1_ln(p, ctx, L, x_in, xnT)
    p.barrier()
    if _ph_on():
        _p2_kv(p, ctx, L, xnT)
    if L == 1 and _ph_on():
        _p2_rope(p, ctx, xnT)
    if L == 0 and _ph_on():
        _p2_gate(p, ctx, xnT)
    if _ph_on():
        _p2_collective(p, ctx)
        ctx["cc_on"] = True
    else:
        ctx["cc_on"] = False
    if L == 0 and _ph_on():
        _mbuild(p, ctx)
    p.barrier()
    if ctx["cc_on"] and _ph_on():
        _p3_partner(p, ctx)
    if _ph_on():
        if L == 0:
            _p3_moe(p, ctx, xnT)
        else:
            _p3_ffn(p, ctx, xnT)
    p.barrier()
    pes = contextlib.ExitStack()
    attnout = sb("attnout%d" % L, (128, NCH, SC), BF16, pes)
    if ctx["cc_on"] and _ph_on():
        _p4_attn(p, ctx, L, xnT, attnout, pes)
    p.barrier()
    if _ph_on():
        _p5_residual(p, ctx, L, attnout, x_in, x_out)
    p.barrier()
    pes.close()
    les.close()



def _p1_ln(p, ctx, L, x_in, xnT):
    A, G, sb = ctx["A"], ctx["G"], ctx["sb"]
    st = G["stats"]
    SUM, SQ, MEAN, VAR, SD, RSTD, NM = 0, 16, 32, 48, 64, 80, 96
    ph = contextlib.ExitStack()
    xb = [sb("p1x%d_%d" % (L, i), (128, D), F32, ph) for i in range(3)]
    xnb = [sb("p1n%d_%d" % (L, i), (128, D), BF16, ph) for i in range(2)]

    ve_done = [0] * NT
    sc_done = [0] * NT
    with Sec(p, "sync") as s, Sec(p, "vector") as v, Sec(p, "scalar") as c:
        for t in range(NT):
            if t >= 3:
                s.w("ve", ve_done[t - 3])
                s.w("sc", sc_done[t - 3])
            dt = s.dma(xb[t % 3][:], x_in[t * 128:(t + 1) * 128, :])
            v.w("dS", dt)
            ve_done[t] = v.o(
                lambda e, b=xb[t % 3], tt=t:
                e.reduce_sum(st[:, SUM + tt:SUM + tt + 1], b[:], axis=AX.X),
                inc="ve")
            c.w("dS", dt)
            if t >= 1:
                c.w("sc", sc_done[t - 1])
            sc_done[t] = c.o(
                lambda e, b=xb[t % 3], tt=t:
                e.activation(G["scr_bf"][:], b[:], AF.Square,
                             accum_out=st[:, SQ + tt:SQ + tt + 1]), inc="sc")
    with Sec(p, "vector") as v:
        v.w("ve", p.v("ve"))
        v.w("sc", p.v("sc"))
        v.o(lambda e: e.tensor_scalar(st[:, MEAN:MEAN + 16],
                                      st[:, SUM:SUM + 16], 1.0 / D, None,
                                      ALU.mult))
        v.o(lambda e: e.tensor_scalar(st[:, VAR:VAR + 16],
                                      st[:, SQ:SQ + 16], 1.0 / D, None,
                                      ALU.mult))
        v.drain()
        v.o(lambda e: e.tensor_tensor(st[:, SD:SD + 16], st[:, MEAN:MEAN + 16],
                                      st[:, MEAN:MEAN + 16], ALU.mult))
        v.drain()
        v.o(lambda e: e.tensor_tensor(st[:, VAR:VAR + 16], st[:, VAR:VAR + 16],
                                      st[:, SD:SD + 16], ALU.subtract))
        v.drain()
        veq = v.o(lambda e: e.tensor_scalar(st[:, VAR:VAR + 16],
                                            st[:, VAR:VAR + 16], EPS, None,
                                            ALU.add), inc="ve")
    with Sec(p, "scalar") as c:
        c.w("ve", veq)
        scq = c.o(lambda e: e.activation(st[:, SD:SD + 16],
                                         st[:, VAR:VAR + 16], AF.Sqrt),
                  inc="sc")
    with Sec(p, "vector") as v:
        v.w("sc", scq)
        v.o(lambda e: e.reciprocal(st[:, RSTD:RSTD + 16], st[:, SD:SD + 16]))
        v.drain()
        v.o(lambda e: e.tensor_tensor(st[:, NM:NM + 16], st[:, MEAN:MEAN + 16],
                                      st[:, RSTD:RSTD + 16], ALU.mult))
        v.drain()
        stat_q = v.o(lambda e: e.tensor_scalar(st[:, NM:NM + 16],
                                               st[:, NM:NM + 16], -1.0, None,
                                               ALU.mult), inc="ve")

    psv = [G["psA"].ap().bitcast(BF16), G["psB"].ap().bitcast(BF16)]
    tr_done = [0] * NT
    cp_done = [0] * NT
    sc2_done = [0] * NT
    with Sec(p, "sync") as s, Sec(p, "scalar") as c, Sec(p, "tensor") as t_, \
            Sec(p, "vector") as v:
        for t in range(NT):
            if t >= 3:
                s.w("sc", sc2_done[t - 3])
            else:
                s.w("ve", stat_q)
            dt = s.dma(xb[t % 3][:], x_in[t * 128:(t + 1) * 128, :])
            c.w("dS", dt)
            c.w("ve", stat_q)
            if t >= 2:
                c.w("te", tr_done[t - 2])
            sc2_done[t] = c.o(
                lambda e, b=xb[t % 3], n=xnb[t % 2], tt=t:
                e.activation(n[:], b[:], AF.Identity,
                             scale=st[:, RSTD + tt:RSTD + tt + 1],
                             bias=st[:, NM + tt:NM + tt + 1]), inc="sc")
            t_.w("sc", sc2_done[t])
            if t >= 2:
                t_.w("ve", cp_done[t - 2])
            for cc in range(NCH):
                r = t_.o(
                    lambda e, n=xnb[t % 2], c2=cc, pv=psv[t % 2]:
                    e.transpose(pv[:, c2 * 128:(c2 + 1) * 128],
                                n[:, c2 * 128:(c2 + 1) * 128], G["eye_t"][:]),
                    inc="te" if cc == NCH - 1 else None)
                if r:
                    tr_done[t] = r
            v.w("te", tr_done[t])
            cp_done[t] = v.o(
                lambda e, pv=psv[t % 2], tt=t: e.tensor_copy(
                    xnT[:, :, tt * 128:(tt + 1) * 128],
                    pv[:].rearrange("p (c s) -> p c s", s=128)), inc="ve")
            if "xn" in ctx["dbg"] and L == 0:
                s.w("sc", sc2_done[t])
                s.dma(ctx["dbg"]["xn"][t * 128:(t + 1) * 128, :],
                      xnb[t % 2][:])
    p.barrier()
    ph.close()



def _p2_kv(p, ctx, L, xnT):
    A, G, sb = ctx["A"], ctx["G"], ctx["sb"]
    kviK, kvi = ctx["kviK"], ctx["kvi"]
    ph = contextlib.ExitStack()
    wvb = sb("p2wv%d" % L, (128, NCH, D), BF16, ph)
    vstg = [sb("p2vs%d_%d" % (L, i), (128, D), BF16, ph) for i in range(2)]
    ones_row = sb("p2or%d" % L, (1, SC), BF16, ph)
    G["ones_row"] = ones_row
    if L == 0:
        G["sq_bf"] = sb("p2sq%d" % L, (64, SK), BF16, ph)
    with Sec(p, "vector") as v:
        vq_or = v.o(lambda e: e.memset(ones_row[:], 1.0), inc="ve")

    if L == 0:
        wkb = sb("p2wk", (128, NCH, D), BF16, ph)
        bkt = sb("p2bk", (128, NCH), F32, ph)
        bvt = sb("p2bvr", (128, D), F32, ph)
        kstg = [sb("p2ks%d" % i, (128, SC), BF16, ph) for i in range(2)]
        with Sec(p, "gpsimd") as g, Sec(p, "sync") as s:
            p.fence(g)
            wk_t = g.dma(wkb[:],
                         A["wkT"].rearrange("(c p) d -> p c d", p=128), "dW")
            wv_t = g.dma(wvb[:],
                         A["wvT"].rearrange("(c p) d -> p c d", p=128), "dW")
            s.dma(bkt[:], A["bk_r"])
            bv_t = s.dma(bvt[:], A["bv_rep"])
        with Sec(p, "scalar") as c:
            c.w("dS", bv_t)
            bk8 = c.o(lambda e: e.activation(bkt[:], bkt[:], AF.Identity,
                                             scale=RSQ), inc="sc")
        km_last = 0
        with Sec(p, "tensor") as t_, Sec(p, "scalar") as c, \
                Sec(p, "sync") as s, Sec(p, "vector") as v:
            t_.w("dW", wk_t)
            stg_free = [0, 0]
            evac = {}
            first = True
            for pr in range(NH // 2):
                sl = pr % 2
                if pr >= 2:
                    t_.w("dS", stg_free[sl])
                for s4 in range(4):
                    psl = s4 % 2
                    psx = [G["psA"], G["psB"]][psl]
                    t_.w("sc", evac.get((pr * 4 + s4) - 2, 0))
                    for dc in range(NCH):
                        t_.o(lambda e, ps=psx, d=dc, pp=pr, ss=s4:
                             e.matmul(ps[:],
                                      wkb[:, d, pp * 128:(pp + 1) * 128],
                                      xnT[:, d, ss * 512:(ss + 1) * 512],
                                      start=(d == 0), stop=(d == NCH - 1)),
                             inc="te" if dc == NCH - 1 else None)
                    c.w("te", p.v("te"))
                    c.w("sc", bk8)
                    evac[pr * 4 + s4] = c.o(
                        lambda e, ps=psx, pp=pr, ss=s4, k=kstg[sl]:
                        e.activation(k[:, ss * 512:(ss + 1) * 512], ps[:],
                                     AF.Identity, scale=RSQ,
                                     bias=bkt[:, pp:pp + 1]), inc="sc")
                s.w("sc", evac[pr * 4 + 3])
                s.dma(kviK[65 * 2 * pr:65 * 2 * pr + 64, :],
                      kstg[sl][0:64, :])
                stg_free[sl] = s.dma(
                    kviK[65 * (2 * pr + 1):65 * (2 * pr + 1) + 64, :],
                    kstg[sl][64:128, :])
                km_last = _k2_pair(p, ctx, t_, c, v, kstg[sl],
                                   evac[pr * 4 + 3], first)
                first = False
        _k2_finish(p, ctx, km_last)
    else:
        with Sec(p, "gpsimd") as g:
            p.fence(g)
            wv_t = g.dma(wvb[:],
                         A["vw"].rearrange("(c p) d -> p c d", p=128), "dW")

    with Sec(p, "sync") as s:
        s.w("ve", vq_or)
        for h in range(NH):
            s.dma(kviK[65 * h + 64:65 * h + 65, :], G["ones_row"][:])

    with Sec(p, "tensor") as t_, Sec(p, "vector") as v, \
            Sec(p, "scalar") as c, Sec(p, "sync") as s:
        t_.w("dW", wv_t)
        if L == 0:
            t_.w("dS", bv_t)
        vfree = [0, 0]
        vev = [0, 0]
        for t in range(NT):
            sl = t % 2
            psb = G["psQ"]
            if t >= 2:
                t_.w("dS", vfree[sl])
                t_.w("ve" if L == 0 else "sc", vev[sl])
            for dc in range(NCH):
                for n in range(2):
                    t_.o(lambda e, d=dc, tt=t, nn=n, hh=sl:
                         e.matmul(psb[:, hh * 1024 + nn * 512:
                                       hh * 1024 + (nn + 1) * 512],
                                  xnT[:, d, tt * 128:(tt + 1) * 128],
                                  wvb[:, d, nn * 512:(nn + 1) * 512],
                                  start=(d == 0), stop=(d == NCH - 1)),
                         inc="te" if (dc == NCH - 1 and n == 1) else None)
            if L == 0:
                v.w("te", p.v("te"))
                vev[sl] = v.o(lambda e, o=vstg[sl], hh=sl: e.tensor_tensor(
                    o[:], psb[:, hh * 1024:hh * 1024 + 1024], bvt[:],
                    ALU.add), inc="ve")
                s.w("ve", vev[sl])
            else:
                c.w("te", p.v("te"))
                vev[sl] = c.o(lambda e, o=vstg[sl], hh=sl: e.activation(
                    o[:], psb[:, hh * 1024:hh * 1024 + 1024], AF.Identity),
                    inc="sc")
                s.w("sc", vev[sl])
            vfree[sl] = s.dma(
                kvi[VBASE + t * 128:VBASE + (t + 1) * 128, :], vstg[sl][:])
    ph.close()


def _k2_pair(p, ctx, t_, c, v, ktile, ready, first):
    G = ctx["G"]
    sq = G["sq_bf"]
    c.w("sc", ready)
    c.w("te", p.v("te"))
    c.o(lambda e: e.activation(sq[0:64, 0:SC], ktile[0:64, :], AF.Square),
        inc="sc")
    sq2 = c.o(lambda e: e.activation(sq[0:64, SC:2 * SC], ktile[64:128, :],
                                     AF.Square), inc="sc")
    last = 0
    for i in range(8):
        t_.w("sc", sq2)
        t_.w("ve", last or p.v("ve"))
        tq = t_.o(lambda e, ii=i: e.matmul(
            G["psD"][0:1, :], G["ones64"][:],
            sq[0:64, ii * 512:(ii + 1) * 512], start=True, stop=True),
            inc="te")
        v.w("te", tq)
        v.o(lambda e: e.reduce_max(G["krow"][0:1, 0:1], G["psD"][0:1, :],
                                   axis=AX.X))
        v.drain()
        if first and i == 0:
            v.o(lambda e: e.tensor_copy(G["kmax"][0:1, 0:1],
                                        G["krow"][0:1, 0:1]))
        else:
            v.o(lambda e: e.tensor_tensor(G["kmax"][0:1, 0:1],
                                          G["kmax"][0:1, 0:1],
                                          G["krow"][0:1, 0:1], ALU.max))
        v.drain()
        last = v.o(lambda e: e.drain(), inc="ve")
    return last


def _k2_finish(p, ctx, last):
    G, kvi = ctx["G"], ctx["kvi"]
    with Sec(p, "vector") as v, Sec(p, "sync") as s:
        v.w("ve", last)
        vq = v.o(lambda e: e.tensor_scalar(G["kmax_bf"][0:1, 0:1],
                                           G["kmax"][0:1, 0:1], 1.02, None,
                                           ALU.mult), inc="ve")
        s.w("ve", vq)
        s.dma(kvi[STATROW:STATROW + 1, 0:1], G["kmax_bf"][0:1, 0:1])



def _p2_rope(p, ctx, xnT):
    A, G, sb = ctx["A"], ctx["G"], ctx["sb"]
    kviK, kvi, qfm_dram = ctx["kviK"], ctx["kvi"], ctx["qfm_dram"]
    ph = contextlib.ExitStack()
    G["sq_bf"] = sb("r_sq", (64, SK), BF16, ph)
    qwb = sb("r_qw", (128, NCH, D), BF16, ph)
    kwb = sb("r_kw", (128, NCH, D), BF16, ph)
    fmb = sb("r_fm", (128, NCH, SC), BF16, ph)
    tb = [sb("r_tb%d" % i, (128, 1024), F32, ph) for i in range(2)]
    tmp = [sb("r_tmp%d" % i, (128, 512), F32, ph) for i in range(4)]
    rtm = [sb("r_tm%d" % i, (128, D), BF16, ph) for i in range(2)]

    with Sec(p, "gpsimd") as g:
        p.fence(g)
        qw_t = g.dma(qwb[:],
                     A["qw"].rearrange("(c p) d -> p c d", p=128), "dW")
        kw_t = g.dma(kwb[:],
                     A["kw"].rearrange("(c p) d -> p c d", p=128), "dW")

    qstg_done = 0
    for which in range(2):
        wb = qwb if which == 0 else kwb
        wt = qw_t if which == 0 else kw_t
        rope_q = [0, 0]
        cp_q = [0, 0]
        for t in range(NT):
            sl = t % 2
            with Sec(p, "sync") as s:
                if t >= 2:
                    s.w("ve", rope_q[sl])
                s.dma(tb[sl][:, 0:512], A["cos8"][t * 128:(t + 1) * 128, :])
                tbl_t = s.dma(tb[sl][:, 512:1024],
                              A["sin8"][t * 128:(t + 1) * 128, :])
            with Sec(p, "tensor") as t_:
                t_.w("dW", wt)
                if t + which > 0:
                    t_.w("ve", p.v("ve"))
                for dc in range(NCH):
                    for n in range(2):
                        t_.o(lambda e, d=dc, tt=t, nn=n, w2=wb:
                             e.matmul(G["psQ"][:, nn * 512:(nn + 1) * 512],
                                      xnT[:, d, tt * 128:(tt + 1) * 128],
                                      w2[:, d, nn * 512:(nn + 1) * 512],
                                      start=(d == 0), stop=(d == NCH - 1)),
                             inc="te" if (dc == NCH - 1 and n == 1) else None)
            with Sec(p, "vector") as v:
                v.w("te", p.v("te"))
                v.w("dS", tbl_t)
                if t >= 2:
                    v.w("te", cp_q[sl])
                pv = G["psQ"].ap()[:, 0:1024].rearrange(
                    "p (q two) -> p q two", two=2)
                ro = rtm[sl][:].rearrange("p (q two) -> p q two", two=2)
                ct, st_ = tb[sl][:, 0:512], tb[sl][:, 512:1024]
                v.o(lambda e, px=pv, cc=ct: e.tensor_tensor(
                    tmp[0][:], px[:, :, 0], cc, ALU.mult))
                v.o(lambda e, px=pv, cc=st_: e.tensor_tensor(
                    tmp[1][:], px[:, :, 1], cc, ALU.mult))
                v.o(lambda e, px=pv, cc=st_: e.tensor_tensor(
                    tmp[2][:], px[:, :, 0], cc, ALU.mult))
                v.o(lambda e, px=pv, cc=ct: e.tensor_tensor(
                    tmp[3][:], px[:, :, 1], cc, ALU.mult))
                v.drain()
                v.o(lambda e, o=ro: e.tensor_tensor(
                    o[:, :, 0], tmp[0][:], tmp[1][:], ALU.subtract))
                rope_q[sl] = v.o(lambda e, o=ro: e.tensor_tensor(
                    o[:, :, 1], tmp[2][:], tmp[3][:], ALU.add), inc="ve")
            with Sec(p, "tensor") as t_, Sec(p, "vector") as v:
                t_.w("ve", rope_q[sl])
                if which == 1 and t == 0:
                    v.w("dS", qstg_done)
                pbf = G["psC"].ap().bitcast(BF16)
                for c4 in range(2):
                    tq = 0
                    for cc in range(c4 * 4, c4 * 4 + 4):
                        r = t_.o(lambda e, c2=cc, s2=sl: e.transpose(
                            pbf[:, (c2 % 4) * 128:(c2 % 4 + 1) * 128],
                            rtm[s2][:, c2 * 128:(c2 + 1) * 128],
                            G["eye_t"][:]),
                            inc="te" if cc % 4 == 3 else None)
                        if r:
                            tq = r
                    v.w("te", tq)
                    cp_q[sl] = v.o(lambda e, b2=c4 * 4, tt=t: e.tensor_copy(
                        fmb[:, b2:b2 + 4, tt * 128:(tt + 1) * 128],
                        pbf[:, 0:512].rearrange("p (c s) -> p c s", s=128)),
                        inc="ve")
                    t_.w("ve", cp_q[sl])
        if which == 0:
            with Sec(p, "sync") as s:
                s.w("ve", p.v("ve"))
                for cc in range(NCH):
                    qstg_done = s.dma(qfm_dram[cc * 128:(cc + 1) * 128, :],
                                      fmb[:, cc, :])
        else:
            km_last = 0
            with Sec(p, "sync") as s, Sec(p, "tensor") as t_, \
                    Sec(p, "scalar") as c, Sec(p, "vector") as v:
                s.w("ve", p.v("ve"))
                vq = p.v("ve")
                for pr in range(NH // 2):
                    s.dma(kviK[65 * 2 * pr:65 * 2 * pr + 64, :],
                          fmb[0:64, pr, :])
                    s.dma(kviK[65 * (2 * pr + 1):65 * (2 * pr + 1) + 64, :],
                          fmb[64:128, pr, :])
                first = True
                for pr in range(NH // 2):
                    km_last = _k2_pair(p, ctx, t_, c, v, fmb[:, pr, :], vq,
                                       first)
                    first = False
            _k2_finish(p, ctx, km_last)
    ph.close()



def _p2_gate(p, ctx, xnT):
    A, G, sb = ctx["A"], ctx["G"], ctx["sb"]
    ph = contextlib.ExitStack()
    gwb = sb("p2gw", (128, NCH, 8), BF16, ph)
    ga, geq, m1 = G["gate_all"], G["gate_eq"], G["stats"]
    with Sec(p, "gpsimd") as g:
        p.fence(g)
        gw_t = g.dma(gwb[:],
                     A["gate_w"].rearrange("(c p) e -> p c e", p=128), "dW")
    with Sec(p, "tensor") as t_, Sec(p, "vector") as v:
        t_.w("dW", gw_t)
        for t in range(NT):
            if t >= 1:
                t_.w("ve", p.v("ve"))
            for dc in range(NCH):
                t_.o(lambda e, d=dc, tt=t: e.matmul(
                    G["psC"][:, 0:8], xnT[:, d, tt * 128:(tt + 1) * 128],
                    gwb[:, d, :], start=(d == 0), stop=(d == NCH - 1)),
                    inc="te" if dc == NCH - 1 else None)
            v.w("te", p.v("te"))
            v.o(lambda e, tt=t: e.tensor_tensor(
                ga[:, tt, :], G["psC"][:, 0:8], G["gateb_t"][:], ALU.add),
                inc="ve")
    with Sec(p, "vector") as v:
        v.w("ve", p.v("ve"))
        vq = v.o(lambda e: e.tensor_reduce(m1[:, 0:16], ga[:], AX.X, ALU.max,
                                           negate=True), inc="ve")
    with Sec(p, "scalar") as c:
        c.w("ve", vq)
        for t in range(NT):
            c.o(lambda e, tt=t: e.activation(ga[:, tt, :], ga[:, tt, :],
                                             AF.Exp, bias=m1[:, tt:tt + 1]),
                inc="sc" if t == NT - 1 else None)
    b16 = lambda ap: ap.unsqueeze(2).broadcast_to((128, 16, 8))
    with Sec(p, "vector") as v:
        v.w("sc", p.v("sc"))
        v.o(lambda e: e.tensor_reduce(m1[:, 16:32], ga[:], AX.X, ALU.add))
        v.drain()
        v.o(lambda e: e.reciprocal(m1[:, 32:48], m1[:, 16:32]))
        v.drain()
        v.o(lambda e: e.tensor_tensor(ga[:], ga[:], b16(m1[:, 32:48]),
                                      ALU.mult))
        v.drain()
        v.o(lambda e: e.tensor_reduce(m1[:, 0:16], ga[:], AX.X, ALU.max))
        v.drain()
        v.o(lambda e: e.tensor_tensor(geq[:], ga[:], b16(m1[:, 0:16]),
                                      ALU.is_equal))
        v.drain()
        v.o(lambda e: e.scalar_tensor_tensor(geq[:], geq[:], -1e30, ga[:],
                                             ALU.mult, ALU.add))
        v.drain()
        v.o(lambda e: e.tensor_reduce(m1[:, 16:32], geq[:], AX.X, ALU.max))
        v.drain()
        v.o(lambda e: e.tensor_tensor(geq[:], ga[:], b16(m1[:, 16:32]),
                                      ALU.is_ge))
        v.drain()
        v.o(lambda e: e.tensor_tensor(ga[:], ga[:], geq[:], ALU.mult))
        v.drain()
        vq = v.o(lambda e: e.tensor_copy(
            G["scr_bf"][:, 0:128], ga[:].rearrange("p t e -> p (t e)")),
            inc="ve")
    with Sec(p, "tensor") as t_, Sec(p, "vector") as v:
        t_.w("ve", vq)
        pbf = [G["psC"].ap().bitcast(BF16), G["psD"].ap().bitcast(BF16)]
        cpq = [0, 0]
        for t in range(NT):
            sl = t % 2
            if t >= 2:
                t_.w("ve", cpq[sl])
            tq = t_.o(lambda e, tt=t, pb=pbf[sl]: e.transpose(
                pb[0:8, 0:128], G["scr_bf"][:, tt * 8:tt * 8 + 8],
                G["eye_t"][:]), inc="te")
            v.w("te", tq)
            cpq[sl] = v.o(lambda e, tt=t, pb=pbf[sl]: e.tensor_copy(
                G["wT_sb"][0:8, tt * 128:(tt + 1) * 128],
                pb[0:8, 0:128]),
                inc="ve")
    with Sec(p, "sync") as s:
        if "gate" in ctx["dbg"]:
            s.w("ve", p.v("ve"))
            s.dma(ctx["dbg"]["gate"], ga[:].rearrange("p t e -> p (t e)"))
    ph.close()



def _mbuild(p, ctx):
    A, G, sb = ctx["A"], ctx["G"], ctx["sb"]
    ph = contextlib.ExitStack()
    wd = sb("mb_wd", (128, 12, 32), BF16, ph)
    wo = sb("mb_wo", (128, 12, 32), BF16, ph)
    wox = sb("mb_wox", (128, 12, 32), BF16, ph)
    mxr = sb("mb_mx", (128, 7, 128), BF16, ph)
    Sd = sb("mb_S", (128, 32, 128), BF16, ph)
    t2 = sb("mb_t2", (128, 32, 128), BF16, ph)
    Mc = [sb("mb_M%d" % i, (128, 32, 128), BF16, ph) for i in range(2)]

    with Sec(p, "gpsimd") as g0:
        p.fence(g0)
        mx_t = g0.dma(mxr[:], A["maskxor"].rearrange("s p n -> p s n"), "dG")

    b32 = lambda ap: ap.unsqueeze(1).broadcast_to((128, 32, 128))
    bj = lambda ap: ap.unsqueeze(2).broadcast_to((128, 32, 128))
    flat = lambda ap: ap.rearrange("p c n -> p (c n)")

    for pre, Mout in (("up", G["Mup"]), ("dn", G["Mdn"])):
        with Sec(p, "gpsimd") as g:
            g.w("ve", p.v("ve"))
            g.dma(wd[:], A[pre + "_wd"].rearrange("s p c -> p s c"), "dG")
            g.dma(wo[:], A[pre + "_wo"].rearrange("s p c -> p s c"), "dG")
            w_t = g.dma(wox[:], A[pre + "_wox"].rearrange("s p c -> p s c"),
                        "dG")

        def s_dense(v, stage, transposed):
            v.o(lambda e, st=stage: e.tensor_tensor(
                Sd[:], b32(G["eye_t"][:]), bj(wd[:, st, :]), ALU.mult))
            src = wox if transposed else wo
            v.o(lambda e, st=stage, s2=src: e.tensor_tensor(
                t2[:], b32(mxr[:, st, :]), bj(s2[:, st, :]), ALU.mult))
            v.drain()
            return v.o(lambda e: e.tensor_tensor(flat(Sd[:]), flat(Sd[:]),
                                                 flat(t2[:]), ALU.add),
                       inc="ve")

        with Sec(p, "vector") as v:
            v.w("dG", w_t)
            v.w("dG", mx_t)
            s_dense(v, 0, False)
            v.drain()
            v.o(lambda e: e.tensor_copy(flat(Mc[0][:]), flat(Sd[:])),
                inc="ve")
        cur = 0
        for stage in range(1, 7):
            with Sec(p, "vector") as v:
                v.w("ve", p.v("ve"))
                v.w("te", p.v("te"))
                sdq = s_dense(v, stage, True)
            with Sec(p, "tensor") as t_, Sec(p, "vector") as v:
                t_.w("ve", sdq)
                for c4 in range(8):
                    if c4 >= 2:
                        t_.w("ve", p.v("ve"))
                    tq = 0
                    for cc in range(c4 * 4, c4 * 4 + 4):
                        psx = [G["psC"], G["psD"]][c4 % 2]
                        r = t_.o(lambda e, c2=cc, cu=cur, ps=psx: e.matmul(
                            ps[:, (c2 % 4) * 128:(c2 % 4 + 1) * 128],
                            Sd[:, c2, :], Mc[cu][:, c2, :],
                            start=True, stop=True),
                            inc="te" if cc % 4 == 3 else None)
                        if r:
                            tq = r
                    v.w("te", tq)
                    v.o(lambda e, b2=c4 * 4, cu=cur, ps=[G["psC"],
                                                         G["psD"]][c4 % 2]:
                        e.tensor_copy(
                            flat(Mc[1 - cu][:, b2:b2 + 4, :]), ps[:]),
                        inc="ve")
            cur = 1 - cur
        with Sec(p, "tensor") as t_, Sec(p, "vector") as v:
            t_.w("ve", p.v("ve"))
            pbf = [G["psC"].ap().bitcast(BF16), G["psD"].ap().bitcast(BF16)]
            for c4 in range(8):
                if c4 >= 2:
                    t_.w("ve", p.v("ve"))
                tq = 0
                for cc in range(c4 * 4, c4 * 4 + 4):
                    r = t_.o(lambda e, c2=cc, cu=cur, pb=pbf[c4 % 2]:
                             e.transpose(
                                 pb[:, (c2 % 4) * 128:(c2 % 4 + 1) * 128],
                                 Mc[cu][:, c2, :], G["eye_t"][:]),
                             inc="te" if cc % 4 == 3 else None)
                    if r:
                        tq = r
                v.w("te", tq)
                v.o(lambda e, b2=c4 * 4, pb=pbf[c4 % 2]: e.tensor_copy(
                    Mout[:, b2 * 128:(b2 + 4) * 128], pb[:, 0:512]),
                    inc="ve")
    ph.close()


def _p2_collective(p, ctx):
    with Sec(p, "gpsimd") as g:
        g.w("dS", p.v("dS"))
        g.w("dG", p.v("dG"))
        g.o(lambda e: e.collective_compute(
            "AllGather", ALU.bypass,
            replica_groups=[[0, 1], [2, 3], [4, 5], [6, 7]],
            ins=[ctx["kv_in_t"].ap().opt()],
            outs=[ctx["kv_ga_t"].ap().opt()]), inc="cc")


def _p3_partner(p, ctx):
    G, sb = ctx["G"], ctx["sb"]
    kvga, kvp = ctx["kvgaflat"], ctx["kvp"]
    ph = contextlib.ExitStack()
    buf = [sb("pr_buf%d" % i, (128, 1024), BF16, ph) for i in range(2)]
    with Sec(p, "gpsimd") as g:
        g.w("cc", p.v("cc"))
        done = [0, 0]
        for r in range(KVROWS // 128):
            sl = r % 2
            if r >= 2:
                g.w("dG", done[sl])
            gt = g.o(lambda e, rr=r, b=buf[sl]: e.indirect_dma_start(
                out=b[:], out_offset=None, in_=kvga,
                in_offset=IndirectOffsetOnAxis(
                    ap=G["pidx_t"][:, rr:rr + 1], axis=0)), inc="dG", n=16)
            g.w("dG", gt)
            done[sl] = g.dma(kvp[r * 128:(r + 1) * 128, :], buf[sl][:], "dG")
    with Sec(p, "gpsimd") as g:
        g.w("dG", p.v("dG"))
    ph.close()



class _Stream:

    def __init__(self, p, bufs, sched, sem="dW"):
        self.p = p
        self.bufs = bufs
        self.sched = sched
        self.sem = sem
        self.load_t = {}
        self.use_t = {}
        self.next_load = 0
        self.next_use = 0

    def _load(self, g):
        q = self.next_load
        if q >= len(self.sched):
            return
        self.next_load += 1
        if q >= 2:
            g.w("te", self.use_t[q - 2])
        self.load_t[q] = g.dma(self.bufs[q % 2][:], self.sched[q], self.sem)

    def prime(self, g):
        self._load(g)
        self._load(g)

    def acquire(self, t_):
        q = self.next_use
        t_.w(self.sem, self.load_t[q])
        return q, self.bufs[q % 2]

    def release(self, g):
        q = self.next_use
        self.next_use += 1
        self.use_t[q] = self.p.v("te")
        self._load(g)


def _p3_moe(p, ctx, xnT):
    A, G, sb = ctx["A"], ctx["G"], ctx["sb"]
    moe_dram = ctx["moe_dram"]
    ph = contextlib.ExitStack()
    w1b = [sb("m_w1%d" % i, (128, NCH, 512), BF16, ph) for i in range(2)]
    w2b = [sb("m_w2%d" % i, (128, 16, 256), BF16, ph) for i in range(2)]
    w3b = [sb("m_w3%d" % i, (128, 16, 512), BF16, ph) for i in range(2)]
    b1t = [sb("m_b1%d" % i, (128, 16), F32, ph) for i in range(2)]
    b2t = [sb("m_b2%d" % i, (128, 16), F32, ph) for i in range(2)]
    h1g = sb("m_h1", (128, 16, 1024), BF16, ph)
    h2g = sb("m_h2", (128, 16, 1024), BF16, ph)
    w3s = [sb("m_w3s%d" % i, (128, 1024), F32, ph) for i in range(2)]
    ga = G["gate_all"]
    psQ = G["psQ"]

    rr = lambda ap: ap.rearrange("(c p) h -> p c h", p=128)
    s1 = _Stream(p, w1b, [rr(A["w1"][e][:, i * 512:(i + 1) * 512])
                          for e in range(NE) for th in range(2)
                          for i in range(4)])
    s2 = _Stream(p, w2b, [rr(A["w2"][e][:, i * 256:(i + 1) * 256])
                          for e in range(NE) for th in range(2)
                          for i in range(8)])
    s3 = _Stream(p, w3b, [rr(A["w3"][e][:, i * 512:(i + 1) * 512])
                          for e in range(NE) for i in range(2)])
    b_t, b_use = {}, {}

    def load_b(g, e):
        g.w("sc", b_use.get(e - 2, 0))
        g.dma(b1t[e % 2][:], A["b1_r"][e], "dG")
        b_t[e] = g.dma(b2t[e % 2][:], A["b2_r"][e], "dG")

    with Sec(p, "gpsimd") as g:
        p.fence(g)
        load_b(g, 0)
        s1.prime(g)
        s2.prime(g)
        s3.prime(g)

    w3acc = [0, 0]
    h1ev = [0, 0]
    h2ev = [0, 0]
    w3ev = [0, 0]
    for e in range(NE):
        with Sec(p, "tensor") as t_, Sec(p, "scalar") as c, \
                Sec(p, "gpsimd") as g:
            if e + 1 < NE:
                load_b(g, e + 1)
            q3, _ = s3.acquire(t_)
            t_.w("dW", s3.load_t[q3 + 1])
            for th in range(2):
                for hw in range(4):
                    _, w1cur = s1.acquire(t_)
                    for hcs in range(4):
                        hc = hw * 4 + hcs
                        half = hc % 2
                        t_.w("sc", h1ev[half])
                        for dc in range(NCH):
                            for n in range(2):
                                t_.o(lambda e_, d=dc, nn=n, hh=half,
                                     wb=w1cur, h4=hcs, t2=th:
                                     e_.matmul(
                                         psQ[:, hh * 1024 + nn * 512:
                                             hh * 1024 + (nn + 1) * 512],
                                         wb[:, d, h4 * 128:(h4 + 1) * 128],
                                         xnT[:, d,
                                             t2 * 1024 + nn * 512:
                                             t2 * 1024 + (nn + 1) * 512],
                                         start=(d == 0),
                                         stop=(d == NCH - 1)),
                                     inc="te" if (dc == NCH - 1 and n == 1)
                                     else None)
                        c.w("te", p.v("te"))
                        c.w("dG", b_t[e])
                        h1ev[half] = c.o(
                            lambda e_, hh=half, hc2=hc, sl=e % 2:
                            e_.activation(
                                h1g[:, hc2, :],
                                psQ[:, hh * 1024:hh * 1024 + 1024],
                                AF.Gelu, bias=b1t[sl][:, hc2:hc2 + 1]),
                            inc="sc")
                    s1.release(g)
                b_use[e] = p.v("sc")
                for kw_ in range(8):
                    _, w2cur = s2.acquire(t_)
                    for k2 in range(2):
                        kc = kw_ * 2 + k2
                        half = kc % 2
                        t_.w("sc", h2ev[half])
                        if kc == 0:
                            t_.w("sc", h1ev[0])
                            t_.w("sc", h1ev[1])
                        for hc in range(16):
                            for n in range(2):
                                t_.o(lambda e_, h2=hc, nn=n, hh=half,
                                     wb=w2cur, kk=k2:
                                     e_.matmul(
                                         psQ[:, hh * 1024 + nn * 512:
                                             hh * 1024 + (nn + 1) * 512],
                                         wb[:, h2, kk * 128:(kk + 1) * 128],
                                         h1g[:, h2,
                                             nn * 512:(nn + 1) * 512],
                                         start=(h2 == 0), stop=(h2 == 15)),
                                     inc="te" if (hc == 15 and n == 1)
                                     else None)
                        c.w("te", p.v("te"))
                        h2ev[half] = c.o(
                            lambda e_, hh=half, k3=kc, sl=e % 2:
                            e_.activation(
                                h2g[:, k3, :],
                                psQ[:, hh * 1024:hh * 1024 + 1024],
                                AF.Gelu, bias=b2t[sl][:, k3:k3 + 1]),
                            inc="sc")
                    s2.release(g)
                for tt in range(8):
                    tg = th * 8 + tt
                    sl = tt % 2
                    if tt == 0:
                        t_.w("sc", h2ev[0])
                        t_.w("sc", h2ev[1])
                    for fh in range(2):
                        psx = [G["psA"], G["psB"]][fh]
                        wb3 = w3b[(s3.next_use + fh) % 2]
                        t_.w("sc", w3ev[fh])
                        for kc in range(16):
                            t_.o(lambda e_, k3=kc, ps=psx, wb=wb3, t2=tt:
                                 e_.matmul(ps[:],
                                           h2g[:, k3,
                                               t2 * 128:(t2 + 1) * 128],
                                           wb[:, k3, :],
                                           start=(k3 == 0), stop=(k3 == 15)),
                                 inc="te" if kc == 15 else None)
                        c.w("te", p.v("te"))
                        if w3acc[sl]:
                            c.w("dG", w3acc[sl])
                        w3ev[fh] = c.o(
                            lambda e_, f2=fh, s2=sl, tg2=tg, ee=e, ps=psx:
                            e_.activation(
                                w3s[s2][:, f2 * 512:(f2 + 1) * 512], ps[:],
                                AF.Identity, scale=ga[:, tg2, ee:ee + 1]),
                            inc="sc")
                    g.w("sc", w3ev[1])
                    w3acc[sl] = g.o(
                        lambda e_, s2=sl, tg2=tg, ee=e: e_.dma_start(
                            out=moe_dram[tg2 * 128:(tg2 + 1) * 128, :],
                            in_=w3s[s2][:],
                            accum_op=(ALU.bypass if ee == 0 else ALU.add)),
                        inc="dG", n=16)
            s3.release(g)
            s3.release(g)
    ph.close()



def _p3_ffn(p, ctx, xnT):
    A, G, sb = ctx["A"], ctx["G"], ctx["sb"]
    moe_dram = ctx["moe_dram"]
    ph = contextlib.ExitStack()
    TS = 512
    NS = SC // TS
    wexpb = [sb("f_we%d" % i, (128, NCH, 512), BF16, ph) for i in range(2)]
    wconb = [sb("f_wc%d" % i, (128, 32, 256), BF16, ph) for i in range(2)]
    slab = [sb("f_sl%d" % i, (128, 32, TS), BF16, ph) for i in range(2)]
    tmpb = [sb("f_tmp%d" % i, (128, TS), F32, ph) for i in range(4)]
    cstage = [sb("f_cs%d" % i, (128, D), F32, ph) for i in range(4)]
    bwd = sb("f_bwd", (128, 12, 32), F32, ph)
    bwo = sb("f_bwo", (128, 12, 32), F32, ph)
    dwd = sb("f_dwd", (128, 12, 32), F32, ph)
    dwo = sb("f_dwo", (128, 12, 32), F32, ph)
    bexp = sb("f_bexp", (128, 32), F32, ph)
    upb = sb("f_upb", (128, 32), F32, ph)

    with Sec(p, "sync") as s:
        s.dma(bexp[:], A["bexp_r"])
        cb_t = s.dma(upb[:], A["upb_r"])
    with Sec(p, "sync") as s:
        s.dma(bwd[:], A["up_wd"].rearrange("s p c -> p s c"))
        s.dma(bwo[:], A["up_wo"].rearrange("s p c -> p s c"))
        s.dma(dwd[:], A["dn_wd"].rearrange("s p c -> p s c"))
        bw_t = s.dma(dwo[:], A["dn_wo"].rearrange("s p c -> p s c"))

    we_t, wc_t = {}, {}
    we_use, wc_use = {}, {}

    wexp_bf, wcon_bf = ctx["wexp_bf"], ctx["wcon_bf"]

    def load_wexp(g, key, i):
        g.w("te", we_use.get(key - 2, 0))
        we_t[key] = g.dma(
            wexpb[key % 2][:],
            wexp_bf[i].rearrange("p (c h) -> p c h", c=NCH), "dW")

    def load_wcon(g, key, i):
        g.w("te", wc_use.get(key - 2, 0))
        wc_t[key] = g.dma(
            wconb[key % 2][:],
            wcon_bf[i].rearrange("p (c f) -> p c f", c=32), "dW")

    csfree = [0, 0, 0, 0]
    csq = [0, 0]
    rp_sem = ctx["rp_sem"]
    with Sec(p, "gpsimd") as g:
        g.ops.append(lambda e: e.wait_ge(rp_sem, 192))
        load_wexp(g, 0, 0)
        load_wexp(g, 1, 1)

    psl = [G["psA"], G["psB"]]
    NQ = 4
    evq = [0, 0]
    stq = [0, 0]

    for ts in range(NS):
        with Sec(p, "tensor") as t_, Sec(p, "scalar") as c, \
                Sec(p, "vector") as v, Sec(p, "gpsimd") as g:
            for hc in range(32):
                wi = hc // 4
                key = ts * 8 + wi
                sl = hc % 2
                t_.w("dW", we_t[key])
                t_.w("sc", evq[sl])
                for dc in range(NCH):
                    t_.o(lambda e, d=dc, h2=hc % 4, ps=psl[sl], t2=ts,
                         k2=key % 2:
                         e.matmul(ps[:],
                                  wexpb[k2][:, d, h2 * 128:(h2 + 1) * 128],
                                  xnT[:, d, t2 * TS:(t2 + 1) * TS],
                                  start=(d == 0), stop=(d == NCH - 1)),
                         inc="te" if dc == NCH - 1 else None)
                we_use[key] = p.v("te")
                if hc % 4 == 3 and key + 2 < NS * 8:
                    load_wexp(g, key + 2, (wi + 2) % 8)
                c.w("te", p.v("te"))
                c.w("dS", cb_t)
                evq[sl] = c.o(lambda e, h2=hc, ps=psl[sl]: e.activation(
                    slab[0][:, h2, :], ps[:], AF.Identity,
                    bias=bexp[:, h2:h2 + 1]), inc="sc")
            for hc in range(32):
                sl = hc % 2
                t_.w("sc", evq[sl])
                t_.o(lambda e, c2=hc, ps=psl[sl]: e.matmul(
                    ps[:], G["Mup"][:, c2 * 128:(c2 + 1) * 128],
                    slab[0][:, c2, :], start=True, stop=True), inc="te")
                c.w("te", p.v("te"))
                evq[sl] = c.o(lambda e, c2=hc, ps=psl[sl]: e.activation(
                    slab[1][:, c2, :], ps[:], AF.Identity), inc="sc")
            c.w("dS", bw_t)
            cur = 1
            mq_all = p.v("sc")
            stq4 = [0, 0, 0, 0]
            for sg in range(5):
                k = 1 << sg
                stg = 7 + sg
                for cc in range(32):
                    sl = cc % 4
                    c.w("ve", stq4[sl])
                    stq_sc = c.o(lambda e, c2=cc, s2=stg, cu=cur, sl2=sl:
                                 e.activation(tmpb[sl2][:], slab[cu][:, c2, :],
                                              AF.Identity,
                                              scale=bwd[:, s2, c2:c2 + 1]),
                                 inc="sc")
                    v.w("sc", stq_sc)
                    if sg == 0:
                        v.w("sc", mq_all)
                    stq4[sl] = v.o(
                        lambda e, c2=cc, s2=stg, cu=cur, kk=k, sl2=sl:
                        e.scalar_tensor_tensor(
                            slab[1 - cu][:, c2, :], slab[cu][:, c2 ^ kk, :],
                            bwo[:, s2, c2:c2 + 1], tmpb[sl2][:],
                            ALU.mult, ALU.add), inc="ve")
                cur = 1 - cur
            stq = [stq4[2], stq4[3]]
            gel = 0
            for cc in range(32):
                c.w("ve", stq[0])
                c.w("ve", stq[1])
                gel = c.o(lambda e, c2=cc, cu=cur: e.activation(
                    slab[1 - cu][:, c2, :], slab[cu][:, c2, :], AF.Gelu,
                    bias=upb[:, c2:c2 + 1]), inc="sc")
            cur = 1 - cur
            for hc in range(32):
                sl = hc % 2
                t_.w("sc", gel)
                t_.w("sc", evq[sl])
                t_.w("ve", stq[sl])
                t_.o(lambda e, c2=hc, ps=psl[sl], cu=cur: e.matmul(
                    ps[:], G["Mdn"][:, c2 * 128:(c2 + 1) * 128],
                    slab[cu][:, c2, :], start=True, stop=True), inc="te")
                c.w("te", p.v("te"))
                evq[sl] = c.o(lambda e, c2=hc, ps=psl[sl], cu=cur:
                              e.activation(slab[1 - cu][:, c2, :], ps[:],
                                           AF.Identity), inc="sc")
            cur = 1 - cur
            for sg in range(5):
                k = 1 << sg
                stg = 7 + sg
                for cc in range(32):
                    sl = cc % 2
                    c.w("ve", stq[sl])
                    c.w("sc", evq[0])
                    c.w("sc", evq[1])
                    stq_sc = c.o(lambda e, c2=cc, s2=stg, cu=cur, sl2=sl:
                                 e.activation(tmpb[sl2][:], slab[cu][:, c2, :],
                                              AF.Identity,
                                              scale=dwd[:, s2, c2:c2 + 1]),
                                 inc="sc")
                    v.w("sc", stq_sc)
                    stq[sl] = v.o(
                        lambda e, c2=cc, s2=stg, cu=cur, kk=k, sl2=sl:
                        e.scalar_tensor_tensor(
                            slab[1 - cu][:, c2, :], slab[cu][:, c2 ^ kk, :],
                            dwo[:, s2, c2:c2 + 1], tmpb[sl2][:],
                            ALU.mult, ALU.add), inc="ve")
                cur = 1 - cur
            if ts == 0:
                load_wcon(g, 0, 0)
                load_wcon(g, 1, 1)
        with Sec(p, "tensor") as t_, Sec(p, "scalar") as c, \
                Sec(p, "sync") as s, Sec(p, "gpsimd") as g:
            for fq in range(NQ):
                key = ts * NQ + fq
                for tsub in range(4):
                    psx = [G["psC"], G["psD"]][tsub % 2]
                    t_.w("dW", wc_t[key])
                    t_.w("ve", stq[0])
                    t_.w("ve", stq[1])
                    if fq == 0 and csfree[tsub]:
                        t_.w("dS", csfree[tsub])
                    t_.w("sc", csq[tsub % 2])
                    for hc in range(32):
                        t_.o(lambda e, h2=hc, ps=psx, cu=cur, k2=key % 2,
                             t3=tsub:
                             e.matmul(ps[:, 0:256],
                                      slab[cu][:, h2,
                                               t3 * 128:(t3 + 1) * 128],
                                      wconb[k2][:, h2, :],
                                      start=(h2 == 0), stop=(h2 == 31)),
                             inc="te" if hc == 31 else None)
                    wc_use[key] = p.v("te")
                    if tsub == 3 and key + 2 < NS * NQ:
                        load_wcon(g, key + 2, (fq + 2) % NQ)
                    c.w("te", p.v("te"))
                    csq[tsub % 2] = c.o(
                        lambda e, t3=tsub, f2=fq, ps=psx: e.activation(
                            cstage[t3][:, f2 * 256:(f2 + 1) * 256],
                            ps[:, 0:256], AF.Identity), inc="sc")
                    if fq == NQ - 1:
                        s.w("sc", csq[tsub % 2])
                        row = ts * TS + tsub * 128
                        csfree[tsub] = s.dma(moe_dram[row:row + 128, :],
                                             cstage[tsub][:])
    ph.close()



def _p4_attn(p, ctx, L, xnT, attnout, pes):
    A, G, sb = ctx["A"], ctx["G"], ctx["sb"]
    kvpK, kviK, kvp, kvi = ctx["kvpK"], ctx["kviK"], ctx["kvp"], ctx["kvi"]
    qfm_dram = ctx["qfm_dram"]
    ph = contextlib.ExitStack()
    Kt = [sb("a_K%d_%d" % (L, i), (65, SK), BF16, ph) for i in range(4)]
    Vt = [sb("a_V%d_%d" % (L, i), (128, NSK, 130), BF16, ph)
          for i in range(2)]
    Qt = [sb("a_Q%d_%d" % (L, i), (65, SC), BF16, ph) for i in range(4)]
    PT = sb("a_PT%d" % L, (128, 2048), BF16, ph)
    G["sq_bf"] = sb("a_sq%d" % L, (64, SK), BF16, ph)
    diagm = sb("a_dm%d" % L, (128, 4, 512), BF16, ph)
    G["diagm"] = diagm
    with Sec(p, "gpsimd") as g0:
        g0.dma(diagm[:], A["diagmask"].rearrange("v p n -> p v n"), "dG")
    pvs = [sb("a_pvs%d_%d" % (L, i), (64, 512), F32, ph) for i in range(2)]
    rs = [sb("a_rs%d_%d" % (L, i), (1, 1024), F32, ph) for i in range(2)]
    rsb = [sb("a_rsb%d_%d" % (L, i), (1, 512), BF16, ph) for i in range(2)]
    if L == 0:
        wqb = sb("a_wq", (128, NCH, D), BF16, ph)
        bqt = sb("a_bq", (128, NCH), F32, ph)

    with Sec(p, "vector") as v:
        for i in range(2):
            v.o(lambda e, i2=i: e.memset(Vt[i2][:, :, 64:65], 1.0))
            v.o(lambda e, i2=i: e.memset(Vt[i2][:, :, 129:130], 1.0))

    with Sec(p, "sync") as s, Sec(p, "vector") as v:
        s.w("dG", p.v("dG"))
        s.dma(G["kmax_bf"][0:1, 1:2], kvp[STATROW:STATROW + 1, 0:1])
        kd2 = s.dma(G["kmax_bf"][0:1, 2:3], kvi[STATROW:STATROW + 1, 0:1])
        v.w("dS", kd2)
        v.o(lambda e: e.tensor_copy(G["kmax"][0:1, 1:2],
                                    G["kmax_bf"][0:1, 1:2]))
        v.o(lambda e: e.tensor_copy(G["kmax"][0:1, 2:3],
                                    G["kmax_bf"][0:1, 2:3]))
        v.drain()
        kmq = v.o(lambda e: e.tensor_tensor(
            G["kmax"][0:1, 1:2], G["kmax"][0:1, 1:2], G["kmax"][0:1, 2:3],
            ALU.max), inc="ve")

    wq_t = 0
    bq8 = 0
    if L == 0:
        with Sec(p, "gpsimd") as g, Sec(p, "sync") as s, \
                Sec(p, "scalar") as c:
            p.fence(g)
            wq_t = g.dma(wqb[:],
A["wqT"].rearrange("(c p) d -> p c d", p=128), "dW")
            bq_d = s.dma(bqt[:], A["bq_r"])
            c.w("dS", bq_d)
            bq8 = c.o(lambda e: e.activation(bqt[:], bqt[:], AF.Identity,
                                             scale=RSQ), inc="sc")

    kv_pull = {}
    q_done = {}
    bq_last = [0]

    def pulls(g_idx):
        sl = g_idx % 2
        with Sec(p, "sync") as s, Sec(p, "scalar") as c:
            s.w("te", p.v("te"))
            c.w("te", p.v("te"))
            for hp in range(2):
                h = 2 * g_idx + hp
                s.dma(Kt[2 * sl + hp][:, 0:SC], kvpK[65 * h:65 * h + 65, :])
                s.dma(Kt[2 * sl + hp][:, SC:SK], kviK[65 * h:65 * h + 65, :])
            last = 0
            for j in range(NSK):
                sec = s if j % 2 == 0 else c
                src = kvp if j < 16 else kvi
                row0 = VBASE + (j % 16) * 128
                last = sec.o(
                    lambda e, s2=sl, j2=j, r0=row0, sr=src, g2=g_idx:
                    e.dma_start(
                        out=Vt[s2][:, j2, 0:130].rearrange(
                            "p (h d) -> p h d", d=65)[:, :, 0:64],
                        in_=sr[r0:r0 + 128, g2 * 128:(g2 + 1) * 128]),
                    inc=("dS" if j % 2 == 0 else "dC"), n=16)
            kv_pull[g_idx] = (p.v("dS"), p.v("dC"))

    def build_q(g_idx):
        sl = g_idx % 2
        with Sec(p, "tensor") as t_, Sec(p, "scalar") as c, \
                Sec(p, "vector") as v, Sec(p, "sync") as s:
            for hp in range(2):
                h = 2 * g_idx + hp
                qtile = Qt[2 * sl + hp]
                t_.w("ve", bq_last[0])
                if L == 0:
                    t_.w("dW", wq_t)
                    for s4 in range(4):
                        t_.w("sc", p.v("sc"))
                        for dc in range(NCH):
                            t_.o(lambda e, d=dc, g2=g_idx, ss=s4: e.matmul(
                                G["psD"][:],
                                wqb[:, d, g2 * 128:(g2 + 1) * 128],
                                xnT[:, d, ss * 512:(ss + 1) * 512],
                                start=(d == 0), stop=(d == NCH - 1)),
                                inc="te" if dc == NCH - 1 else None)
                        c.w("te", p.v("te"))
                        c.w("sc", bq8)
                        c.o(lambda e, ss=s4, hp2=hp, q2=qtile, g2=g_idx:
                            e.activation(
                                q2[0:64, ss * 512:(ss + 1) * 512],
                                G["psD"][hp2 * 64:hp2 * 64 + 64, :],
                                AF.Identity, scale=RSQ,
                                bias=bqt[hp2 * 64:hp2 * 64 + 64,
                                         g2:g2 + 1]), inc="sc")
                else:
                    dq = s.dma(qtile[0:64, :],
                               qfm_dram[g_idx * 128 + hp * 64:
                                        g_idx * 128 + hp * 64 + 64, :])
                    c.w("dS", dq)
                c.w("sc", p.v("sc"))
                c.w("te", p.v("te"))
                sq_q = c.o(lambda e, q2=qtile: e.activation(
                    G["sq_bf"][0:64, 0:SC], q2[0:64, :], AF.Square),
                    inc="sc")
                for s4 in range(4):
                    t_.w("sc", sq_q)
                    t_.w("ve", p.v("ve"))
                    tq = t_.o(lambda e, ss=s4: e.matmul(
                        G["psD"][0:1, :], G["ones64"][:],
                        G["sq_bf"][0:64, ss * 512:(ss + 1) * 512],
                        start=True, stop=True), inc="te")
                    v.w("te", tq)
                    v.w("ve", kmq)
                    q_done[(g_idx, hp)] = v.o(
                        lambda e, ss=s4, q2=qtile: e.tensor_scalar(
                            q2[64:65, ss * 512:(ss + 1) * 512],
                            G["psD"][0:1, :], G["kmax"][0:1, 1:2], -0.5,
                            ALU.add, ALU.mult), inc="ve")
                bq_last[0] = q_done[(g_idx, hp)]

    pulls(0)
    build_q(0)

    epi = {"pend": None, "tt0": 0, "tt1": 0}

    def epi_rd(t_, c, v, ci):
        c0p, hpp, pvd, g2p = epi["pend"]
        pspv = [G["psA"], G["psB"]][ci]
        c.w("te", pvd[ci])
        if epi["tt%d" % ci]:
            c.w("ve", epi["tt%d" % ci])
        epi["pq%d" % ci] = c.o(lambda e, pv=pspv, c2=ci: e.activation(
            pvs[c2][:], pv[0:64, :], AF.Identity), inc="sc")
        v.w("te", pvd[ci])
        epi["rd%d" % ci] = v.o(lambda e, pv=pspv, c2=ci: e.tensor_copy(
            rs[c2][0:1, 0:512], pv[64:65, :]), inc="ve")

    def epi_rc(t_, c, v, ci):
        v.drain()
        v.o(lambda e, c2=ci: e.reciprocal(rs[c2][0:1, 512:1024],
                                          rs[c2][0:1, 0:512]))
        v.drain()
        epi["vq%d" % ci] = v.o(lambda e, c2=ci: e.tensor_copy(
            rsb[c2][0:1, :], rs[c2][0:1, 512:1024]), inc="ve")

    def epi_a(t_, c, v, ci):
        epi_rd(t_, c, v, ci)
        epi_rc(t_, c, v, ci)

    def epi_b(t_, c, v):
        c0p, hpp, pvd, g2p = epi["pend"]
        if epi["tt1"]:
            t_.w("ve", epi["tt1"])
        tqs = [0, 0]
        for ci in range(2):
            t_.w("ve", epi["vq%d" % ci])
            tqs[ci] = t_.o(lambda e, c2=ci: e.matmul(
                G["psC"][c2 * 64:c2 * 64 + 64, :], G["ones_col"][:],
                rsb[c2][0:1, :], start=True, stop=True), inc="te")
        for ci in range(2):
            s4 = c0p + ci
            v.w("te", tqs[1])
            v.w("sc", epi["pq%d" % ci])
            epi["tt%d" % ci] = v.o(
                lambda e, hp2=hpp, g2=g2p, ss=s4, c2=ci:
                e.tensor_tensor(
                    attnout[hp2 * 64:hp2 * 64 + 64, g2,
                            ss * 512:(ss + 1) * 512],
                    pvs[c2][:], G["psC"][c2 * 64:c2 * 64 + 64, :],
                    ALU.mult),
                inc="ve")

    for g_idx in range(NH // 2):
        sl = g_idx % 2
        if g_idx + 1 < NH // 2:
            pulls(g_idx + 1)
            build_q(g_idx + 1)
        with Sec(p, "tensor") as t_, Sec(p, "scalar") as c, \
                Sec(p, "vector") as v:
            t_.w("dS", kv_pull[g_idx][0])
            t_.w("dC", kv_pull[g_idx][1])
            exq = [0, 0]
            pvq = [0, 0]
            for hp in range(2):
                h = 2 * g_idx + hp
                ktile = Kt[2 * sl + hp]
                qtile = Qt[2 * sl + hp]
                t_.w("ve", q_done[(g_idx, hp)])
                for sw in range(2):
                    c0 = 2 * sw
                    jm = [20 + 4 * c0, 20 + 4 * c0 + 4]
                    pvps = [G["psA"], G["psB"]]
                    pvd = [0, 0]

                    def emit_scores(j):
                        ph2 = j % 2
                        valid = [ci for ci in range(2) if j < jm[ci]]
                        cstart = valid[0]
                        wid = len(valid) * 512
                        t_.w("sc", exq[ph2])
                        for ci in valid:
                            t_.o(lambda e, jj=j, q4=ci, ss=c0, kt=ktile,
                                 qt=qtile, hh=ph2: e.matmul(
                                     G["psQ"][:, hh * 1024 + q4 * 512:
                                              hh * 1024 + (q4 + 1) * 512],
                                     kt[:, jj * 128:(jj + 1) * 128],
                                     qt[:, (ss + q4) * 512:
                                        (ss + q4 + 1) * 512],
                                     start=True, stop=True),
                                 inc="te" if ci == valid[-1] else None)
                        c.w("te", p.v("te"))
                        if pvq[ph2]:
                            c.w("te", pvq[ph2])
                        kill = j < 16
                        exq[ph2] = c.o(
                            lambda e, hh=ph2, k2=kill, cs=cstart, w2=wid:
                            e.activation(
                                PT[:, hh * 1024 + cs * 512:
                                   hh * 1024 + cs * 512 + w2],
                                G["psQ"][:, hh * 1024 + cs * 512:
                                         hh * 1024 + cs * 512 + w2],
                                AF.Exp,
                                bias=(G["kill_t"][:, 0:1] if k2 else 0.0)),
                            inc="sc")
                        nm = 0
                        mq = 0
                        for ci in valid:
                            jl = j - 16 - 4 * (c0 + ci)
                            if 0 <= jl <= 3:
                                if nm == 0:
                                    v.w("sc", exq[ph2])
                                nm += 1
                                mq = v.o(lambda e, hh=ph2, q4=ci, vv=jl:
                                         e.tensor_tensor(
                                             PT[:, hh * 1024 + q4 * 512:
                                                hh * 1024 + (q4 + 1) * 512],
                                             PT[:, hh * 1024 + q4 * 512:
                                                hh * 1024 + (q4 + 1) * 512],
                                             G["diagm"][:, vv, :], ALU.mult),
                                         inc="ve")
                        return (exq[ph2], mq if nm else 0)

                    def emit_pv(j, gates):
                        ph2 = j % 2
                        valid = [ci for ci in range(2) if j < jm[ci]]
                        sq2, mq = gates
                        if mq:
                            t_.w("ve", mq)
                        else:
                            t_.w("sc", sq2)
                        for ci in valid:
                            t_.o(lambda e, jj=j, q4=ci, pv=pvps[ci], s2=sl,
                                 hp2=hp, hh=ph2, jmx=jm[ci]: e.matmul(
                                     pv[0:65, :],
                                     Vt[s2][:, jj, hp2 * 65:hp2 * 65 + 65],
                                     PT[:, hh * 1024 + q4 * 512:
                                        hh * 1024 + (q4 + 1) * 512],
                                     start=(jj == 0), stop=(jj == jmx - 1)),
                                 inc="te" if ci == valid[-1] else None)
                        pvq[ph2] = p.v("te")
                        for ci in valid:
                            if j == jm[ci] - 1:
                                pvd[ci] = p.v("te")

                    pend0 = epi["pend"] is not None
                    gates = {}
                    gates[0] = emit_scores(0)
                    if pend0:
                        epi_rd(t_, c, v, 0)
                        epi_rd(t_, c, v, 1)
                        epi_rc(t_, c, v, 0)
                        epi_rc(t_, c, v, 1)
                    gates[1] = emit_scores(1)
                    if pend0:
                        t_.w("sc", epi["pq0"])
                        t_.w("ve", epi["rd0"])
                        t_.w("sc", epi["pq1"])
                        t_.w("ve", epi["rd1"])
                    emit_pv(0, gates[0])
                    gates[2] = emit_scores(2)
                    emit_pv(1, gates[1])
                    gates[3] = emit_scores(3)
                    emit_pv(2, gates[2])
                    if pend0:
                        epi_b(t_, c, v)
                    for j in range(4, jm[1]):
                        gates[j] = emit_scores(j)
                        emit_pv(j - 1, gates[j - 1])
                    emit_pv(jm[1] - 1, gates[jm[1] - 1])
                    epi["pend"] = (c0, hp, pvd[:], g_idx)

    with Sec(p, "tensor") as t_, Sec(p, "scalar") as c, \
            Sec(p, "vector") as v:
        epi_a(t_, c, v, 0)
        epi_a(t_, c, v, 1)
        epi_b(t_, c, v)
        epi["pend"] = None
        epi["tt0"] = 0
        epi["tt1"] = 0
        t_.w("ve", p.v("ve"))
    if "attnout" in ctx["dbg"] and L == 0:
        with Sec(p, "sync") as s:
            s.w("ve", p.v("ve"))
            s.dma(ctx["dbg"]["attnout"],
                  attnout[:].rearrange("p c s -> p (c s)"))
    p.barrier()
    ph.close()



def _p5_residual(p, ctx, L, attnout, x_in, x_out):
    A, G, sb = ctx["A"], ctx["G"], ctx["sb"]
    moe_dram = ctx["moe_dram"]
    ph = contextlib.ExitStack()
    wob = sb("p5wo%d" % L, (128, NCH, D), BF16, ph)
    G["biasrep"] = sb("p5br%d" % L, (128, D), F32, ph)
    G["biasrep2"] = sb("p5br2%d" % L, (128, D), F32, ph)
    xb = [sb("p5x%d_%d" % (L, i), (128, D), F32, ph) for i in range(2)]
    mb = [sb("p5m%d_%d" % (L, i), (128, D), F32, ph) for i in range(2)]
    ob = [sb("p5o%d_%d" % (L, i), (128, D), F32, ph) for i in range(2)]

    with Sec(p, "gpsimd") as g:
        p.fence(g)
        wo_t = g.dma(wob[:],
                     A["woT" if L == 0 else "ow"]
                     .rearrange("(c p) d -> p c d", p=128), "dW")
    with Sec(p, "sync") as s, Sec(p, "vector") as v:
        b1 = s.dma(G["biasrep"][:], A["bo_rep" if L == 0 else "bcon_rep"])
        bias_q = 0
        if L == 1:
            b2 = s.dma(G["biasrep2"][:], A["dbias_rep"])
            v.w("dS", b2)
            bias_q = v.o(lambda e: e.tensor_tensor(
                G["biasrep"][:], G["biasrep"][:], G["biasrep2"][:], ALU.add),
                inc="ve")

    ob_q = [0, 0]
    with Sec(p, "sync") as s, Sec(p, "scalar") as c, Sec(p, "tensor") as t_, \
            Sec(p, "vector") as v:
        t_.w("dW", wo_t)
        for t in range(NT):
            sl = t % 2
            if t >= 2:
                s.w("ve", ob_q[sl])
                c.w("ve", ob_q[sl])
            dx = s.dma(xb[sl][:], x_in[t * 128:(t + 1) * 128, :])
            dm = c.dma(mb[sl][:], moe_dram[t * 128:(t + 1) * 128, :],
                       inc="dC")
            if t >= 2:
                t_.w("ve", ob_q[sl])
            for fh in range(2):
                for dc in range(NCH):
                    lastmm = (dc == NCH - 1) and L == 1
                    t_.o(lambda e, d=dc, f2=fh, tt=t, s2=sl, lm=lastmm:
                         e.matmul(
                             G["psQ"][:, s2 * 1024 + f2 * 512:
                                      s2 * 1024 + (f2 + 1) * 512],
                             attnout[:, d, tt * 128:(tt + 1) * 128],
                             wob[:, d, f2 * 512:(f2 + 1) * 512],
                             start=(d == 0), stop=lm),
                         inc="te" if (lastmm and fh == 1) else None)
                if L == 0:
                    t_.o(lambda e, f2=fh, tt=t, s2=sl: e.matmul(
                        G["psQ"][:, s2 * 1024 + f2 * 512:
                                 s2 * 1024 + (f2 + 1) * 512],
                        G["wT_sb"][0:8, tt * 128:(tt + 1) * 128],
                        G["b3_sb"][0:8, f2 * 512:(f2 + 1) * 512],
                        start=False, stop=True),
                        inc="te" if fh == 1 else None)
            v.w("te", p.v("te"))
            v.w("dS", dx)
            v.w("dC", dm)
            v.w("ve", bias_q)
            v.o(lambda e, s2=sl: e.tensor_tensor(
                ob[s2][:], G["psQ"][:, s2 * 1024:s2 * 1024 + 1024],
                xb[s2][:], ALU.add))
            v.drain()
            v.o(lambda e, s2=sl: e.tensor_tensor(ob[s2][:], ob[s2][:],
                                                 mb[s2][:], ALU.add))
            v.drain()
            ob_q[sl] = v.o(lambda e, s2=sl: e.tensor_tensor(
                ob[s2][:], ob[s2][:], G["biasrep"][:], ALU.add), inc="ve")
            s.w("ve", ob_q[sl])
            s.dma(x_out[t * 128:(t + 1) * 128, :], ob[sl][:])
    ph.close()



def kernel(**inputs):
    shared, per_core = _host_prepare(inputs)
    nc = _build(DEBUG)
    in_maps = []
    for c in range(8):
        m = dict(shared)
        m.update(per_core[c])
        in_maps.append(m)
    res = run_bass_kernel_spmd(nc, in_maps, core_ids=list(range(8)))
    out = np.empty((4, 4096, D), np.float32)
    for c in range(8):
        b, hf = c // 2, c % 2
        out[b, hf * SC:(hf + 1) * SC, :] = res.results[c]["out_x"]
    return out


# version 07
# speedup vs baseline: 1.0216x; 1.0216x over previous
import math
import os
import contextlib
import numpy as np
import ml_dtypes

import concourse.bass as bass
import concourse.mybir as mybir
from concourse.bass_utils import run_bass_kernel_spmd
from concourse.bass_core_rust import IndirectOffsetOnAxis

F32 = mybir.dt.float32
BF16 = mybir.dt.bfloat16
I32 = mybir.dt.int32
AF = mybir.ActivationFunctionType
ALU = mybir.AluOpType
AX = mybir.AxisListType

D = 1024
SC = 2048
NT = SC // 128
NH = 16
HD = 64
NE = 8
HM = 2048
DH = 4096
NCH = D // 128
KROWS = NH * 65
VBASE = 2 * KROWS
STATROW = VBASE + SC
KVROWS = 4224
SK = 2 * SC
NSK = SK // 128
RSQ = float((1.0 / 8.0) ** 0.5)
EPS = 1e-5

DEBUG = ()
KPH = int(os.environ.get("KPH", "99"))
_PHC = [0]


def _ph_on():
    _PHC[0] += 1
    return _PHC[0] <= KPH

DEBUG_SHAPES = {
    "xn": (SC, D),
    "gate": (128, NT * NE),
    "attnout": (128, NCH * SC),
}


class Plan:
    ENGS = ("sync", "tensor", "scalar", "vector", "gpsimd")
    QUEUES = {"dS": 6, "dC": 4, "dW": 4, "dG": 6}

    def __init__(self, nc, block, es):
        self.nc = nc
        self.block = block
        self.cnt = {}
        self.sems = {}
        for name in ("cc", "te", "sc", "ve", "gp"):
            self.sems[name] = es.enter_context(nc.semaphore(name))
            self.cnt[name] = 0
        for eng in self.ENGS:
            self.sems["bar_" + eng] = es.enter_context(
                nc.semaphore("bar_" + eng))
        self.dlane = {}
        self.dcnt = {}
        self.dmap = {}
        for q, nl in self.QUEUES.items():
            self.cnt[q] = 0
            self.dlane[q] = [es.enter_context(nc.semaphore("%s%d" % (q, i)))
                             for i in range(nl)]
            self.dcnt[q] = [0] * nl
            self.dmap[q] = {}
        self.nbar = 0

    def inc(self, name, n=1):
        self.cnt[name] += n
        return self.sems[name]

    def v(self, name):
        return self.cnt[name]

    def fence(self, sec):
        for q in self.QUEUES:
            sec.w(q, self.cnt[q])
        for s in ("te", "sc", "ve"):
            sec.w(s, self.cnt[s])

    def queue_wait_ops(self, q, pseudo):
        if pseudo <= 0:
            return []
        snap = self.dmap[q].get(pseudo)
        if snap is None:
            keys = [k for k in self.dmap[q] if k >= pseudo]
            if keys:
                snap = self.dmap[q][min(keys)]
            else:
                snap = tuple(self.dcnt[q])
        return [(self.dlane[q][i], v) for i, v in enumerate(snap) if v > 0]

    def barrier(self):
        self.nbar += 1
        tgt = self.nbar
        dwaits = []
        for q in self.QUEUES:
            dwaits.extend((h, v) for h, v in
                          [(self.dlane[q][i], self.dcnt[q][i])
                           for i in range(len(self.dlane[q]))] if v > 0)
        for eng in self.ENGS:
            own = self.sems["bar_" + eng]
            others = [self.sems["bar_" + x] for x in self.ENGS if x != eng]
            seq_only = eng == "sync"
            def _(e, _t=tgt, _own=own, _o=tuple(others), _dw=tuple(dwaits),
                  _sq=seq_only):
                for h, v in _dw:
                    e.wait_ge(h, v)
                if _sq:
                    e.sem_inc(_own, 1)
                else:
                    e.drain().then_inc(_own, 1)
                for h in _o:
                    e.wait_ge(h, _t)
            getattr(self.block, eng)(_)
class Sec:
    def __init__(self, plan, eng):
        self.p = plan
        self.eng = eng
        self.ops = []

    def __enter__(self):
        return self

    def __exit__(self, et, ev, tb):
        if et is not None:
            return False
        ops = self.ops
        getattr(self.p.block, self.eng)(lambda e: [f(e) for f in ops])
        return False

    def w(self, sem, val):
        if not val or val <= 0:
            return
        if sem in self.p.QUEUES:
            for h, v in self.p.queue_wait_ops(sem, val):
                self.ops.append(lambda e, s=h, vv=v: e.wait_ge(s, vv))
        else:
            self.ops.append(
                lambda e, s=self.p.sems[sem], v=val: e.wait_ge(s, v))

    def o(self, fn, inc=None, n=1):
        if inc is None:
            self.ops.append(fn)
            return None
        if inc in self.p.QUEUES:
            return self._dma_o(fn, inc)
        sem = self.p.inc(inc, n)
        tgt = self.p.v(inc)

        def g(e, _f=fn, _s=sem, _n=n):
            _f(e).then_inc(_s, _n)
        self.ops.append(g)
        return tgt

    def _dma_o(self, fn, q):
        p = self.p
        li = (p.cnt[q] // 16) % len(p.dlane[q])
        prev = p.dcnt[q][li]
        if prev > 0:
            self.ops.append(
                lambda e, s=p.dlane[q][li], v=prev: e.wait_ge(s, v))
        p.dcnt[q][li] += 16
        p.cnt[q] += 16
        p.dmap[q][p.cnt[q]] = tuple(p.dcnt[q])
        sem = p.dlane[q][li]

        def g(e, _f=fn, _s=sem):
            _f(e).then_inc(_s, 16)
        self.ops.append(g)
        return p.cnt[q]

    def dma(self, out, in_, inc="dS", **kw):
        return self.o(lambda e, o=out, i=in_, k=kw:
                      e.dma_start(out=o, in_=i, **k), inc=inc, n=16)

    def drain(self):
        self.ops.append(lambda e: e.drain())



def _butterfly_coefs(w):
    ns = w.shape[0]
    i = np.arange(DH)
    wdiag = np.empty((ns, DH), np.float32)
    woff = np.empty((ns, DH), np.float32)
    for s in range(ns):
        s2 = 1 << s
        k = (i // (2 * s2)) * s2 + (i % s2)
        is_b = ((i // s2) % 2).astype(bool)
        wdiag[s] = np.where(is_b, w[s, k, 1, 1], w[s, k, 0, 0])
        woff[s] = np.where(is_b, w[s, k, 1, 0], w[s, k, 0, 1])
    woffx = np.empty_like(woff)
    for s in range(ns):
        woffx[s] = woff[s][i ^ (1 << s)]

    def jc(a):
        return np.ascontiguousarray(
            a.reshape(ns, 32, 128).transpose(0, 2, 1)).astype(np.float32)
    return jc(wdiag), jc(woff), jc(woffx)


def _host_prepare(inp):
    T = lambda a: np.ascontiguousarray(np.asarray(a, np.float32).T)
    C = lambda a: np.ascontiguousarray(np.asarray(a, np.float32))
    chunks = lambda a, n: np.ascontiguousarray(C(a).reshape(n, 128).T)
    rep = lambda a: np.ascontiguousarray(
        np.broadcast_to(C(a)[None, :], (128, np.asarray(a).shape[-1])))

    sh = {}
    sh["wqT"] = T(inp["L0_attn_Wq"]); sh["wkT"] = T(inp["L0_attn_Wk"])
    sh["wvT"] = T(inp["L0_attn_Wv"]); sh["woT"] = T(inp["L0_attn_Wo"])
    sh["bq_r"] = chunks(inp["L0_attn_bq"], NCH)
    sh["bk_r"] = chunks(inp["L0_attn_bk"], NCH)
    sh["bv_rep"] = rep(inp["L0_attn_bv"])
    sh["bo_rep"] = rep(inp["L0_attn_bo"])
    sh["w1"] = C(inp["L0_ffn_w1"]); sh["w2"] = C(inp["L0_ffn_w2"])
    sh["w3"] = C(inp["L0_ffn_w3"])
    sh["b1_r"] = np.stack([chunks(inp["L0_ffn_b1"][e], HM // 128)
                           for e in range(NE)])
    sh["b2_r"] = np.stack([chunks(inp["L0_ffn_b2"][e], HM // 128)
                           for e in range(NE)])
    sh["b3"] = C(inp["L0_ffn_b3"])
    sh["gate_w"] = C(inp["L0_ffn_gate_w"])
    sh["gate_b_rep"] = rep(inp["L0_ffn_gate_b"])
    sh["qw"] = C(inp["L1_attn_qw"]); sh["kw"] = C(inp["L1_attn_kw"])
    sh["vw"] = C(inp["L1_attn_vw"]); sh["ow"] = C(inp["L1_attn_ow"])
    sh["wexp_sw"] = np.ascontiguousarray(
        C(inp["L1_ffn_W_exp"]).reshape(8, 128, 8, 512)
        .transpose(2, 1, 0, 3).reshape(8, 128, 4096))
    sh["bexp_r"] = chunks(inp["L1_ffn_b_exp"], DH // 128)
    sh["upb_r"] = chunks(inp["L1_ffn_up_bias"], DH // 128)
    sh["wcon_sw"] = np.ascontiguousarray(
        C(inp["L1_ffn_W_con"]).reshape(32, 128, 4, 256)
        .transpose(2, 1, 0, 3).reshape(4, 128, 8192))
    sh["bcon_rep"] = rep(inp["L1_ffn_b_con"])
    sh["dbias_rep"] = rep(inp["L1_ffn_down_bias"])
    for pre, w in (("up", inp["L1_ffn_up_weights"]),
                   ("dn", inp["L1_ffn_down_weights"])):
        wd, wo, wox = _butterfly_coefs(C(w))
        sh[pre + "_wd"], sh[pre + "_wo"], sh[pre + "_wox"] = wd, wo, wox

    sh["eye"] = np.eye(128, dtype=np.float32)
    mx = np.zeros((7, 128, 128), np.float32)
    j = np.arange(128)
    for s in range(7):
        mx[s, j, j ^ (1 << s)] = 1
    sh["maskxor"] = mx
    dm = np.zeros((4, 128, 512), np.float32)
    pc = np.arange(128)[:, None]
    qc = np.arange(512)[None, :]
    for v in range(4):
        dm[v] = ((v * 128 + pc) <= qc)
    sh["diagmask"] = dm

    x_full = np.asarray(inp["L0_attn_x"], np.float32)
    inv = (10000.0 ** (-np.arange(0, HD, 2, dtype=np.float32) / HD)
           ).astype(np.float32)
    per_core = []
    for c in range(8):
        b, hf = c // 2, c % 2
        d = {}
        d["x0"] = np.ascontiguousarray(x_full[b, hf * SC:(hf + 1) * SC, :])
        d["kill"] = np.full((128, 1), 0.0 if hf == 1 else -1e9, np.float32)
        d["pidx"] = np.ascontiguousarray(
            (((c ^ 1) % 2) * KVROWS + np.arange(KVROWS))
            .astype(np.int32).reshape(KVROWS // 128, 128).T)
        pos = np.arange(hf * SC, (hf + 1) * SC, dtype=np.float32)
        f = pos[:, None] * inv[None, :]
        d["cos8"] = np.ascontiguousarray(
            np.tile(np.cos(f) * RSQ, (1, NH)).astype(np.float32))
        d["sin8"] = np.ascontiguousarray(
            np.tile(np.sin(f) * RSQ, (1, NH)).astype(np.float32))
        per_core.append(d)
    return sh, per_core



def _build(debug=()):
    nc = bass.Bass("TRN2", target_bir_lowering=False)
    A = {}

    def di(name, shape, dtype=F32):
        A[name] = nc.dram_tensor(name, list(shape), dtype,
                                 kind="ExternalInput").ap()

    di("x0", (SC, D))
    for n in ("wqT", "wkT", "wvT", "woT", "qw", "kw", "vw", "ow"):
        di(n, (D, D))
    di("bq_r", (128, NCH)); di("bk_r", (128, NCH))
    di("bv_rep", (128, D)); di("bo_rep", (128, D))
    di("w1", (NE, D, HM)); di("w2", (NE, HM, HM)); di("w3", (NE, HM, D))
    di("b1_r", (NE, 128, HM // 128)); di("b2_r", (NE, 128, HM // 128))
    di("b3", (NE, D))
    di("gate_w", (D, NE)); di("gate_b_rep", (128, NE))
    di("wexp_sw", (8, 128, 4096)); di("bexp_r", (128, DH // 128))
    di("upb_r", (128, DH // 128))
    di("wcon_sw", (4, 128, 8192))
    di("bcon_rep", (128, D)); di("dbias_rep", (128, D))
    for pre in ("up", "dn"):
        di(pre + "_wd", (12, 128, 32)); di(pre + "_wo", (12, 128, 32))
        di(pre + "_wox", (12, 128, 32))
    di("eye", (128, 128))
    di("maskxor", (7, 128, 128))
    di("diagmask", (4, 128, 512))
    di("kill", (128, 1))
    di("pidx", (128, KVROWS // 128), I32)
    di("cos8", (SC, NH * 32)); di("sin8", (SC, NH * 32))

    out_x = nc.dram_tensor("out_x", [SC, D], F32, kind="ExternalOutput").ap()
    dbg = {}
    for name in debug:
        dbg[name] = nc.dram_tensor("dbg_" + name, list(DEBUG_SHAPES[name]),
                                   F32, kind="ExternalOutput").ap()

    _kd = os.environ.get("KDBG") == "1"
    _kind = dict(kind="ExternalOutput") if _kd else {}
    x1_dram = nc.dram_tensor("x1_dram", [SC, D], F32, **_kind).ap()
    moe_dram = nc.dram_tensor("moe_dram", [SC, D], F32, **_kind).ap()
    qfm_dram = nc.dram_tensor("qfm_dram", [D, SC], BF16).ap()
    wexp_bf = nc.dram_tensor("wexp_bf", [8, 128, 4096], BF16).ap()
    wcon_bf = nc.dram_tensor("wcon_bf", [4, 128, 8192], BF16).ap()
    kv_in_t = nc.dram_tensor("kv_in", [KVROWS, 1024], BF16)
    kv_ga_t = nc.dram_tensor("kv_gather", [2, KVROWS, 1024], BF16)
    kv_pa_t = nc.dram_tensor("kv_partner", [KVROWS, 1024], BF16)
    kvi = kv_in_t.ap()
    kvp = kv_pa_t.ap()
    kviK = kvi.rearrange("(r two) w -> r (two w)", two=2)[0:KROWS, :]
    kvpK = kvp.rearrange("(r two) w -> r (two w)", two=2)[0:KROWS, :]
    kvgaflat = kv_ga_t.ap().rearrange("s r w -> (s r) w")

    es = contextlib.ExitStack()
    _uid = [0]

    def sb(name, shape, dtype=F32, stack=None):
        _uid[0] += 1
        return (stack or es).enter_context(
            nc.sbuf_tensor("%s_u%d" % (name, _uid[0]), list(shape), dtype))

    with nc.Block() as block:
        p = Plan(nc, block, es)

        G = {}
        G["ones_col"] = sb("ones_col", (1, 64), BF16)
        G["ones64"] = sb("ones64", (64, 1), BF16)
        G["eye_t"] = sb("eye_t", (128, 128), BF16)
        G["stats"] = sb("stats", (128, 112), F32)
        G["scr_bf"] = sb("scr_bf", (128, 1024), BF16)
        G["kill_t"] = sb("kill_t", (128, 1), F32)
        G["gate_all"] = sb("gate_all", (128, NT, NE), F32)
        G["gate_eq"] = sb("gate_eq", (128, NT, NE), F32)
        G["gateb_t"] = sb("gateb_t", (128, NE), F32)
        G["wT_sb"] = sb("wT_sb", (8, NT * 128), BF16)
        G["b3_sb"] = sb("b3_sb", (8, D), BF16)
        G["pidx_t"] = sb("pidx_t", (128, KVROWS // 128), I32)
        G["Mup"] = sb("Mup", (128, 32 * 128), BF16)
        G["Mdn"] = sb("Mdn", (128, 32 * 128), BF16)
        G["kmax"] = sb("kmax", (2, 8), F32)
        G["kmax_bf"] = sb("kmax_bf", (2, 8), BF16)
        G["krow"] = sb("krow", (2, 8), F32)

        G["psA"] = es.enter_context(nc.psum_tensor("psA", [128, 512], F32))
        G["psB"] = es.enter_context(nc.psum_tensor("psB", [128, 512], F32))
        G["psC"] = es.enter_context(nc.psum_tensor("psC", [128, 512], F32))
        G["psD"] = es.enter_context(nc.psum_tensor("psD", [128, 512], F32))
        G["psQ"] = es.enter_context(nc.psum_tensor("psQ", [128, 2048], F32))

        with Sec(p, "sync") as s:
            s.dma(G["kill_t"][:], A["kill"])
            s.dma(G["gateb_t"][:], A["gate_b_rep"])
            s.dma(G["pidx_t"][:], A["pidx"])
        with Sec(p, "gpsimd") as g:
            g.dma(G["b3_sb"][:], A["b3"], "dG")
            g.dma(G["eye_t"][:], A["eye"], "dG")
        with Sec(p, "vector") as v:
            v.o(lambda e: e.memset(G["ones_col"][:], 1.0))
            v.o(lambda e: e.memset(G["ones64"][:], 1.0))
            zq = v.o(lambda e: e.memset(G["scr_bf"][:], 0.0), inc="ve")
        with Sec(p, "sync") as s:
            s.w("ve", zq)
            s.dma(kvi[STATROW:KVROWS, :], G["scr_bf"][0:KVROWS - STATROW, :])
        p.barrier()

        rp_sem = es.enter_context(nc.semaphore("rp"))
        with Sec(p, "gpsimd") as g:
            for i in range(8):
                g.ops.append(lambda e, i2=i: e.dma_start(
                    out=wexp_bf[i2], in_=A["wexp_sw"][i2]).then_inc(
                        rp_sem, 16))
            for i in range(4):
                g.ops.append(lambda e, i2=i: e.dma_start(
                    out=wcon_bf[i2], in_=A["wcon_sw"][i2]).then_inc(
                        rp_sem, 16))

        ctx = dict(A=A, G=G, kvi=kvi, kvp=kvp, kviK=kviK, kvpK=kvpK,
                   kvgaflat=kvgaflat, kv_in_t=kv_in_t, kv_ga_t=kv_ga_t,
                   moe_dram=moe_dram, qfm_dram=qfm_dram, sb=sb, dbg=dbg,
                   wexp_bf=wexp_bf, wcon_bf=wcon_bf, rp_sem=rp_sem)
        for L in range(2):
            _emit_layer(p, ctx, L,
                        x_in=(A["x0"] if L == 0 else x1_dram),
                        x_out=(x1_dram if L == 0 else out_x))

        with Sec(p, "sync") as s:
            s.w("dS", p.v("dS"))
            s.w("dG", p.v("dG"))
            s.w("dW", p.v("dW"))
            s.w("dC", p.v("dC"))

    es.close()
    return nc


def _emit_layer(p, ctx, L, x_in, x_out):
    sb = ctx["sb"]
    les = contextlib.ExitStack()
    xnT = sb("xnT%d" % L, (128, NCH, SC), BF16, les)

    if _ph_on():
        _p1_ln(p, ctx, L, x_in, xnT)
    p.barrier()
    if _ph_on():
        _p2_kv(p, ctx, L, xnT)
    if L == 1 and _ph_on():
        _p2_rope(p, ctx, xnT)
    if L == 0 and _ph_on():
        _p2_gate(p, ctx, xnT)
    if _ph_on():
        _p2_collective(p, ctx)
        ctx["cc_on"] = True
    else:
        ctx["cc_on"] = False
    if L == 0 and _ph_on():
        _mbuild(p, ctx)
    p.barrier()
    if ctx["cc_on"] and _ph_on():
        _p3_partner(p, ctx)
    if _ph_on():
        if L == 0:
            _p3_moe(p, ctx, xnT)
        else:
            _p3_ffn(p, ctx, xnT)
    p.barrier()
    pes = contextlib.ExitStack()
    attnout = sb("attnout%d" % L, (128, NCH, SC), BF16, pes)
    if ctx["cc_on"] and _ph_on():
        _p4_attn(p, ctx, L, xnT, attnout, pes)
    p.barrier()
    if _ph_on():
        _p5_residual(p, ctx, L, attnout, x_in, x_out)
    p.barrier()
    pes.close()
    les.close()



def _p1_ln(p, ctx, L, x_in, xnT):
    A, G, sb = ctx["A"], ctx["G"], ctx["sb"]
    st = G["stats"]
    SUM, SQ, MEAN, VAR, SD, RSTD, NM = 0, 16, 32, 48, 64, 80, 96
    ph = contextlib.ExitStack()
    xb = [sb("p1x%d_%d" % (L, i), (128, D), F32, ph) for i in range(3)]
    xnb = [sb("p1n%d_%d" % (L, i), (128, D), BF16, ph) for i in range(2)]

    ve_done = [0] * NT
    sc_done = [0] * NT
    with Sec(p, "sync") as s, Sec(p, "vector") as v, Sec(p, "scalar") as c:
        for t in range(NT):
            if t >= 3:
                s.w("ve", ve_done[t - 3])
                s.w("sc", sc_done[t - 3])
            dt = s.dma(xb[t % 3][:], x_in[t * 128:(t + 1) * 128, :])
            v.w("dS", dt)
            ve_done[t] = v.o(
                lambda e, b=xb[t % 3], tt=t:
                e.reduce_sum(st[:, SUM + tt:SUM + tt + 1], b[:], axis=AX.X),
                inc="ve")
            c.w("dS", dt)
            if t >= 1:
                c.w("sc", sc_done[t - 1])
            sc_done[t] = c.o(
                lambda e, b=xb[t % 3], tt=t:
                e.activation(G["scr_bf"][:], b[:], AF.Square,
                             accum_out=st[:, SQ + tt:SQ + tt + 1]), inc="sc")
    with Sec(p, "vector") as v:
        v.w("ve", p.v("ve"))
        v.w("sc", p.v("sc"))
        v.o(lambda e: e.tensor_scalar(st[:, MEAN:MEAN + 16],
                                      st[:, SUM:SUM + 16], 1.0 / D, None,
                                      ALU.mult))
        v.o(lambda e: e.tensor_scalar(st[:, VAR:VAR + 16],
                                      st[:, SQ:SQ + 16], 1.0 / D, None,
                                      ALU.mult))
        v.drain()
        v.o(lambda e: e.tensor_tensor(st[:, SD:SD + 16], st[:, MEAN:MEAN + 16],
                                      st[:, MEAN:MEAN + 16], ALU.mult))
        v.drain()
        v.o(lambda e: e.tensor_tensor(st[:, VAR:VAR + 16], st[:, VAR:VAR + 16],
                                      st[:, SD:SD + 16], ALU.subtract))
        v.drain()
        veq = v.o(lambda e: e.tensor_scalar(st[:, VAR:VAR + 16],
                                            st[:, VAR:VAR + 16], EPS, None,
                                            ALU.add), inc="ve")
    with Sec(p, "scalar") as c:
        c.w("ve", veq)
        scq = c.o(lambda e: e.activation(st[:, SD:SD + 16],
                                         st[:, VAR:VAR + 16], AF.Sqrt),
                  inc="sc")
    with Sec(p, "vector") as v:
        v.w("sc", scq)
        v.o(lambda e: e.reciprocal(st[:, RSTD:RSTD + 16], st[:, SD:SD + 16]))
        v.drain()
        v.o(lambda e: e.tensor_tensor(st[:, NM:NM + 16], st[:, MEAN:MEAN + 16],
                                      st[:, RSTD:RSTD + 16], ALU.mult))
        v.drain()
        stat_q = v.o(lambda e: e.tensor_scalar(st[:, NM:NM + 16],
                                               st[:, NM:NM + 16], -1.0, None,
                                               ALU.mult), inc="ve")

    psv = [G["psA"].ap().bitcast(BF16), G["psB"].ap().bitcast(BF16)]
    tr_done = [0] * NT
    cp_done = [0] * NT
    sc2_done = [0] * NT
    with Sec(p, "sync") as s, Sec(p, "scalar") as c, Sec(p, "tensor") as t_, \
            Sec(p, "vector") as v:
        for t in range(NT):
            if t >= 3:
                s.w("sc", sc2_done[t - 3])
            else:
                s.w("ve", stat_q)
            dt = s.dma(xb[t % 3][:], x_in[t * 128:(t + 1) * 128, :])
            c.w("dS", dt)
            c.w("ve", stat_q)
            if t >= 2:
                c.w("te", tr_done[t - 2])
            sc2_done[t] = c.o(
                lambda e, b=xb[t % 3], n=xnb[t % 2], tt=t:
                e.activation(n[:], b[:], AF.Identity,
                             scale=st[:, RSTD + tt:RSTD + tt + 1],
                             bias=st[:, NM + tt:NM + tt + 1]), inc="sc")
            t_.w("sc", sc2_done[t])
            if t >= 2:
                t_.w("ve", cp_done[t - 2])
            for cc in range(NCH):
                r = t_.o(
                    lambda e, n=xnb[t % 2], c2=cc, pv=psv[t % 2]:
                    e.transpose(pv[:, c2 * 128:(c2 + 1) * 128],
                                n[:, c2 * 128:(c2 + 1) * 128], G["eye_t"][:]),
                    inc="te" if cc == NCH - 1 else None)
                if r:
                    tr_done[t] = r
            v.w("te", tr_done[t])
            cp_done[t] = v.o(
                lambda e, pv=psv[t % 2], tt=t: e.tensor_copy(
                    xnT[:, :, tt * 128:(tt + 1) * 128],
                    pv[:].rearrange("p (c s) -> p c s", s=128)), inc="ve")
            if "xn" in ctx["dbg"] and L == 0:
                s.w("sc", sc2_done[t])
                s.dma(ctx["dbg"]["xn"][t * 128:(t + 1) * 128, :],
                      xnb[t % 2][:])
    p.barrier()
    ph.close()



def _p2_kv(p, ctx, L, xnT):
    A, G, sb = ctx["A"], ctx["G"], ctx["sb"]
    kviK, kvi = ctx["kviK"], ctx["kvi"]
    ph = contextlib.ExitStack()
    wvb = sb("p2wv%d" % L, (128, NCH, D), BF16, ph)
    vstg = [sb("p2vs%d_%d" % (L, i), (128, D), BF16, ph) for i in range(2)]
    ones_row = sb("p2or%d" % L, (1, SC), BF16, ph)
    G["ones_row"] = ones_row
    if L == 0:
        G["sq_bf"] = sb("p2sq%d" % L, (64, SK), BF16, ph)
    with Sec(p, "vector") as v:
        vq_or = v.o(lambda e: e.memset(ones_row[:], 1.0), inc="ve")

    if L == 0:
        wkb = sb("p2wk", (128, NCH, D), BF16, ph)
        bkt = sb("p2bk", (128, NCH), F32, ph)
        bvt = sb("p2bvr", (128, D), F32, ph)
        kstg = [sb("p2ks%d" % i, (128, SC), BF16, ph) for i in range(2)]
        with Sec(p, "gpsimd") as g, Sec(p, "sync") as s:
            p.fence(g)
            wk_t = g.dma(wkb[:],
                         A["wkT"].rearrange("(c p) d -> p c d", p=128), "dW")
            wv_t = g.dma(wvb[:],
                         A["wvT"].rearrange("(c p) d -> p c d", p=128), "dW")
            s.dma(bkt[:], A["bk_r"])
            bv_t = s.dma(bvt[:], A["bv_rep"])
        with Sec(p, "scalar") as c:
            c.w("dS", bv_t)
            bk8 = c.o(lambda e: e.activation(bkt[:], bkt[:], AF.Identity,
                                             scale=RSQ), inc="sc")
        km_last = 0
        with Sec(p, "tensor") as t_, Sec(p, "scalar") as c, \
                Sec(p, "sync") as s, Sec(p, "vector") as v:
            t_.w("dW", wk_t)
            stg_free = [0, 0]
            evac = {}
            first = True
            for pr in range(NH // 2):
                sl = pr % 2
                if pr >= 2:
                    t_.w("dS", stg_free[sl])
                for s4 in range(4):
                    psl = s4 % 2
                    psx = [G["psA"], G["psB"]][psl]
                    t_.w("sc", evac.get((pr * 4 + s4) - 2, 0))
                    for dc in range(NCH):
                        t_.o(lambda e, ps=psx, d=dc, pp=pr, ss=s4:
                             e.matmul(ps[:],
                                      wkb[:, d, pp * 128:(pp + 1) * 128],
                                      xnT[:, d, ss * 512:(ss + 1) * 512],
                                      start=(d == 0), stop=(d == NCH - 1)),
                             inc="te" if dc == NCH - 1 else None)
                    c.w("te", p.v("te"))
                    c.w("sc", bk8)
                    evac[pr * 4 + s4] = c.o(
                        lambda e, ps=psx, pp=pr, ss=s4, k=kstg[sl]:
                        e.activation(k[:, ss * 512:(ss + 1) * 512], ps[:],
                                     AF.Identity, scale=RSQ,
                                     bias=bkt[:, pp:pp + 1]), inc="sc")
                s.w("sc", evac[pr * 4 + 3])
                s.dma(kviK[65 * 2 * pr:65 * 2 * pr + 64, :],
                      kstg[sl][0:64, :])
                stg_free[sl] = s.dma(
                    kviK[65 * (2 * pr + 1):65 * (2 * pr + 1) + 64, :],
                    kstg[sl][64:128, :])
                km_last = _k2_pair(p, ctx, t_, c, v, kstg[sl],
                                   evac[pr * 4 + 3], first)
                first = False
        _k2_finish(p, ctx, km_last)
    else:
        with Sec(p, "gpsimd") as g:
            p.fence(g)
            wv_t = g.dma(wvb[:],
                         A["vw"].rearrange("(c p) d -> p c d", p=128), "dW")

    with Sec(p, "sync") as s:
        s.w("ve", vq_or)
        for h in range(NH):
            s.dma(kviK[65 * h + 64:65 * h + 65, :], G["ones_row"][:])

    with Sec(p, "tensor") as t_, Sec(p, "vector") as v, \
            Sec(p, "scalar") as c, Sec(p, "sync") as s:
        t_.w("dW", wv_t)
        if L == 0:
            t_.w("dS", bv_t)
        vfree = [0, 0]
        vev = [0, 0]
        for t in range(NT):
            sl = t % 2
            psb = G["psQ"]
            if t >= 2:
                t_.w("dS", vfree[sl])
                t_.w("ve" if L == 0 else "sc", vev[sl])
            for dc in range(NCH):
                for n in range(2):
                    t_.o(lambda e, d=dc, tt=t, nn=n, hh=sl:
                         e.matmul(psb[:, hh * 1024 + nn * 512:
                                       hh * 1024 + (nn + 1) * 512],
                                  xnT[:, d, tt * 128:(tt + 1) * 128],
                                  wvb[:, d, nn * 512:(nn + 1) * 512],
                                  start=(d == 0), stop=(d == NCH - 1)),
                         inc="te" if (dc == NCH - 1 and n == 1) else None)
            if L == 0:
                v.w("te", p.v("te"))
                vev[sl] = v.o(lambda e, o=vstg[sl], hh=sl: e.tensor_tensor(
                    o[:], psb[:, hh * 1024:hh * 1024 + 1024], bvt[:],
                    ALU.add), inc="ve")
                s.w("ve", vev[sl])
            else:
                c.w("te", p.v("te"))
                vev[sl] = c.o(lambda e, o=vstg[sl], hh=sl: e.activation(
                    o[:], psb[:, hh * 1024:hh * 1024 + 1024], AF.Identity),
                    inc="sc")
                s.w("sc", vev[sl])
            vfree[sl] = s.dma(
                kvi[VBASE + t * 128:VBASE + (t + 1) * 128, :], vstg[sl][:])
    ph.close()


def _k2_pair(p, ctx, t_, c, v, ktile, ready, first):
    G = ctx["G"]
    sq = G["sq_bf"]
    c.w("sc", ready)
    c.w("te", p.v("te"))
    c.o(lambda e: e.activation(sq[0:64, 0:SC], ktile[0:64, :], AF.Square),
        inc="sc")
    sq2 = c.o(lambda e: e.activation(sq[0:64, SC:2 * SC], ktile[64:128, :],
                                     AF.Square), inc="sc")
    last = 0
    for i in range(8):
        t_.w("sc", sq2)
        t_.w("ve", last or p.v("ve"))
        tq = t_.o(lambda e, ii=i: e.matmul(
            G["psD"][0:1, :], G["ones64"][:],
            sq[0:64, ii * 512:(ii + 1) * 512], start=True, stop=True),
            inc="te")
        v.w("te", tq)
        v.o(lambda e: e.reduce_max(G["krow"][0:1, 0:1], G["psD"][0:1, :],
                                   axis=AX.X))
        v.drain()
        if first and i == 0:
            v.o(lambda e: e.tensor_copy(G["kmax"][0:1, 0:1],
                                        G["krow"][0:1, 0:1]))
        else:
            v.o(lambda e: e.tensor_tensor(G["kmax"][0:1, 0:1],
                                          G["kmax"][0:1, 0:1],
                                          G["krow"][0:1, 0:1], ALU.max))
        v.drain()
        last = v.o(lambda e: e.drain(), inc="ve")
    return last


def _k2_finish(p, ctx, last):
    G, kvi = ctx["G"], ctx["kvi"]
    with Sec(p, "vector") as v, Sec(p, "sync") as s:
        v.w("ve", last)
        vq = v.o(lambda e: e.tensor_scalar(G["kmax_bf"][0:1, 0:1],
                                           G["kmax"][0:1, 0:1], 1.02, None,
                                           ALU.mult), inc="ve")
        s.w("ve", vq)
        s.dma(kvi[STATROW:STATROW + 1, 0:1], G["kmax_bf"][0:1, 0:1])



def _p2_rope(p, ctx, xnT):
    A, G, sb = ctx["A"], ctx["G"], ctx["sb"]
    kviK, kvi, qfm_dram = ctx["kviK"], ctx["kvi"], ctx["qfm_dram"]
    ph = contextlib.ExitStack()
    G["sq_bf"] = sb("r_sq", (64, SK), BF16, ph)
    qwb = sb("r_qw", (128, NCH, D), BF16, ph)
    kwb = sb("r_kw", (128, NCH, D), BF16, ph)
    fmb = sb("r_fm", (128, NCH, SC), BF16, ph)
    tb = [sb("r_tb%d" % i, (128, 1024), F32, ph) for i in range(2)]
    tmp = [sb("r_tmp%d" % i, (128, 512), F32, ph) for i in range(4)]
    rtm = [sb("r_tm%d" % i, (128, D), BF16, ph) for i in range(2)]

    with Sec(p, "gpsimd") as g:
        p.fence(g)
        qw_t = g.dma(qwb[:],
                     A["qw"].rearrange("(c p) d -> p c d", p=128), "dW")
        kw_t = g.dma(kwb[:],
                     A["kw"].rearrange("(c p) d -> p c d", p=128), "dW")

    qstg_done = 0
    for which in range(2):
        wb = qwb if which == 0 else kwb
        wt = qw_t if which == 0 else kw_t
        rope_q = [0, 0]
        cp_q = [0, 0]
        for t in range(NT):
            sl = t % 2
            with Sec(p, "sync") as s:
                if t >= 2:
                    s.w("ve", rope_q[sl])
                s.dma(tb[sl][:, 0:512], A["cos8"][t * 128:(t + 1) * 128, :])
                tbl_t = s.dma(tb[sl][:, 512:1024],
                              A["sin8"][t * 128:(t + 1) * 128, :])
            with Sec(p, "tensor") as t_:
                t_.w("dW", wt)
                if t + which > 0:
                    t_.w("ve", p.v("ve"))
                for dc in range(NCH):
                    for n in range(2):
                        t_.o(lambda e, d=dc, tt=t, nn=n, w2=wb:
                             e.matmul(G["psQ"][:, nn * 512:(nn + 1) * 512],
                                      xnT[:, d, tt * 128:(tt + 1) * 128],
                                      w2[:, d, nn * 512:(nn + 1) * 512],
                                      start=(d == 0), stop=(d == NCH - 1)),
                             inc="te" if (dc == NCH - 1 and n == 1) else None)
            with Sec(p, "vector") as v:
                v.w("te", p.v("te"))
                v.w("dS", tbl_t)
                if t >= 2:
                    v.w("te", cp_q[sl])
                pv = G["psQ"].ap()[:, 0:1024].rearrange(
                    "p (q two) -> p q two", two=2)
                ro = rtm[sl][:].rearrange("p (q two) -> p q two", two=2)
                ct, st_ = tb[sl][:, 0:512], tb[sl][:, 512:1024]
                v.o(lambda e, px=pv, cc=ct: e.tensor_tensor(
                    tmp[0][:], px[:, :, 0], cc, ALU.mult))
                v.o(lambda e, px=pv, cc=st_: e.tensor_tensor(
                    tmp[1][:], px[:, :, 1], cc, ALU.mult))
                v.o(lambda e, px=pv, cc=st_: e.tensor_tensor(
                    tmp[2][:], px[:, :, 0], cc, ALU.mult))
                v.o(lambda e, px=pv, cc=ct: e.tensor_tensor(
                    tmp[3][:], px[:, :, 1], cc, ALU.mult))
                v.drain()
                v.o(lambda e, o=ro: e.tensor_tensor(
                    o[:, :, 0], tmp[0][:], tmp[1][:], ALU.subtract))
                rope_q[sl] = v.o(lambda e, o=ro: e.tensor_tensor(
                    o[:, :, 1], tmp[2][:], tmp[3][:], ALU.add), inc="ve")
            with Sec(p, "tensor") as t_, Sec(p, "vector") as v:
                t_.w("ve", rope_q[sl])
                if which == 1 and t == 0:
                    v.w("dS", qstg_done)
                pbf = G["psC"].ap().bitcast(BF16)
                for c4 in range(2):
                    tq = 0
                    for cc in range(c4 * 4, c4 * 4 + 4):
                        r = t_.o(lambda e, c2=cc, s2=sl: e.transpose(
                            pbf[:, (c2 % 4) * 128:(c2 % 4 + 1) * 128],
                            rtm[s2][:, c2 * 128:(c2 + 1) * 128],
                            G["eye_t"][:]),
                            inc="te" if cc % 4 == 3 else None)
                        if r:
                            tq = r
                    v.w("te", tq)
                    cp_q[sl] = v.o(lambda e, b2=c4 * 4, tt=t: e.tensor_copy(
                        fmb[:, b2:b2 + 4, tt * 128:(tt + 1) * 128],
                        pbf[:, 0:512].rearrange("p (c s) -> p c s", s=128)),
                        inc="ve")
                    t_.w("ve", cp_q[sl])
        if which == 0:
            with Sec(p, "sync") as s:
                s.w("ve", p.v("ve"))
                for cc in range(NCH):
                    qstg_done = s.dma(qfm_dram[cc * 128:(cc + 1) * 128, :],
                                      fmb[:, cc, :])
        else:
            km_last = 0
            with Sec(p, "sync") as s, Sec(p, "tensor") as t_, \
                    Sec(p, "scalar") as c, Sec(p, "vector") as v:
                s.w("ve", p.v("ve"))
                vq = p.v("ve")
                for pr in range(NH // 2):
                    s.dma(kviK[65 * 2 * pr:65 * 2 * pr + 64, :],
                          fmb[0:64, pr, :])
                    s.dma(kviK[65 * (2 * pr + 1):65 * (2 * pr + 1) + 64, :],
                          fmb[64:128, pr, :])
                first = True
                for pr in range(NH // 2):
                    km_last = _k2_pair(p, ctx, t_, c, v, fmb[:, pr, :], vq,
                                       first)
                    first = False
            _k2_finish(p, ctx, km_last)
    ph.close()



def _p2_gate(p, ctx, xnT):
    A, G, sb = ctx["A"], ctx["G"], ctx["sb"]
    ph = contextlib.ExitStack()
    gwb = sb("p2gw", (128, NCH, 8), BF16, ph)
    ga, geq, m1 = G["gate_all"], G["gate_eq"], G["stats"]
    with Sec(p, "gpsimd") as g:
        p.fence(g)
        gw_t = g.dma(gwb[:],
                     A["gate_w"].rearrange("(c p) e -> p c e", p=128), "dW")
    with Sec(p, "tensor") as t_, Sec(p, "vector") as v:
        t_.w("dW", gw_t)
        for t in range(NT):
            if t >= 1:
                t_.w("ve", p.v("ve"))
            for dc in range(NCH):
                t_.o(lambda e, d=dc, tt=t: e.matmul(
                    G["psC"][:, 0:8], xnT[:, d, tt * 128:(tt + 1) * 128],
                    gwb[:, d, :], start=(d == 0), stop=(d == NCH - 1)),
                    inc="te" if dc == NCH - 1 else None)
            v.w("te", p.v("te"))
            v.o(lambda e, tt=t: e.tensor_tensor(
                ga[:, tt, :], G["psC"][:, 0:8], G["gateb_t"][:], ALU.add),
                inc="ve")
    with Sec(p, "vector") as v:
        v.w("ve", p.v("ve"))
        vq = v.o(lambda e: e.tensor_reduce(m1[:, 0:16], ga[:], AX.X, ALU.max,
                                           negate=True), inc="ve")
    with Sec(p, "scalar") as c:
        c.w("ve", vq)
        for t in range(NT):
            c.o(lambda e, tt=t: e.activation(ga[:, tt, :], ga[:, tt, :],
                                             AF.Exp, bias=m1[:, tt:tt + 1]),
                inc="sc" if t == NT - 1 else None)
    b16 = lambda ap: ap.unsqueeze(2).broadcast_to((128, 16, 8))
    with Sec(p, "vector") as v:
        v.w("sc", p.v("sc"))
        v.o(lambda e: e.tensor_reduce(m1[:, 16:32], ga[:], AX.X, ALU.add))
        v.drain()
        v.o(lambda e: e.reciprocal(m1[:, 32:48], m1[:, 16:32]))
        v.drain()
        v.o(lambda e: e.tensor_tensor(ga[:], ga[:], b16(m1[:, 32:48]),
                                      ALU.mult))
        v.drain()
        v.o(lambda e: e.tensor_reduce(m1[:, 0:16], ga[:], AX.X, ALU.max))
        v.drain()
        v.o(lambda e: e.tensor_tensor(geq[:], ga[:], b16(m1[:, 0:16]),
                                      ALU.is_equal))
        v.drain()
        v.o(lambda e: e.scalar_tensor_tensor(geq[:], geq[:], -1e30, ga[:],
                                             ALU.mult, ALU.add))
        v.drain()
        v.o(lambda e: e.tensor_reduce(m1[:, 16:32], geq[:], AX.X, ALU.max))
        v.drain()
        v.o(lambda e: e.tensor_tensor(geq[:], ga[:], b16(m1[:, 16:32]),
                                      ALU.is_ge))
        v.drain()
        v.o(lambda e: e.tensor_tensor(ga[:], ga[:], geq[:], ALU.mult))
        v.drain()
        vq = v.o(lambda e: e.tensor_copy(
            G["scr_bf"][:, 0:128], ga[:].rearrange("p t e -> p (t e)")),
            inc="ve")
    with Sec(p, "tensor") as t_, Sec(p, "vector") as v:
        t_.w("ve", vq)
        pbf = [G["psC"].ap().bitcast(BF16), G["psD"].ap().bitcast(BF16)]
        cpq = [0, 0]
        for t in range(NT):
            sl = t % 2
            if t >= 2:
                t_.w("ve", cpq[sl])
            tq = t_.o(lambda e, tt=t, pb=pbf[sl]: e.transpose(
                pb[0:8, 0:128], G["scr_bf"][:, tt * 8:tt * 8 + 8],
                G["eye_t"][:]), inc="te")
            v.w("te", tq)
            cpq[sl] = v.o(lambda e, tt=t, pb=pbf[sl]: e.tensor_copy(
                G["wT_sb"][0:8, tt * 128:(tt + 1) * 128],
                pb[0:8, 0:128]),
                inc="ve")
    with Sec(p, "sync") as s:
        if "gate" in ctx["dbg"]:
            s.w("ve", p.v("ve"))
            s.dma(ctx["dbg"]["gate"], ga[:].rearrange("p t e -> p (t e)"))
    ph.close()



def _mbuild(p, ctx):
    A, G, sb = ctx["A"], ctx["G"], ctx["sb"]
    ph = contextlib.ExitStack()
    wd = sb("mb_wd", (128, 12, 32), BF16, ph)
    wo = sb("mb_wo", (128, 12, 32), BF16, ph)
    wox = sb("mb_wox", (128, 12, 32), BF16, ph)
    mxr = sb("mb_mx", (128, 7, 128), BF16, ph)
    Sd = sb("mb_S", (128, 32, 128), BF16, ph)
    t2 = sb("mb_t2", (128, 32, 128), BF16, ph)
    Mc = [sb("mb_M%d" % i, (128, 32, 128), BF16, ph) for i in range(2)]

    with Sec(p, "gpsimd") as g0:
        p.fence(g0)
        mx_t = g0.dma(mxr[:], A["maskxor"].rearrange("s p n -> p s n"), "dG")

    b32 = lambda ap: ap.unsqueeze(1).broadcast_to((128, 32, 128))
    bj = lambda ap: ap.unsqueeze(2).broadcast_to((128, 32, 128))
    flat = lambda ap: ap.rearrange("p c n -> p (c n)")

    for pre, Mout in (("up", G["Mup"]), ("dn", G["Mdn"])):
        with Sec(p, "gpsimd") as g:
            g.w("ve", p.v("ve"))
            g.dma(wd[:], A[pre + "_wd"].rearrange("s p c -> p s c"), "dG")
            g.dma(wo[:], A[pre + "_wo"].rearrange("s p c -> p s c"), "dG")
            w_t = g.dma(wox[:], A[pre + "_wox"].rearrange("s p c -> p s c"),
                        "dG")

        def s_dense(v, stage, transposed):
            v.o(lambda e, st=stage: e.tensor_tensor(
                Sd[:], b32(G["eye_t"][:]), bj(wd[:, st, :]), ALU.mult))
            src = wox if transposed else wo
            v.o(lambda e, st=stage, s2=src: e.tensor_tensor(
                t2[:], b32(mxr[:, st, :]), bj(s2[:, st, :]), ALU.mult))
            v.drain()
            return v.o(lambda e: e.tensor_tensor(flat(Sd[:]), flat(Sd[:]),
                                                 flat(t2[:]), ALU.add),
                       inc="ve")

        with Sec(p, "vector") as v:
            v.w("dG", w_t)
            v.w("dG", mx_t)
            s_dense(v, 0, False)
            v.drain()
            v.o(lambda e: e.tensor_copy(flat(Mc[0][:]), flat(Sd[:])),
                inc="ve")
        cur = 0
        for stage in range(1, 7):
            with Sec(p, "vector") as v:
                v.w("ve", p.v("ve"))
                v.w("te", p.v("te"))
                sdq = s_dense(v, stage, True)
            with Sec(p, "tensor") as t_, Sec(p, "vector") as v:
                t_.w("ve", sdq)
                for c4 in range(8):
                    if c4 >= 2:
                        t_.w("ve", p.v("ve"))
                    tq = 0
                    for cc in range(c4 * 4, c4 * 4 + 4):
                        psx = [G["psC"], G["psD"]][c4 % 2]
                        r = t_.o(lambda e, c2=cc, cu=cur, ps=psx: e.matmul(
                            ps[:, (c2 % 4) * 128:(c2 % 4 + 1) * 128],
                            Sd[:, c2, :], Mc[cu][:, c2, :],
                            start=True, stop=True),
                            inc="te" if cc % 4 == 3 else None)
                        if r:
                            tq = r
                    v.w("te", tq)
                    v.o(lambda e, b2=c4 * 4, cu=cur, ps=[G["psC"],
                                                         G["psD"]][c4 % 2]:
                        e.tensor_copy(
                            flat(Mc[1 - cu][:, b2:b2 + 4, :]), ps[:]),
                        inc="ve")
            cur = 1 - cur
        with Sec(p, "tensor") as t_, Sec(p, "vector") as v:
            t_.w("ve", p.v("ve"))
            pbf = [G["psC"].ap().bitcast(BF16), G["psD"].ap().bitcast(BF16)]
            for c4 in range(8):
                if c4 >= 2:
                    t_.w("ve", p.v("ve"))
                tq = 0
                for cc in range(c4 * 4, c4 * 4 + 4):
                    r = t_.o(lambda e, c2=cc, cu=cur, pb=pbf[c4 % 2]:
                             e.transpose(
                                 pb[:, (c2 % 4) * 128:(c2 % 4 + 1) * 128],
                                 Mc[cu][:, c2, :], G["eye_t"][:]),
                             inc="te" if cc % 4 == 3 else None)
                    if r:
                        tq = r
                v.w("te", tq)
                v.o(lambda e, b2=c4 * 4, pb=pbf[c4 % 2]: e.tensor_copy(
                    Mout[:, b2 * 128:(b2 + 4) * 128], pb[:, 0:512]),
                    inc="ve")
    ph.close()


def _p2_collective(p, ctx):
    with Sec(p, "gpsimd") as g:
        g.w("dS", p.v("dS"))
        g.w("dG", p.v("dG"))
        g.o(lambda e: e.collective_compute(
            "AllGather", ALU.bypass,
            replica_groups=[[0, 1], [2, 3], [4, 5], [6, 7]],
            ins=[ctx["kv_in_t"].ap().opt()],
            outs=[ctx["kv_ga_t"].ap().opt()]), inc="cc")


def _p3_partner(p, ctx):
    G, sb = ctx["G"], ctx["sb"]
    kvga, kvp = ctx["kvgaflat"], ctx["kvp"]
    ph = contextlib.ExitStack()
    buf = [sb("pr_buf%d" % i, (128, 1024), BF16, ph) for i in range(2)]
    with Sec(p, "gpsimd") as g:
        g.w("cc", p.v("cc"))
        done = [0, 0]
        for r in range(KVROWS // 128):
            sl = r % 2
            if r >= 2:
                g.w("dG", done[sl])
            gt = g.o(lambda e, rr=r, b=buf[sl]: e.indirect_dma_start(
                out=b[:], out_offset=None, in_=kvga,
                in_offset=IndirectOffsetOnAxis(
                    ap=G["pidx_t"][:, rr:rr + 1], axis=0)), inc="dG", n=16)
            g.w("dG", gt)
            done[sl] = g.dma(kvp[r * 128:(r + 1) * 128, :], buf[sl][:], "dG")
    with Sec(p, "gpsimd") as g:
        g.w("dG", p.v("dG"))
    ph.close()



class _Stream:

    def __init__(self, p, bufs, sched, sem="dW"):
        self.p = p
        self.bufs = bufs
        self.sched = sched
        self.sem = sem
        self.load_t = {}
        self.use_t = {}
        self.next_load = 0
        self.next_use = 0

    def _load(self, g):
        q = self.next_load
        if q >= len(self.sched):
            return
        self.next_load += 1
        if q >= 2:
            g.w("te", self.use_t[q - 2])
        self.load_t[q] = g.dma(self.bufs[q % 2][:], self.sched[q], self.sem)

    def prime(self, g):
        self._load(g)
        self._load(g)

    def acquire(self, t_):
        q = self.next_use
        t_.w(self.sem, self.load_t[q])
        return q, self.bufs[q % 2]

    def release(self, g):
        q = self.next_use
        self.next_use += 1
        self.use_t[q] = self.p.v("te")
        self._load(g)


def _p3_moe(p, ctx, xnT):
    A, G, sb = ctx["A"], ctx["G"], ctx["sb"]
    moe_dram = ctx["moe_dram"]
    ph = contextlib.ExitStack()
    w1b = [sb("m_w1%d" % i, (128, NCH, 512), BF16, ph) for i in range(2)]
    w2b = [sb("m_w2%d" % i, (128, 16, 256), BF16, ph) for i in range(2)]
    w3b = [sb("m_w3%d" % i, (128, 16, 512), BF16, ph) for i in range(2)]
    b1t = [sb("m_b1%d" % i, (128, 16), F32, ph) for i in range(2)]
    b2t = [sb("m_b2%d" % i, (128, 16), F32, ph) for i in range(2)]
    h1g = sb("m_h1", (128, 16, 1024), BF16, ph)
    h2g = sb("m_h2", (128, 16, 1024), BF16, ph)
    w3s = [sb("m_w3s%d" % i, (128, 1024), F32, ph) for i in range(2)]
    ga = G["gate_all"]
    psQ = G["psQ"]

    rr = lambda ap: ap.rearrange("(c p) h -> p c h", p=128)
    s1 = _Stream(p, w1b, [rr(A["w1"][e][:, i * 512:(i + 1) * 512])
                          for e in range(NE) for th in range(2)
                          for i in range(4)])
    s2 = _Stream(p, w2b, [rr(A["w2"][e][:, i * 256:(i + 1) * 256])
                          for e in range(NE) for th in range(2)
                          for i in range(8)])
    s3 = _Stream(p, w3b, [rr(A["w3"][e][:, i * 512:(i + 1) * 512])
                          for e in range(NE) for i in range(2)])
    b_t, b_use = {}, {}

    def load_b(g, e):
        g.w("sc", b_use.get(e - 2, 0))
        g.dma(b1t[e % 2][:], A["b1_r"][e], "dG")
        b_t[e] = g.dma(b2t[e % 2][:], A["b2_r"][e], "dG")

    with Sec(p, "gpsimd") as g:
        p.fence(g)
        load_b(g, 0)
        s1.prime(g)
        s2.prime(g)
        s3.prime(g)

    w3acc = [0, 0]
    h1ev = [0, 0]
    h2ev = [0, 0]
    w3ev = [0, 0]
    for e in range(NE):
        with Sec(p, "tensor") as t_, Sec(p, "scalar") as c, \
                Sec(p, "gpsimd") as g:
            if e + 1 < NE:
                load_b(g, e + 1)
            q3, _ = s3.acquire(t_)
            t_.w("dW", s3.load_t[q3 + 1])
            for th in range(2):
                for hw in range(4):
                    _, w1cur = s1.acquire(t_)
                    for hcs in range(4):
                        hc = hw * 4 + hcs
                        half = hc % 2
                        t_.w("sc", h1ev[half])
                        for dc in range(NCH):
                            for n in range(2):
                                t_.o(lambda e_, d=dc, nn=n, hh=half,
                                     wb=w1cur, h4=hcs, t2=th:
                                     e_.matmul(
                                         psQ[:, hh * 1024 + nn * 512:
                                             hh * 1024 + (nn + 1) * 512],
                                         wb[:, d, h4 * 128:(h4 + 1) * 128],
                                         xnT[:, d,
                                             t2 * 1024 + nn * 512:
                                             t2 * 1024 + (nn + 1) * 512],
                                         start=(d == 0),
                                         stop=(d == NCH - 1)),
                                     inc="te" if (dc == NCH - 1 and n == 1)
                                     else None)
                        c.w("te", p.v("te"))
                        c.w("dG", b_t[e])
                        h1ev[half] = c.o(
                            lambda e_, hh=half, hc2=hc, sl=e % 2:
                            e_.activation(
                                h1g[:, hc2, :],
                                psQ[:, hh * 1024:hh * 1024 + 1024],
                                AF.Gelu, bias=b1t[sl][:, hc2:hc2 + 1]),
                            inc="sc")
                    s1.release(g)
                b_use[e] = p.v("sc")
                for kw_ in range(8):
                    _, w2cur = s2.acquire(t_)
                    for k2 in range(2):
                        kc = kw_ * 2 + k2
                        half = kc % 2
                        t_.w("sc", h2ev[half])
                        if kc == 0:
                            t_.w("sc", h1ev[0])
                            t_.w("sc", h1ev[1])
                        for hc in range(16):
                            for n in range(2):
                                t_.o(lambda e_, h2=hc, nn=n, hh=half,
                                     wb=w2cur, kk=k2:
                                     e_.matmul(
                                         psQ[:, hh * 1024 + nn * 512:
                                             hh * 1024 + (nn + 1) * 512],
                                         wb[:, h2, kk * 128:(kk + 1) * 128],
                                         h1g[:, h2,
                                             nn * 512:(nn + 1) * 512],
                                         start=(h2 == 0), stop=(h2 == 15)),
                                     inc="te" if (hc == 15 and n == 1)
                                     else None)
                        c.w("te", p.v("te"))
                        h2ev[half] = c.o(
                            lambda e_, hh=half, k3=kc, sl=e % 2:
                            e_.activation(
                                h2g[:, k3, :],
                                psQ[:, hh * 1024:hh * 1024 + 1024],
                                AF.Gelu, bias=b2t[sl][:, k3:k3 + 1]),
                            inc="sc")
                    s2.release(g)
                for tt in range(8):
                    tg = th * 8 + tt
                    sl = tt % 2
                    if tt == 0:
                        t_.w("sc", h2ev[0])
                        t_.w("sc", h2ev[1])
                    for fh in range(2):
                        psx = [G["psA"], G["psB"]][fh]
                        wb3 = w3b[(s3.next_use + fh) % 2]
                        t_.w("sc", w3ev[fh])
                        for kc in range(16):
                            t_.o(lambda e_, k3=kc, ps=psx, wb=wb3, t2=tt:
                                 e_.matmul(ps[:],
                                           h2g[:, k3,
                                               t2 * 128:(t2 + 1) * 128],
                                           wb[:, k3, :],
                                           start=(k3 == 0), stop=(k3 == 15)),
                                 inc="te" if kc == 15 else None)
                        c.w("te", p.v("te"))
                        if w3acc[sl]:
                            c.w("dG", w3acc[sl])
                        w3ev[fh] = c.o(
                            lambda e_, f2=fh, s2=sl, tg2=tg, ee=e, ps=psx:
                            e_.activation(
                                w3s[s2][:, f2 * 512:(f2 + 1) * 512], ps[:],
                                AF.Identity, scale=ga[:, tg2, ee:ee + 1]),
                            inc="sc")
                    g.w("sc", w3ev[1])
                    w3acc[sl] = g.o(
                        lambda e_, s2=sl, tg2=tg, ee=e: e_.dma_start(
                            out=moe_dram[tg2 * 128:(tg2 + 1) * 128, :],
                            in_=w3s[s2][:],
                            accum_op=(ALU.bypass if ee == 0 else ALU.add)),
                        inc="dG", n=16)
            s3.release(g)
            s3.release(g)
    ph.close()



def _p3_ffn(p, ctx, xnT):
    A, G, sb = ctx["A"], ctx["G"], ctx["sb"]
    moe_dram = ctx["moe_dram"]
    wexp_bf, wcon_bf = ctx["wexp_bf"], ctx["wcon_bf"]
    rp_sem = ctx["rp_sem"]
    ph = contextlib.ExitStack()
    TS = 256
    NS = SC // TS
    sl4 = [sb("f_sl%d" % i, (128, 32, TS), BF16, ph) for i in range(4)]
    wexpb = [sb("f_we%d" % i, (128, NCH, 512), BF16, ph) for i in range(2)]
    wconb = [sb("f_wc%d" % i, (128, 32, 256), BF16, ph) for i in range(2)]
    tmpb = [sb("f_tmp%d" % i, (128, TS), F32, ph) for i in range(4)]
    cstage = [sb("f_cs%d" % i, (128, D), F32, ph) for i in range(4)]
    bwd = sb("f_bwd", (128, 12, 32), F32, ph)
    bwo = sb("f_bwo", (128, 12, 32), F32, ph)
    dwd = sb("f_dwd", (128, 12, 32), F32, ph)
    dwo = sb("f_dwo", (128, 12, 32), F32, ph)
    bexp = sb("f_bexp", (128, 32), F32, ph)
    upb = sb("f_upb", (128, 32), F32, ph)

    with Sec(p, "sync") as s:
        s.dma(bexp[:], A["bexp_r"])
        cb_t = s.dma(upb[:], A["upb_r"])
    with Sec(p, "sync") as s:
        s.dma(bwd[:], A["up_wd"].rearrange("s p c -> p s c"))
        s.dma(bwo[:], A["up_wo"].rearrange("s p c -> p s c"))
        s.dma(dwd[:], A["dn_wd"].rearrange("s p c -> p s c"))
        bw_t = s.dma(dwo[:], A["dn_wo"].rearrange("s p c -> p s c"))

    banks = [G["psA"], G["psB"], G["psC"], G["psD"]]
    we_t, wc_t = {}, {}
    we_use, wc_use = {}, {}
    weq = {}
    muq = {}
    mdq = {}
    wcq = {}
    mupd = {}
    gelq = {}
    stqfin = {}
    csfree = [0, 0, 0, 0]
    stq4 = [0, 0, 0, 0]

    def load_wexp(g, key):
        if key >= NS * 8:
            return
        g.w("te", we_use.get(key - 2, 0))
        we_t[key] = g.dma(
            wexpb[key % 2][:],
            wexp_bf[key % 8].rearrange("p (c h) -> p c h", c=NCH), "dW")

    def load_wcon(g, key):
        if key >= NS * 4:
            return
        g.w("te", wc_use.get(key - 2, 0))
        wc_t[key] = g.dma(
            wconb[key % 2][:],
            wcon_bf[key % 4].rearrange("p (c f) -> p c f", c=32), "dW")

    with Sec(p, "tensor") as t_, Sec(p, "scalar") as c, \
            Sec(p, "gpsimd") as g:
        g.ops.append(lambda e: e.wait_ge(rp_sem, 192))
        load_wexp(g, 0)
        load_wexp(g, 1)
        load_wcon(g, 0)
        load_wcon(g, 1)
        c.w("dS", cb_t)
        c.w("dS", bw_t)
        for hc in range(32):
            wi = hc // 4
            key = wi
            bk = hc % 4
            t_.w("dW", we_t[key])
            t_.w("sc", weq.get((0, hc - 4), 0))
            for dc in range(NCH):
                t_.o(lambda e, d=dc, h2=hc % 4, k2=key % 2, b2=bk:
                     e.matmul(banks[b2][:, 0:TS],
                              wexpb[k2][:, d, h2 * 128:(h2 + 1) * 128],
                              xnT[:, d, 0:TS],
                              start=(d == 0), stop=(d == NCH - 1)),
                     inc="te" if dc == NCH - 1 else None)
            we_use[key] = p.v("te")
            if hc % 4 == 3:
                load_wexp(g, key + 2)
            c.w("te", p.v("te"))
            weq[(0, hc)] = c.o(lambda e, h2=hc, b2=bk: e.activation(
                sl4[0][:, h2, :], banks[b2][:, 0:TS], AF.Identity,
                bias=bexp[:, h2:h2 + 1]), inc="sc")
        for cc in range(32):
            bk = cc % 4
            t_.w("sc", weq[(0, min(28 + bk, 31))])
            t_.w("sc", muq.get((0, cc - 4), 0))
            t_.o(lambda e, c2=cc, b2=bk: e.matmul(
                banks[b2][:, 0:TS], G["Mup"][:, c2 * 128:(c2 + 1) * 128],
                sl4[0][:, c2, :], start=True, stop=True), inc="te")
            c.w("te", p.v("te"))
            muq[(0, cc)] = c.o(lambda e, c2=cc, b2=bk: e.activation(
                sl4[1][:, c2, :], banks[b2][:, 0:TS], AF.Identity),
                inc="sc")
        mupd[0] = p.v("sc")

    for w in range(NS):
        P_ = sl4[2 * (w % 2)]
        Q_ = sl4[2 * (w % 2) + 1]
        Pn = sl4[2 * ((w + 1) % 2)]
        Qn = sl4[2 * ((w + 1) % 2) + 1]
        has_wc = w >= 1
        has_we = w + 1 < NS
        n_wc = 8 if has_wc else 0
        n_we = 32 if has_we else 0
        te0 = p.v("te")
        te_wc = [te0 + g2 + 1 for g2 in range(n_wc)]
        te_md = [te0 + n_wc + c2 + 1 for c2 in range(32)]
        te_we = [te0 + n_wc + 32 + h2 + 1 for h2 in range(n_we)]
        te_mu = [te0 + n_wc + 32 + n_we + c2 + 1 for c2 in range(n_we)]

        if has_we:
            for wi in range(8):
                we_use[(w + 1) * 8 + wi] = te_we[4 * wi + 3]
        if has_wc:
            for fq in range(4):
                wc_use[(w - 1) * 4 + fq] = te_wc[fq * 2 + 1]

        with Sec(p, "tensor") as t_, Sec(p, "scalar") as c, \
                Sec(p, "vector") as v, Sec(p, "sync") as s, \
                Sec(p, "gpsimd") as g:
            if w >= 1:
                for j in range(4):
                    load_wcon(g, (w - 1) * 4 + 2 + j)
            if has_we:
                for wi in range(8):
                    load_wexp(g, (w + 1) * 8 + wi + 2)
            wc_ev = []
            if has_wc:
                for g2 in range(8):
                    fq, tsub = g2 // 2, g2 % 2
                    tile = ((w - 1) % 2) * 2 + tsub
                    bk = 2 + (g2 % 2)

                    def _wcev(g3=g2, f3=fq, t3=tile, b3=bk, w3=w):
                        c.w("te", te_wc[g3])
                        if f3 == 0:
                            c.w("dS", csfree[t3])
                        wcq[(w3 - 1, g3)] = c.o(
                            lambda e, f4=f3, t4=t3, b4=b3: e.activation(
                                cstage[t4][:, f4 * 256:(f4 + 1) * 256],
                                banks[b4][:, 0:256], AF.Identity), inc="sc")
                    wc_ev.append(_wcev)
            we_ev = []
            if has_we:
                for hc in range(32):
                    bk = hc % 4

                    def _weev(h3=hc, b3=bk, w3=w):
                        c.w("te", te_we[h3])
                        if h3 == 0:
                            c.w("ve", stqfin.get(w3 - 1, 0))
                        weq[(w3 + 1, h3)] = c.o(
                            lambda e, h4=h3, b4=b3: e.activation(
                                Pn[:, h4, :], banks[b4][:, 0:TS],
                                AF.Identity,
                                bias=bexp[:, h4:h4 + 1]), inc="sc")
                    we_ev.append(_weev)

            opn = 0
            cur = [Q_, P_]
            for sg in range(5):
                k = 1 << sg
                stg = 7 + sg
                f_, d_ = cur
                for cc in range(32):
                    sl = cc % 4
                    c.w("ve", stq4[sl])
                    stq_sc = c.o(lambda e, c2=cc, s2=stg, f2=f_, sl2=sl:
                                 e.activation(tmpb[sl2][:], f2[:, c2, :],
                                              AF.Identity,
                                              scale=bwd[:, s2, c2:c2 + 1]),
                                 inc="sc")
                    v.w("sc", stq_sc)
                    if sg == 0 and cc == 0:
                        v.w("sc", mupd[w])
                    stq4[sl] = v.o(
                        lambda e, c2=cc, s2=stg, f2=f_, d2=d_, kk=k, sl2=sl:
                        e.scalar_tensor_tensor(
                            d2[:, c2, :], f2[:, c2 ^ kk, :],
                            bwo[:, s2, c2:c2 + 1], tmpb[sl2][:],
                            ALU.mult, ALU.add), inc="ve")
                    opn += 1
                    if opn % 24 == 0 and wc_ev:
                        wc_ev.pop(0)()
                cur = [d_, f_]
            c.w("ve", stq4[3])
            for cc in range(32):
                gelq[w] = c.o(lambda e, c2=cc: e.activation(
                    Q_[:, c2, :], P_[:, c2, :], AF.Gelu,
                    bias=upb[:, c2:c2 + 1]), inc="sc")
                opn += 1
                if opn % 24 == 0 and wc_ev:
                    wc_ev.pop(0)()
            while wc_ev:
                wc_ev.pop(0)()
            for cc in range(32):
                bk = cc % 4
                c.w("te", te_md[cc])
                mdq[(w, cc)] = c.o(lambda e, c2=cc, b2=bk: e.activation(
                    P_[:, c2, :], banks[b2][:, 0:TS], AF.Identity),
                    inc="sc")
            opn = 0
            cur = [P_, Q_]
            for sg in range(5):
                k = 1 << sg
                stg = 7 + sg
                f_, d_ = cur
                for cc in range(32):
                    sl = cc % 4
                    c.w("ve", stq4[sl])
                    stq_sc = c.o(lambda e, c2=cc, s2=stg, f2=f_, sl2=sl:
                                 e.activation(tmpb[sl2][:], f2[:, c2, :],
                                              AF.Identity,
                                              scale=dwd[:, s2, c2:c2 + 1]),
                                 inc="sc")
                    v.w("sc", stq_sc)
                    if sg == 0 and cc == 0:
                        v.w("sc", mdq[(w, 31)])
                    stq4[sl] = v.o(
                        lambda e, c2=cc, s2=stg, f2=f_, d2=d_, kk=k, sl2=sl:
                        e.scalar_tensor_tensor(
                            d2[:, c2, :], f2[:, c2 ^ kk, :],
                            dwo[:, s2, c2:c2 + 1], tmpb[sl2][:],
                            ALU.mult, ALU.add), inc="ve")
                    opn += 1
                    if opn % 5 == 0 and we_ev:
                        we_ev.pop(0)()
                cur = [d_, f_]
            while we_ev:
                we_ev.pop(0)()
            stqfin[w] = p.v("ve")
            if has_we:
                for cc in range(32):
                    bk = cc % 4
                    c.w("te", te_mu[cc])
                    muq[(w + 1, cc)] = c.o(
                        lambda e, c2=cc, b2=bk: e.activation(
                            Qn[:, c2, :], banks[b2][:, 0:TS], AF.Identity),
                        inc="sc")
                mupd[w + 1] = p.v("sc")

            Qp = sl4[2 * ((w - 1) % 2) + 1]
            if has_wc:
                t_.w("ve", stqfin[w - 1])
                for g2 in range(8):
                    fq, tsub = g2 // 2, g2 % 2
                    key = (w - 1) * 4 + fq
                    bk = 2 + (g2 % 2)
                    t_.w("dW", wc_t[key])
                    if g2 < 2:
                        t_.w("sc", muq[(w, 28 + bk)])
                    else:
                        t_.w("sc", wcq[(w - 1, g2 - 2)])
                    for hc in range(32):
                        t_.o(lambda e, h2=hc, k2=key % 2, b2=bk, t3=tsub:
                             e.matmul(banks[b2][:, 0:256],
                                      Qp[:, h2, t3 * 128:(t3 + 1) * 128],
                                      wconb[k2][:, h2, :],
                                      start=(h2 == 0), stop=(h2 == 31)),
                             inc="te" if hc == 31 else None)
                    assert p.v("te") == te_wc[g2]
            t_.w("sc", gelq[w])
            for cc in range(32):
                bk = cc % 4
                if cc < 4:
                    if has_wc and bk >= 2:
                        t_.w("sc", wcq[(w - 1, 6 + (bk - 2))])
                    else:
                        t_.w("sc", muq[(w, 28 + bk)])
                else:
                    t_.w("sc", mdq[(w, cc - 4)])
                t_.o(lambda e, c2=cc, b2=bk: e.matmul(
                    banks[b2][:, 0:TS], G["Mdn"][:, c2 * 128:(c2 + 1) * 128],
                    Q_[:, c2, :], start=True, stop=True), inc="te")
                assert p.v("te") == te_md[cc]
            if has_we:
                for hc in range(32):
                    wi = hc // 4
                    key = (w + 1) * 8 + wi
                    bk = hc % 4
                    t_.w("dW", we_t[key])
                    if hc < 4:
                        t_.w("sc", mdq[(w, 28 + bk)])
                    else:
                        t_.w("sc", weq[(w + 1, hc - 4)])
                    for dc in range(NCH):
                        t_.o(lambda e, d=dc, h2=hc % 4, k2=key % 2, b2=bk,
                             w3=w: e.matmul(
                                 banks[b2][:, 0:TS],
                                 wexpb[k2][:, d, h2 * 128:(h2 + 1) * 128],
                                 xnT[:, d, (w3 + 1) * TS:(w3 + 2) * TS],
                                 start=(d == 0), stop=(d == NCH - 1)),
                             inc="te" if dc == NCH - 1 else None)
                    assert p.v("te") == te_we[hc]
                for cc in range(32):
                    bk = cc % 4
                    t_.w("sc", weq[(w + 1, min(28 + bk, 31))])
                    t_.w("sc", weq[(w + 1, cc)])
                    t_.w("sc", muq.get((w + 1, cc - 4), 0))
                    t_.o(lambda e, c2=cc, b2=bk: e.matmul(
                        banks[b2][:, 0:TS],
                        G["Mup"][:, c2 * 128:(c2 + 1) * 128],
                        Pn[:, c2, :], start=True, stop=True), inc="te")
                    assert p.v("te") == te_mu[cc]

            if has_wc:
                for tsub in range(2):
                    tile = ((w - 1) % 2) * 2 + tsub
                    s.w("sc", wcq[(w - 1, 6 + tsub)])
                    row = (w - 1) * TS + tsub * 128
                    csfree[tile] = s.dma(moe_dram[row:row + 128, :],
                                         cstage[tile][:])

    w = NS
    with Sec(p, "tensor") as t_, Sec(p, "scalar") as c, \
            Sec(p, "sync") as s, Sec(p, "gpsimd") as g:
        Qp = sl4[2 * ((w - 1) % 2) + 1]
        t_.w("ve", stqfin[w - 1])
        for g2 in range(8):
            fq, tsub = g2 // 2, g2 % 2
            key = (w - 1) * 4 + fq
            tile = ((w - 1) % 2) * 2 + tsub
            bk = 2 + (g2 % 2)
            t_.w("dW", wc_t[key])
            if g2 >= 2:
                t_.w("sc", wcq[(w - 1, g2 - 2)])
            else:
                t_.w("sc", mdq[(NS - 1, 28 + bk)])
            for hc in range(32):
                t_.o(lambda e, h2=hc, k2=key % 2, b2=bk, t3=tsub:
                     e.matmul(banks[b2][:, 0:256],
                              Qp[:, h2, t3 * 128:(t3 + 1) * 128],
                              wconb[k2][:, h2, :],
                              start=(h2 == 0), stop=(h2 == 31)),
                     inc="te" if hc == 31 else None)
            wc_use[key] = p.v("te")
            if g2 % 2 == 1 and key + 2 < NS * 4:
                load_wcon(g, key + 2)
            c.w("te", p.v("te"))
            if fq == 0:
                c.w("dS", csfree[tile])
            wcq[(w - 1, g2)] = c.o(
                lambda e, f4=fq, t4=tile, b4=bk: e.activation(
                    cstage[t4][:, f4 * 256:(f4 + 1) * 256],
                    banks[b4][:, 0:256], AF.Identity), inc="sc")
        for tsub in range(2):
            tile = ((w - 1) % 2) * 2 + tsub
            s.w("sc", wcq[(w - 1, 6 + tsub)])
            row = (w - 1) * TS + tsub * 128
            csfree[tile] = s.dma(moe_dram[row:row + 128, :],
                                 cstage[tile][:])
    ph.close()



def _p4_attn(p, ctx, L, xnT, attnout, pes):
    A, G, sb = ctx["A"], ctx["G"], ctx["sb"]
    kvpK, kviK, kvp, kvi = ctx["kvpK"], ctx["kviK"], ctx["kvp"], ctx["kvi"]
    qfm_dram = ctx["qfm_dram"]
    ph = contextlib.ExitStack()
    Kt = [sb("a_K%d_%d" % (L, i), (65, SK), BF16, ph) for i in range(4)]
    Vt = [sb("a_V%d_%d" % (L, i), (128, NSK, 130), BF16, ph)
          for i in range(2)]
    Qt = [sb("a_Q%d_%d" % (L, i), (65, SC), BF16, ph) for i in range(4)]
    PT = sb("a_PT%d" % L, (128, 2048), BF16, ph)
    G["sq_bf"] = sb("a_sq%d" % L, (64, SK), BF16, ph)
    diagm = sb("a_dm%d" % L, (128, 4, 512), BF16, ph)
    G["diagm"] = diagm
    with Sec(p, "gpsimd") as g0:
        g0.dma(diagm[:], A["diagmask"].rearrange("v p n -> p v n"), "dG")
    pvs = [sb("a_pvs%d_%d" % (L, i), (64, 512), F32, ph) for i in range(2)]
    rs = [sb("a_rs%d_%d" % (L, i), (1, 1024), F32, ph) for i in range(2)]
    rsb = [sb("a_rsb%d_%d" % (L, i), (1, 512), BF16, ph) for i in range(2)]
    if L == 0:
        wqb = sb("a_wq", (128, NCH, D), BF16, ph)
        bqt = sb("a_bq", (128, NCH), F32, ph)

    with Sec(p, "vector") as v:
        for i in range(2):
            v.o(lambda e, i2=i: e.memset(Vt[i2][:, :, 64:65], 1.0))
            v.o(lambda e, i2=i: e.memset(Vt[i2][:, :, 129:130], 1.0))

    with Sec(p, "sync") as s, Sec(p, "vector") as v:
        s.w("dG", p.v("dG"))
        s.dma(G["kmax_bf"][0:1, 1:2], kvp[STATROW:STATROW + 1, 0:1])
        kd2 = s.dma(G["kmax_bf"][0:1, 2:3], kvi[STATROW:STATROW + 1, 0:1])
        v.w("dS", kd2)
        v.o(lambda e: e.tensor_copy(G["kmax"][0:1, 1:2],
                                    G["kmax_bf"][0:1, 1:2]))
        v.o(lambda e: e.tensor_copy(G["kmax"][0:1, 2:3],
                                    G["kmax_bf"][0:1, 2:3]))
        v.drain()
        kmq = v.o(lambda e: e.tensor_tensor(
            G["kmax"][0:1, 1:2], G["kmax"][0:1, 1:2], G["kmax"][0:1, 2:3],
            ALU.max), inc="ve")

    wq_t = 0
    bq8 = 0
    if L == 0:
        with Sec(p, "gpsimd") as g, Sec(p, "sync") as s, \
                Sec(p, "scalar") as c:
            p.fence(g)
            wq_t = g.dma(wqb[:],
A["wqT"].rearrange("(c p) d -> p c d", p=128), "dW")
            bq_d = s.dma(bqt[:], A["bq_r"])
            c.w("dS", bq_d)
            bq8 = c.o(lambda e: e.activation(bqt[:], bqt[:], AF.Identity,
                                             scale=RSQ), inc="sc")

    kv_pull = {}
    q_done = {}
    bq_last = [0]

    def pulls(g_idx):
        sl = g_idx % 2
        with Sec(p, "sync") as s, Sec(p, "scalar") as c:
            s.w("te", p.v("te"))
            c.w("te", p.v("te"))
            for hp in range(2):
                h = 2 * g_idx + hp
                s.dma(Kt[2 * sl + hp][:, 0:SC], kvpK[65 * h:65 * h + 65, :])
                s.dma(Kt[2 * sl + hp][:, SC:SK], kviK[65 * h:65 * h + 65, :])
            last = 0
            for j in range(NSK):
                sec = s if j % 2 == 0 else c
                src = kvp if j < 16 else kvi
                row0 = VBASE + (j % 16) * 128
                last = sec.o(
                    lambda e, s2=sl, j2=j, r0=row0, sr=src, g2=g_idx:
                    e.dma_start(
                        out=Vt[s2][:, j2, 0:130].rearrange(
                            "p (h d) -> p h d", d=65)[:, :, 0:64],
                        in_=sr[r0:r0 + 128, g2 * 128:(g2 + 1) * 128]),
                    inc=("dS" if j % 2 == 0 else "dC"), n=16)
            kv_pull[g_idx] = (p.v("dS"), p.v("dC"))

    def build_q(g_idx):
        sl = g_idx % 2
        with Sec(p, "tensor") as t_, Sec(p, "scalar") as c, \
                Sec(p, "vector") as v, Sec(p, "sync") as s:
            for hp in range(2):
                h = 2 * g_idx + hp
                qtile = Qt[2 * sl + hp]
                t_.w("ve", bq_last[0])
                if L == 0:
                    t_.w("dW", wq_t)
                    for s4 in range(4):
                        t_.w("sc", p.v("sc"))
                        for dc in range(NCH):
                            t_.o(lambda e, d=dc, g2=g_idx, ss=s4: e.matmul(
                                G["psD"][:],
                                wqb[:, d, g2 * 128:(g2 + 1) * 128],
                                xnT[:, d, ss * 512:(ss + 1) * 512],
                                start=(d == 0), stop=(d == NCH - 1)),
                                inc="te" if dc == NCH - 1 else None)
                        c.w("te", p.v("te"))
                        c.w("sc", bq8)
                        c.o(lambda e, ss=s4, hp2=hp, q2=qtile, g2=g_idx:
                            e.activation(
                                q2[0:64, ss * 512:(ss + 1) * 512],
                                G["psD"][hp2 * 64:hp2 * 64 + 64, :],
                                AF.Identity, scale=RSQ,
                                bias=bqt[hp2 * 64:hp2 * 64 + 64,
                                         g2:g2 + 1]), inc="sc")
                else:
                    dq = s.dma(qtile[0:64, :],
                               qfm_dram[g_idx * 128 + hp * 64:
                                        g_idx * 128 + hp * 64 + 64, :])
                    c.w("dS", dq)
                c.w("sc", p.v("sc"))
                c.w("te", p.v("te"))
                sq_q = c.o(lambda e, q2=qtile: e.activation(
                    G["sq_bf"][0:64, 0:SC], q2[0:64, :], AF.Square),
                    inc="sc")
                for s4 in range(4):
                    t_.w("sc", sq_q)
                    t_.w("ve", p.v("ve"))
                    tq = t_.o(lambda e, ss=s4: e.matmul(
                        G["psD"][0:1, :], G["ones64"][:],
                        G["sq_bf"][0:64, ss * 512:(ss + 1) * 512],
                        start=True, stop=True), inc="te")
                    v.w("te", tq)
                    v.w("ve", kmq)
                    q_done[(g_idx, hp)] = v.o(
                        lambda e, ss=s4, q2=qtile: e.tensor_scalar(
                            q2[64:65, ss * 512:(ss + 1) * 512],
                            G["psD"][0:1, :], G["kmax"][0:1, 1:2], -0.5,
                            ALU.add, ALU.mult), inc="ve")
                bq_last[0] = q_done[(g_idx, hp)]

    pulls(0)
    build_q(0)

    epi = {"pend": None, "tt0": 0, "tt1": 0}

    def epi_rd(t_, c, v, ci):
        c0p, hpp, pvd, g2p = epi["pend"]
        pspv = [G["psA"], G["psB"]][ci]
        c.w("te", pvd[ci])
        if epi["tt%d" % ci]:
            c.w("ve", epi["tt%d" % ci])
        epi["pq%d" % ci] = c.o(lambda e, pv=pspv, c2=ci: e.activation(
            pvs[c2][:], pv[0:64, :], AF.Identity), inc="sc")
        v.w("te", pvd[ci])
        epi["rd%d" % ci] = v.o(lambda e, pv=pspv, c2=ci: e.tensor_copy(
            rs[c2][0:1, 0:512], pv[64:65, :]), inc="ve")

    def epi_rc(t_, c, v, ci):
        v.drain()
        v.o(lambda e, c2=ci: e.reciprocal(rs[c2][0:1, 512:1024],
                                          rs[c2][0:1, 0:512]))
        v.drain()
        epi["vq%d" % ci] = v.o(lambda e, c2=ci: e.tensor_copy(
            rsb[c2][0:1, :], rs[c2][0:1, 512:1024]), inc="ve")

    def epi_a(t_, c, v, ci):
        epi_rd(t_, c, v, ci)
        epi_rc(t_, c, v, ci)

    def epi_b(t_, c, v):
        c0p, hpp, pvd, g2p = epi["pend"]
        if epi["tt1"]:
            t_.w("ve", epi["tt1"])
        tqs = [0, 0]
        for ci in range(2):
            t_.w("ve", epi["vq%d" % ci])
            tqs[ci] = t_.o(lambda e, c2=ci: e.matmul(
                G["psC"][c2 * 64:c2 * 64 + 64, :], G["ones_col"][:],
                rsb[c2][0:1, :], start=True, stop=True), inc="te")
        for ci in range(2):
            s4 = c0p + ci
            v.w("te", tqs[1])
            v.w("sc", epi["pq%d" % ci])
            epi["tt%d" % ci] = v.o(
                lambda e, hp2=hpp, g2=g2p, ss=s4, c2=ci:
                e.tensor_tensor(
                    attnout[hp2 * 64:hp2 * 64 + 64, g2,
                            ss * 512:(ss + 1) * 512],
                    pvs[c2][:], G["psC"][c2 * 64:c2 * 64 + 64, :],
                    ALU.mult),
                inc="ve")

    for g_idx in range(NH // 2):
        sl = g_idx % 2
        if g_idx + 1 < NH // 2:
            pulls(g_idx + 1)
            build_q(g_idx + 1)
        with Sec(p, "tensor") as t_, Sec(p, "scalar") as c, \
                Sec(p, "vector") as v:
            t_.w("dS", kv_pull[g_idx][0])
            t_.w("dC", kv_pull[g_idx][1])
            exq = [0, 0]
            pvq = [0, 0]
            for hp in range(2):
                h = 2 * g_idx + hp
                ktile = Kt[2 * sl + hp]
                qtile = Qt[2 * sl + hp]
                t_.w("ve", q_done[(g_idx, hp)])
                for sw in range(2):
                    c0 = 2 * sw
                    jm = [20 + 4 * c0, 20 + 4 * c0 + 4]
                    pvps = [G["psA"], G["psB"]]
                    pvd = [0, 0]

                    def emit_scores(j):
                        ph2 = j % 2
                        valid = [ci for ci in range(2) if j < jm[ci]]
                        cstart = valid[0]
                        wid = len(valid) * 512
                        t_.w("sc", exq[ph2])
                        for ci in valid:
                            t_.o(lambda e, jj=j, q4=ci, ss=c0, kt=ktile,
                                 qt=qtile, hh=ph2: e.matmul(
                                     G["psQ"][:, hh * 1024 + q4 * 512:
                                              hh * 1024 + (q4 + 1) * 512],
                                     kt[:, jj * 128:(jj + 1) * 128],
                                     qt[:, (ss + q4) * 512:
                                        (ss + q4 + 1) * 512],
                                     start=True, stop=True),
                                 inc="te" if ci == valid[-1] else None)
                        c.w("te", p.v("te"))
                        if pvq[ph2]:
                            c.w("te", pvq[ph2])
                        kill = j < 16
                        exq[ph2] = c.o(
                            lambda e, hh=ph2, k2=kill, cs=cstart, w2=wid:
                            e.activation(
                                PT[:, hh * 1024 + cs * 512:
                                   hh * 1024 + cs * 512 + w2],
                                G["psQ"][:, hh * 1024 + cs * 512:
                                         hh * 1024 + cs * 512 + w2],
                                AF.Exp,
                                bias=(G["kill_t"][:, 0:1] if k2 else 0.0)),
                            inc="sc")
                        nm = 0
                        mq = 0
                        for ci in valid:
                            jl = j - 16 - 4 * (c0 + ci)
                            if 0 <= jl <= 3:
                                if nm == 0:
                                    v.w("sc", exq[ph2])
                                nm += 1
                                mq = v.o(lambda e, hh=ph2, q4=ci, vv=jl:
                                         e.tensor_tensor(
                                             PT[:, hh * 1024 + q4 * 512:
                                                hh * 1024 + (q4 + 1) * 512],
                                             PT[:, hh * 1024 + q4 * 512:
                                                hh * 1024 + (q4 + 1) * 512],
                                             G["diagm"][:, vv, :], ALU.mult),
                                         inc="ve")
                        return (exq[ph2], mq if nm else 0)

                    def emit_pv(j, gates):
                        ph2 = j % 2
                        valid = [ci for ci in range(2) if j < jm[ci]]
                        sq2, mq = gates
                        if mq:
                            t_.w("ve", mq)
                        else:
                            t_.w("sc", sq2)
                        for ci in valid:
                            t_.o(lambda e, jj=j, q4=ci, pv=pvps[ci], s2=sl,
                                 hp2=hp, hh=ph2, jmx=jm[ci]: e.matmul(
                                     pv[0:65, :],
                                     Vt[s2][:, jj, hp2 * 65:hp2 * 65 + 65],
                                     PT[:, hh * 1024 + q4 * 512:
                                        hh * 1024 + (q4 + 1) * 512],
                                     start=(jj == 0), stop=(jj == jmx - 1)),
                                 inc="te" if ci == valid[-1] else None)
                        pvq[ph2] = p.v("te")
                        for ci in valid:
                            if j == jm[ci] - 1:
                                pvd[ci] = p.v("te")

                    pend0 = epi["pend"] is not None
                    gates = {}
                    gates[0] = emit_scores(0)
                    if pend0:
                        epi_rd(t_, c, v, 0)
                        epi_rd(t_, c, v, 1)
                        epi_rc(t_, c, v, 0)
                        epi_rc(t_, c, v, 1)
                    gates[1] = emit_scores(1)
                    if pend0:
                        t_.w("sc", epi["pq0"])
                        t_.w("ve", epi["rd0"])
                        t_.w("sc", epi["pq1"])
                        t_.w("ve", epi["rd1"])
                    emit_pv(0, gates[0])
                    gates[2] = emit_scores(2)
                    emit_pv(1, gates[1])
                    gates[3] = emit_scores(3)
                    emit_pv(2, gates[2])
                    if pend0:
                        epi_b(t_, c, v)
                    for j in range(4, jm[1]):
                        gates[j] = emit_scores(j)
                        emit_pv(j - 1, gates[j - 1])
                    emit_pv(jm[1] - 1, gates[jm[1] - 1])
                    epi["pend"] = (c0, hp, pvd[:], g_idx)

    with Sec(p, "tensor") as t_, Sec(p, "scalar") as c, \
            Sec(p, "vector") as v:
        epi_a(t_, c, v, 0)
        epi_a(t_, c, v, 1)
        epi_b(t_, c, v)
        epi["pend"] = None
        epi["tt0"] = 0
        epi["tt1"] = 0
        t_.w("ve", p.v("ve"))
    if "attnout" in ctx["dbg"] and L == 0:
        with Sec(p, "sync") as s:
            s.w("ve", p.v("ve"))
            s.dma(ctx["dbg"]["attnout"],
                  attnout[:].rearrange("p c s -> p (c s)"))
    p.barrier()
    ph.close()



def _p5_residual(p, ctx, L, attnout, x_in, x_out):
    A, G, sb = ctx["A"], ctx["G"], ctx["sb"]
    moe_dram = ctx["moe_dram"]
    ph = contextlib.ExitStack()
    wob = sb("p5wo%d" % L, (128, NCH, D), BF16, ph)
    G["biasrep"] = sb("p5br%d" % L, (128, D), F32, ph)
    G["biasrep2"] = sb("p5br2%d" % L, (128, D), F32, ph)
    xb = [sb("p5x%d_%d" % (L, i), (128, D), F32, ph) for i in range(2)]
    mb = [sb("p5m%d_%d" % (L, i), (128, D), F32, ph) for i in range(2)]
    ob = [sb("p5o%d_%d" % (L, i), (128, D), F32, ph) for i in range(2)]

    with Sec(p, "gpsimd") as g:
        p.fence(g)
        wo_t = g.dma(wob[:],
                     A["woT" if L == 0 else "ow"]
                     .rearrange("(c p) d -> p c d", p=128), "dW")
    with Sec(p, "sync") as s, Sec(p, "vector") as v:
        b1 = s.dma(G["biasrep"][:], A["bo_rep" if L == 0 else "bcon_rep"])
        bias_q = 0
        if L == 1:
            b2 = s.dma(G["biasrep2"][:], A["dbias_rep"])
            v.w("dS", b2)
            bias_q = v.o(lambda e: e.tensor_tensor(
                G["biasrep"][:], G["biasrep"][:], G["biasrep2"][:], ALU.add),
                inc="ve")

    ob_q = [0, 0]
    with Sec(p, "sync") as s, Sec(p, "scalar") as c, Sec(p, "tensor") as t_, \
            Sec(p, "vector") as v:
        t_.w("dW", wo_t)
        for t in range(NT):
            sl = t % 2
            if t >= 2:
                s.w("ve", ob_q[sl])
                c.w("ve", ob_q[sl])
            dx = s.dma(xb[sl][:], x_in[t * 128:(t + 1) * 128, :])
            dm = c.dma(mb[sl][:], moe_dram[t * 128:(t + 1) * 128, :],
                       inc="dC")
            if t >= 2:
                t_.w("ve", ob_q[sl])
            for fh in range(2):
                for dc in range(NCH):
                    lastmm = (dc == NCH - 1) and L == 1
                    t_.o(lambda e, d=dc, f2=fh, tt=t, s2=sl, lm=lastmm:
                         e.matmul(
                             G["psQ"][:, s2 * 1024 + f2 * 512:
                                      s2 * 1024 + (f2 + 1) * 512],
                             attnout[:, d, tt * 128:(tt + 1) * 128],
                             wob[:, d, f2 * 512:(f2 + 1) * 512],
                             start=(d == 0), stop=lm),
                         inc="te" if (lastmm and fh == 1) else None)
                if L == 0:
                    t_.o(lambda e, f2=fh, tt=t, s2=sl: e.matmul(
                        G["psQ"][:, s2 * 1024 + f2 * 512:
                                 s2 * 1024 + (f2 + 1) * 512],
                        G["wT_sb"][0:8, tt * 128:(tt + 1) * 128],
                        G["b3_sb"][0:8, f2 * 512:(f2 + 1) * 512],
                        start=False, stop=True),
                        inc="te" if fh == 1 else None)
            v.w("te", p.v("te"))
            v.w("dS", dx)
            v.w("dC", dm)
            v.w("ve", bias_q)
            v.o(lambda e, s2=sl: e.tensor_tensor(
                ob[s2][:], G["psQ"][:, s2 * 1024:s2 * 1024 + 1024],
                xb[s2][:], ALU.add))
            v.drain()
            v.o(lambda e, s2=sl: e.tensor_tensor(ob[s2][:], ob[s2][:],
                                                 mb[s2][:], ALU.add))
            v.drain()
            ob_q[sl] = v.o(lambda e, s2=sl: e.tensor_tensor(
                ob[s2][:], ob[s2][:], G["biasrep"][:], ALU.add), inc="ve")
            s.w("ve", ob_q[sl])
            s.dma(x_out[t * 128:(t + 1) * 128, :], ob[sl][:])
    ph.close()



def kernel(**inputs):
    shared, per_core = _host_prepare(inputs)
    nc = _build(DEBUG)
    in_maps = []
    for c in range(8):
        m = dict(shared)
        m.update(per_core[c])
        in_maps.append(m)
    res = run_bass_kernel_spmd(nc, in_maps, core_ids=list(range(8)))
    out = np.empty((4, 4096, D), np.float32)
    for c in range(8):
        b, hf = c // 2, c % 2
        out[b, hf * SC:(hf + 1) * SC, :] = res.results[c]["out_x"]
    return out


# version 08
# speedup vs baseline: 1.0377x; 1.0158x over previous
import math
import os
import contextlib
import numpy as np
import ml_dtypes

import concourse.bass as bass
import concourse.mybir as mybir
from concourse.bass_utils import run_bass_kernel_spmd
from concourse.bass_core_rust import IndirectOffsetOnAxis

F32 = mybir.dt.float32
BF16 = mybir.dt.bfloat16
I32 = mybir.dt.int32
AF = mybir.ActivationFunctionType
ALU = mybir.AluOpType
AX = mybir.AxisListType

D = 1024
SC = 2048
NT = SC // 128
NH = 16
HD = 64
NE = 8
HM = 2048
DH = 4096
NCH = D // 128
KROWS = NH * 65
VBASE = 2 * KROWS
STATROW = VBASE + SC
KVROWS = 4224
SK = 2 * SC
NSK = SK // 128
RSQ = float((1.0 / 8.0) ** 0.5)
EPS = 1e-5

DEBUG = ()
KPH = int(os.environ.get("KPH", "99"))
_PHC = [0]


def _ph_on():
    _PHC[0] += 1
    return _PHC[0] <= KPH

DEBUG_SHAPES = {
    "xn": (SC, D),
    "gate": (128, NT * NE),
    "attnout": (128, NCH * SC),
}


class Plan:
    ENGS = ("sync", "tensor", "scalar", "vector", "gpsimd")
    QUEUES = {"dS": 6, "dC": 4, "dW": 4, "dG": 6}

    def __init__(self, nc, block, es):
        self.nc = nc
        self.block = block
        self.cnt = {}
        self.sems = {}
        for name in ("cc", "te", "sc", "ve", "gp"):
            self.sems[name] = es.enter_context(nc.semaphore(name))
            self.cnt[name] = 0
        for eng in self.ENGS:
            self.sems["bar_" + eng] = es.enter_context(
                nc.semaphore("bar_" + eng))
        self.dlane = {}
        self.dcnt = {}
        self.dmap = {}
        for q, nl in self.QUEUES.items():
            self.cnt[q] = 0
            self.dlane[q] = [es.enter_context(nc.semaphore("%s%d" % (q, i)))
                             for i in range(nl)]
            self.dcnt[q] = [0] * nl
            self.dmap[q] = {}
        self.nbar = 0

    def inc(self, name, n=1):
        self.cnt[name] += n
        return self.sems[name]

    def v(self, name):
        return self.cnt[name]

    def fence(self, sec):
        for q in self.QUEUES:
            sec.w(q, self.cnt[q])
        for s in ("te", "sc", "ve"):
            sec.w(s, self.cnt[s])

    def queue_wait_ops(self, q, pseudo):
        if pseudo <= 0:
            return []
        snap = self.dmap[q].get(pseudo)
        if snap is None:
            keys = [k for k in self.dmap[q] if k >= pseudo]
            if keys:
                snap = self.dmap[q][min(keys)]
            else:
                snap = tuple(self.dcnt[q])
        return [(self.dlane[q][i], v) for i, v in enumerate(snap) if v > 0]

    def barrier(self):
        self.nbar += 1
        tgt = self.nbar
        dwaits = []
        for q in self.QUEUES:
            dwaits.extend((h, v) for h, v in
                          [(self.dlane[q][i], self.dcnt[q][i])
                           for i in range(len(self.dlane[q]))] if v > 0)
        for eng in self.ENGS:
            own = self.sems["bar_" + eng]
            others = [self.sems["bar_" + x] for x in self.ENGS if x != eng]
            seq_only = eng == "sync"
            def _(e, _t=tgt, _own=own, _o=tuple(others), _dw=tuple(dwaits),
                  _sq=seq_only):
                for h, v in _dw:
                    e.wait_ge(h, v)
                if _sq:
                    e.sem_inc(_own, 1)
                else:
                    e.drain().then_inc(_own, 1)
                for h in _o:
                    e.wait_ge(h, _t)
            getattr(self.block, eng)(_)
class Sec:
    def __init__(self, plan, eng):
        self.p = plan
        self.eng = eng
        self.ops = []

    def __enter__(self):
        return self

    def __exit__(self, et, ev, tb):
        if et is not None:
            return False
        ops = self.ops
        getattr(self.p.block, self.eng)(lambda e: [f(e) for f in ops])
        return False

    def w(self, sem, val):
        if not val or val <= 0:
            return
        if sem in self.p.QUEUES:
            for h, v in self.p.queue_wait_ops(sem, val):
                self.ops.append(lambda e, s=h, vv=v: e.wait_ge(s, vv))
        else:
            self.ops.append(
                lambda e, s=self.p.sems[sem], v=val: e.wait_ge(s, v))

    def o(self, fn, inc=None, n=1):
        if inc is None:
            self.ops.append(fn)
            return None
        if inc in self.p.QUEUES:
            return self._dma_o(fn, inc)
        sem = self.p.inc(inc, n)
        tgt = self.p.v(inc)

        def g(e, _f=fn, _s=sem, _n=n):
            _f(e).then_inc(_s, _n)
        self.ops.append(g)
        return tgt

    def _dma_o(self, fn, q):
        p = self.p
        li = (p.cnt[q] // 16) % len(p.dlane[q])
        prev = p.dcnt[q][li]
        if prev > 0:
            self.ops.append(
                lambda e, s=p.dlane[q][li], v=prev: e.wait_ge(s, v))
        p.dcnt[q][li] += 16
        p.cnt[q] += 16
        p.dmap[q][p.cnt[q]] = tuple(p.dcnt[q])
        sem = p.dlane[q][li]

        def g(e, _f=fn, _s=sem):
            _f(e).then_inc(_s, 16)
        self.ops.append(g)
        return p.cnt[q]

    def dma(self, out, in_, inc="dS", **kw):
        return self.o(lambda e, o=out, i=in_, k=kw:
                      e.dma_start(out=o, in_=i, **k), inc=inc, n=16)

    def drain(self):
        self.ops.append(lambda e: e.drain())



def _butterfly_coefs(w):
    ns = w.shape[0]
    i = np.arange(DH)
    wdiag = np.empty((ns, DH), np.float32)
    woff = np.empty((ns, DH), np.float32)
    for s in range(ns):
        s2 = 1 << s
        k = (i // (2 * s2)) * s2 + (i % s2)
        is_b = ((i // s2) % 2).astype(bool)
        wdiag[s] = np.where(is_b, w[s, k, 1, 1], w[s, k, 0, 0])
        woff[s] = np.where(is_b, w[s, k, 1, 0], w[s, k, 0, 1])
    woffx = np.empty_like(woff)
    for s in range(ns):
        woffx[s] = woff[s][i ^ (1 << s)]

    def jc(a):
        return np.ascontiguousarray(
            a.reshape(ns, 32, 128).transpose(0, 2, 1)).astype(np.float32)
    return jc(wdiag), jc(woff), jc(woffx)


def _host_prepare(inp):
    T = lambda a: np.ascontiguousarray(np.asarray(a, np.float32).T)
    C = lambda a: np.ascontiguousarray(np.asarray(a, np.float32))
    chunks = lambda a, n: np.ascontiguousarray(C(a).reshape(n, 128).T)
    rep = lambda a: np.ascontiguousarray(
        np.broadcast_to(C(a)[None, :], (128, np.asarray(a).shape[-1])))

    sh = {}
    sh["wqT"] = T(inp["L0_attn_Wq"]); sh["wkT"] = T(inp["L0_attn_Wk"])
    sh["wvT"] = T(inp["L0_attn_Wv"]); sh["woT"] = T(inp["L0_attn_Wo"])
    sh["bq_r"] = chunks(inp["L0_attn_bq"], NCH)
    sh["bk_r"] = chunks(inp["L0_attn_bk"], NCH)
    sh["bv_rep"] = rep(inp["L0_attn_bv"])
    sh["bo_rep"] = rep(inp["L0_attn_bo"])
    sh["w1"] = C(inp["L0_ffn_w1"]); sh["w2"] = C(inp["L0_ffn_w2"])
    sh["w3"] = C(inp["L0_ffn_w3"])
    sh["b1_r"] = np.stack([chunks(inp["L0_ffn_b1"][e], HM // 128)
                           for e in range(NE)])
    sh["b2_r"] = np.stack([chunks(inp["L0_ffn_b2"][e], HM // 128)
                           for e in range(NE)])
    sh["b3"] = C(inp["L0_ffn_b3"])
    sh["gate_w"] = C(inp["L0_ffn_gate_w"])
    sh["gate_b_rep"] = rep(inp["L0_ffn_gate_b"])
    sh["qw"] = C(inp["L1_attn_qw"]); sh["kw"] = C(inp["L1_attn_kw"])
    sh["vw"] = C(inp["L1_attn_vw"]); sh["ow"] = C(inp["L1_attn_ow"])
    sh["wexp_sw"] = np.ascontiguousarray(
        C(inp["L1_ffn_W_exp"]).reshape(8, 128, 8, 512)
        .transpose(2, 1, 0, 3).reshape(8, 128, 4096))
    sh["bexp_r"] = chunks(inp["L1_ffn_b_exp"], DH // 128)
    sh["upb_r"] = chunks(inp["L1_ffn_up_bias"], DH // 128)
    sh["wcon_sw"] = np.ascontiguousarray(
        C(inp["L1_ffn_W_con"]).reshape(32, 128, 4, 256)
        .transpose(2, 1, 0, 3).reshape(4, 128, 8192))
    sh["bcon_rep"] = rep(inp["L1_ffn_b_con"])
    sh["dbias_rep"] = rep(inp["L1_ffn_down_bias"])
    for pre, w in (("up", inp["L1_ffn_up_weights"]),
                   ("dn", inp["L1_ffn_down_weights"])):
        wd, wo, wox = _butterfly_coefs(C(w))
        sh[pre + "_wd"], sh[pre + "_wo"], sh[pre + "_wox"] = wd, wo, wox

    sh["eye"] = np.eye(128, dtype=np.float32)
    mx = np.zeros((7, 128, 128), np.float32)
    j = np.arange(128)
    for s in range(7):
        mx[s, j, j ^ (1 << s)] = 1
    sh["maskxor"] = mx
    dm = np.zeros((4, 128, 512), np.float32)
    pc = np.arange(128)[:, None]
    qc = np.arange(512)[None, :]
    for v in range(4):
        dm[v] = ((v * 128 + pc) <= qc)
    sh["diagmask"] = dm

    x_full = np.asarray(inp["L0_attn_x"], np.float32)
    inv = (10000.0 ** (-np.arange(0, HD, 2, dtype=np.float32) / HD)
           ).astype(np.float32)
    per_core = []
    for c in range(8):
        b, hf = c // 2, c % 2
        d = {}
        d["x0"] = np.ascontiguousarray(x_full[b, hf * SC:(hf + 1) * SC, :])
        d["kill"] = np.full((128, 1), 0.0 if hf == 1 else -1e9, np.float32)
        d["pidx"] = np.ascontiguousarray(
            (((c ^ 1) % 2) * KVROWS + np.arange(KVROWS))
            .astype(np.int32).reshape(KVROWS // 128, 128).T)
        pos = np.arange(hf * SC, (hf + 1) * SC, dtype=np.float32)
        f = pos[:, None] * inv[None, :]
        d["cos8"] = np.ascontiguousarray(
            np.tile(np.cos(f) * RSQ, (1, NH)).astype(np.float32))
        d["sin8"] = np.ascontiguousarray(
            np.tile(np.sin(f) * RSQ, (1, NH)).astype(np.float32))
        per_core.append(d)
    return sh, per_core



def _build(debug=()):
    nc = bass.Bass("TRN2", target_bir_lowering=False)
    A = {}

    def di(name, shape, dtype=F32):
        A[name] = nc.dram_tensor(name, list(shape), dtype,
                                 kind="ExternalInput").ap()

    di("x0", (SC, D))
    for n in ("wqT", "wkT", "wvT", "woT", "qw", "kw", "vw", "ow"):
        di(n, (D, D))
    di("bq_r", (128, NCH)); di("bk_r", (128, NCH))
    di("bv_rep", (128, D)); di("bo_rep", (128, D))
    di("w1", (NE, D, HM)); di("w2", (NE, HM, HM)); di("w3", (NE, HM, D))
    di("b1_r", (NE, 128, HM // 128)); di("b2_r", (NE, 128, HM // 128))
    di("b3", (NE, D))
    di("gate_w", (D, NE)); di("gate_b_rep", (128, NE))
    di("wexp_sw", (8, 128, 4096)); di("bexp_r", (128, DH // 128))
    di("upb_r", (128, DH // 128))
    di("wcon_sw", (4, 128, 8192))
    di("bcon_rep", (128, D)); di("dbias_rep", (128, D))
    for pre in ("up", "dn"):
        di(pre + "_wd", (12, 128, 32)); di(pre + "_wo", (12, 128, 32))
        di(pre + "_wox", (12, 128, 32))
    di("eye", (128, 128))
    di("maskxor", (7, 128, 128))
    di("diagmask", (4, 128, 512))
    di("kill", (128, 1))
    di("pidx", (128, KVROWS // 128), I32)
    di("cos8", (SC, NH * 32)); di("sin8", (SC, NH * 32))

    out_x = nc.dram_tensor("out_x", [SC, D], F32, kind="ExternalOutput").ap()
    dbg = {}
    for name in debug:
        dbg[name] = nc.dram_tensor("dbg_" + name, list(DEBUG_SHAPES[name]),
                                   F32, kind="ExternalOutput").ap()

    _kd = os.environ.get("KDBG") == "1"
    _kind = dict(kind="ExternalOutput") if _kd else {}
    x1_dram = nc.dram_tensor("x1_dram", [SC, D], F32, **_kind).ap()
    moe_dram = nc.dram_tensor("moe_dram", [SC, D], F32, **_kind).ap()
    qfm_dram = nc.dram_tensor("qfm_dram", [D, SC], BF16).ap()
    wexp_bf = nc.dram_tensor("wexp_bf", [8, 128, 4096], BF16).ap()
    wcon_bf = nc.dram_tensor("wcon_bf", [4, 128, 8192], BF16).ap()
    kv_in_t = nc.dram_tensor("kv_in", [KVROWS, 1024], BF16)
    kv_ga_t = nc.dram_tensor("kv_gather", [2, KVROWS, 1024], BF16)
    kv_pa_t = nc.dram_tensor("kv_partner", [KVROWS, 1024], BF16)
    kvi = kv_in_t.ap()
    kvp = kv_pa_t.ap()
    kviK = kvi.rearrange("(r two) w -> r (two w)", two=2)[0:KROWS, :]
    kvpK = kvp.rearrange("(r two) w -> r (two w)", two=2)[0:KROWS, :]
    kvgaflat = kv_ga_t.ap().rearrange("s r w -> (s r) w")

    es = contextlib.ExitStack()
    _uid = [0]

    def sb(name, shape, dtype=F32, stack=None):
        _uid[0] += 1
        return (stack or es).enter_context(
            nc.sbuf_tensor("%s_u%d" % (name, _uid[0]), list(shape), dtype))

    with nc.Block() as block:
        p = Plan(nc, block, es)

        G = {}
        G["ones_col"] = sb("ones_col", (1, 64), BF16)
        G["ones64"] = sb("ones64", (64, 1), BF16)
        G["eye_t"] = sb("eye_t", (128, 128), BF16)
        G["stats"] = sb("stats", (128, 112), F32)
        G["scr_bf"] = sb("scr_bf", (128, 1024), BF16)
        G["kill_t"] = sb("kill_t", (128, 1), F32)
        G["gate_all"] = sb("gate_all", (128, NT, NE), F32)
        G["gate_eq"] = sb("gate_eq", (128, NT, NE), F32)
        G["gateb_t"] = sb("gateb_t", (128, NE), F32)
        G["wT_sb"] = sb("wT_sb", (8, NT * 128), BF16)
        G["b3_sb"] = sb("b3_sb", (8, D), BF16)
        G["pidx_t"] = sb("pidx_t", (128, KVROWS // 128), I32)
        G["Mup"] = sb("Mup", (128, 32 * 128), BF16)
        G["Mdn"] = sb("Mdn", (128, 32 * 128), BF16)
        G["kmax"] = sb("kmax", (2, 8), F32)
        G["kmax_bf"] = sb("kmax_bf", (2, 8), BF16)
        G["krow"] = sb("krow", (2, 8), F32)

        G["psA"] = es.enter_context(nc.psum_tensor("psA", [128, 512], F32))
        G["psB"] = es.enter_context(nc.psum_tensor("psB", [128, 512], F32))
        G["psC"] = es.enter_context(nc.psum_tensor("psC", [128, 512], F32))
        G["psD"] = es.enter_context(nc.psum_tensor("psD", [128, 512], F32))
        G["psQ"] = es.enter_context(nc.psum_tensor("psQ", [128, 2048], F32))

        with Sec(p, "sync") as s:
            s.dma(G["kill_t"][:], A["kill"])
            s.dma(G["gateb_t"][:], A["gate_b_rep"])
            s.dma(G["pidx_t"][:], A["pidx"])
        with Sec(p, "gpsimd") as g:
            g.dma(G["b3_sb"][:], A["b3"], "dG")
            g.dma(G["eye_t"][:], A["eye"], "dG")
        with Sec(p, "vector") as v:
            v.o(lambda e: e.memset(G["ones_col"][:], 1.0))
            v.o(lambda e: e.memset(G["ones64"][:], 1.0))
            zq = v.o(lambda e: e.memset(G["scr_bf"][:], 0.0), inc="ve")
        with Sec(p, "sync") as s:
            s.w("ve", zq)
            s.dma(kvi[STATROW:KVROWS, :], G["scr_bf"][0:KVROWS - STATROW, :])
        p.barrier()

        rp_sem = es.enter_context(nc.semaphore("rp"))
        with Sec(p, "gpsimd") as g:
            for i in range(8):
                g.ops.append(lambda e, i2=i: e.dma_start(
                    out=wexp_bf[i2], in_=A["wexp_sw"][i2]).then_inc(
                        rp_sem, 16))
            for i in range(4):
                g.ops.append(lambda e, i2=i: e.dma_start(
                    out=wcon_bf[i2], in_=A["wcon_sw"][i2]).then_inc(
                        rp_sem, 16))

        ctx = dict(A=A, G=G, kvi=kvi, kvp=kvp, kviK=kviK, kvpK=kvpK,
                   kvgaflat=kvgaflat, kv_in_t=kv_in_t, kv_ga_t=kv_ga_t,
                   moe_dram=moe_dram, qfm_dram=qfm_dram, sb=sb, dbg=dbg,
                   wexp_bf=wexp_bf, wcon_bf=wcon_bf, rp_sem=rp_sem)
        for L in range(2):
            _emit_layer(p, ctx, L,
                        x_in=(A["x0"] if L == 0 else x1_dram),
                        x_out=(x1_dram if L == 0 else out_x))

        with Sec(p, "sync") as s:
            s.w("dS", p.v("dS"))
            s.w("dG", p.v("dG"))
            s.w("dW", p.v("dW"))
            s.w("dC", p.v("dC"))

    es.close()
    return nc


def _emit_layer(p, ctx, L, x_in, x_out):
    sb = ctx["sb"]
    les = contextlib.ExitStack()
    xnT = sb("xnT%d" % L, (128, NCH, SC), BF16, les)

    if _ph_on():
        _p1_ln(p, ctx, L, x_in, xnT)
    p.barrier()
    if _ph_on():
        _p2_kv(p, ctx, L, xnT)
    if L == 1 and _ph_on():
        _p2_rope(p, ctx, xnT)
    if L == 0 and _ph_on():
        _p2_gate(p, ctx, xnT)
    if _ph_on():
        _p2_collective(p, ctx)
        ctx["cc_on"] = True
    else:
        ctx["cc_on"] = False
    if L == 0 and _ph_on():
        _mbuild(p, ctx)
    pcb = (lambda: _p3_partner(p, ctx)) if ctx["cc_on"] else None
    if _ph_on():
        if L == 0:
            _p3_moe(p, ctx, xnT, pcb)
        else:
            _p3_ffn(p, ctx, xnT, pcb)
    elif pcb:
        pcb()
    p.barrier()
    pes = contextlib.ExitStack()
    attnout = sb("attnout%d" % L, (128, NCH, SC), BF16, pes)
    if ctx["cc_on"] and _ph_on():
        _p4_attn(p, ctx, L, xnT, attnout, pes)
    p.barrier()
    if _ph_on():
        _p5_residual(p, ctx, L, attnout, x_in, x_out)
    p.barrier()
    pes.close()
    les.close()



def _p1_ln(p, ctx, L, x_in, xnT):
    A, G, sb = ctx["A"], ctx["G"], ctx["sb"]
    st = G["stats"]
    SUM, SQ, MEAN, VAR, SD, RSTD, NM = 0, 16, 32, 48, 64, 80, 96
    ph = contextlib.ExitStack()
    xb = [sb("p1x%d_%d" % (L, i), (128, D), F32, ph) for i in range(3)]
    xnb = [sb("p1n%d_%d" % (L, i), (128, D), BF16, ph) for i in range(2)]

    ve_done = [0] * NT
    sc_done = [0] * NT
    with Sec(p, "sync") as s, Sec(p, "vector") as v, Sec(p, "scalar") as c:
        for t in range(NT):
            if t >= 3:
                s.w("ve", ve_done[t - 3])
                s.w("sc", sc_done[t - 3])
            dt = s.dma(xb[t % 3][:], x_in[t * 128:(t + 1) * 128, :])
            v.w("dS", dt)
            ve_done[t] = v.o(
                lambda e, b=xb[t % 3], tt=t:
                e.reduce_sum(st[:, SUM + tt:SUM + tt + 1], b[:], axis=AX.X),
                inc="ve")
            c.w("dS", dt)
            if t >= 1:
                c.w("sc", sc_done[t - 1])
            sc_done[t] = c.o(
                lambda e, b=xb[t % 3], tt=t:
                e.activation(G["scr_bf"][:], b[:], AF.Square,
                             accum_out=st[:, SQ + tt:SQ + tt + 1]), inc="sc")
    with Sec(p, "vector") as v:
        v.w("ve", p.v("ve"))
        v.w("sc", p.v("sc"))
        v.o(lambda e: e.tensor_scalar(st[:, MEAN:MEAN + 16],
                                      st[:, SUM:SUM + 16], 1.0 / D, None,
                                      ALU.mult))
        v.o(lambda e: e.tensor_scalar(st[:, VAR:VAR + 16],
                                      st[:, SQ:SQ + 16], 1.0 / D, None,
                                      ALU.mult))
        v.drain()
        v.o(lambda e: e.tensor_tensor(st[:, SD:SD + 16], st[:, MEAN:MEAN + 16],
                                      st[:, MEAN:MEAN + 16], ALU.mult))
        v.drain()
        v.o(lambda e: e.tensor_tensor(st[:, VAR:VAR + 16], st[:, VAR:VAR + 16],
                                      st[:, SD:SD + 16], ALU.subtract))
        v.drain()
        veq = v.o(lambda e: e.tensor_scalar(st[:, VAR:VAR + 16],
                                            st[:, VAR:VAR + 16], EPS, None,
                                            ALU.add), inc="ve")
    with Sec(p, "scalar") as c:
        c.w("ve", veq)
        scq = c.o(lambda e: e.activation(st[:, SD:SD + 16],
                                         st[:, VAR:VAR + 16], AF.Sqrt),
                  inc="sc")
    with Sec(p, "vector") as v:
        v.w("sc", scq)
        v.o(lambda e: e.reciprocal(st[:, RSTD:RSTD + 16], st[:, SD:SD + 16]))
        v.drain()
        v.o(lambda e: e.tensor_tensor(st[:, NM:NM + 16], st[:, MEAN:MEAN + 16],
                                      st[:, RSTD:RSTD + 16], ALU.mult))
        v.drain()
        stat_q = v.o(lambda e: e.tensor_scalar(st[:, NM:NM + 16],
                                               st[:, NM:NM + 16], -1.0, None,
                                               ALU.mult), inc="ve")

    psv = [G["psA"].ap().bitcast(BF16), G["psB"].ap().bitcast(BF16)]
    tr_done = [0] * NT
    cp_done = [0] * NT
    sc2_done = [0] * NT
    with Sec(p, "sync") as s, Sec(p, "scalar") as c, Sec(p, "tensor") as t_, \
            Sec(p, "vector") as v:
        for t in range(NT):
            if t >= 3:
                s.w("sc", sc2_done[t - 3])
            else:
                s.w("ve", stat_q)
            dt = s.dma(xb[t % 3][:], x_in[t * 128:(t + 1) * 128, :])
            c.w("dS", dt)
            c.w("ve", stat_q)
            if t >= 2:
                c.w("te", tr_done[t - 2])
            sc2_done[t] = c.o(
                lambda e, b=xb[t % 3], n=xnb[t % 2], tt=t:
                e.activation(n[:], b[:], AF.Identity,
                             scale=st[:, RSTD + tt:RSTD + tt + 1],
                             bias=st[:, NM + tt:NM + tt + 1]), inc="sc")
            t_.w("sc", sc2_done[t])
            if t >= 2:
                t_.w("ve", cp_done[t - 2])
            for cc in range(NCH):
                r = t_.o(
                    lambda e, n=xnb[t % 2], c2=cc, pv=psv[t % 2]:
                    e.transpose(pv[:, c2 * 128:(c2 + 1) * 128],
                                n[:, c2 * 128:(c2 + 1) * 128], G["eye_t"][:]),
                    inc="te" if cc == NCH - 1 else None)
                if r:
                    tr_done[t] = r
            v.w("te", tr_done[t])
            cp_done[t] = v.o(
                lambda e, pv=psv[t % 2], tt=t: e.tensor_copy(
                    xnT[:, :, tt * 128:(tt + 1) * 128],
                    pv[:].rearrange("p (c s) -> p c s", s=128)), inc="ve")
            if "xn" in ctx["dbg"] and L == 0:
                s.w("sc", sc2_done[t])
                s.dma(ctx["dbg"]["xn"][t * 128:(t + 1) * 128, :],
                      xnb[t % 2][:])
    p.barrier()
    ph.close()



def _p2_kv(p, ctx, L, xnT):
    A, G, sb = ctx["A"], ctx["G"], ctx["sb"]
    kviK, kvi = ctx["kviK"], ctx["kvi"]
    ph = contextlib.ExitStack()
    wvb = sb("p2wv%d" % L, (128, NCH, D), BF16, ph)
    vstg = [sb("p2vs%d_%d" % (L, i), (128, D), BF16, ph) for i in range(2)]
    ones_row = sb("p2or%d" % L, (1, SC), BF16, ph)
    G["ones_row"] = ones_row
    if L == 0:
        G["sq_bf"] = sb("p2sq%d" % L, (64, SK), BF16, ph)
    with Sec(p, "vector") as v:
        vq_or = v.o(lambda e: e.memset(ones_row[:], 1.0), inc="ve")

    if L == 0:
        wkb = sb("p2wk", (128, NCH, D), BF16, ph)
        bkt = sb("p2bk", (128, NCH), F32, ph)
        bvt = sb("p2bvr", (128, D), F32, ph)
        kstg = [sb("p2ks%d" % i, (128, SC), BF16, ph) for i in range(2)]
        with Sec(p, "gpsimd") as g, Sec(p, "sync") as s:
            p.fence(g)
            wk_t = g.dma(wkb[:],
                         A["wkT"].rearrange("(c p) d -> p c d", p=128), "dW")
            wv_t = g.dma(wvb[:],
                         A["wvT"].rearrange("(c p) d -> p c d", p=128), "dW")
            s.dma(bkt[:], A["bk_r"])
            bv_t = s.dma(bvt[:], A["bv_rep"])
        with Sec(p, "scalar") as c:
            c.w("dS", bv_t)
            bk8 = c.o(lambda e: e.activation(bkt[:], bkt[:], AF.Identity,
                                             scale=RSQ), inc="sc")
        km_last = 0
        with Sec(p, "tensor") as t_, Sec(p, "scalar") as c, \
                Sec(p, "sync") as s, Sec(p, "vector") as v:
            t_.w("dW", wk_t)
            stg_free = [0, 0]
            evac = {}
            first = True
            for pr in range(NH // 2):
                sl = pr % 2
                if pr >= 2:
                    t_.w("dS", stg_free[sl])
                for s4 in range(4):
                    psl = s4 % 2
                    psx = [G["psA"], G["psB"]][psl]
                    t_.w("sc", evac.get((pr * 4 + s4) - 2, 0))
                    for dc in range(NCH):
                        t_.o(lambda e, ps=psx, d=dc, pp=pr, ss=s4:
                             e.matmul(ps[:],
                                      wkb[:, d, pp * 128:(pp + 1) * 128],
                                      xnT[:, d, ss * 512:(ss + 1) * 512],
                                      start=(d == 0), stop=(d == NCH - 1)),
                             inc="te" if dc == NCH - 1 else None)
                    c.w("te", p.v("te"))
                    c.w("sc", bk8)
                    evac[pr * 4 + s4] = c.o(
                        lambda e, ps=psx, pp=pr, ss=s4, k=kstg[sl]:
                        e.activation(k[:, ss * 512:(ss + 1) * 512], ps[:],
                                     AF.Identity, scale=RSQ,
                                     bias=bkt[:, pp:pp + 1]), inc="sc")
                s.w("sc", evac[pr * 4 + 3])
                s.dma(kviK[65 * 2 * pr:65 * 2 * pr + 64, :],
                      kstg[sl][0:64, :])
                stg_free[sl] = s.dma(
                    kviK[65 * (2 * pr + 1):65 * (2 * pr + 1) + 64, :],
                    kstg[sl][64:128, :])
                km_last = _k2_pair(p, ctx, t_, c, v, kstg[sl],
                                   evac[pr * 4 + 3], first)
                first = False
        _k2_finish(p, ctx, km_last)
    else:
        with Sec(p, "gpsimd") as g:
            p.fence(g)
            wv_t = g.dma(wvb[:],
                         A["vw"].rearrange("(c p) d -> p c d", p=128), "dW")

    with Sec(p, "sync") as s:
        s.w("ve", vq_or)
        for h in range(NH):
            s.dma(kviK[65 * h + 64:65 * h + 65, :], G["ones_row"][:])

    with Sec(p, "tensor") as t_, Sec(p, "vector") as v, \
            Sec(p, "scalar") as c, Sec(p, "sync") as s:
        t_.w("dW", wv_t)
        if L == 0:
            t_.w("dS", bv_t)
        vfree = [0, 0]
        vev = [0, 0]
        for t in range(NT):
            sl = t % 2
            psb = G["psQ"]
            if t >= 2:
                t_.w("dS", vfree[sl])
                t_.w("ve" if L == 0 else "sc", vev[sl])
            for dc in range(NCH):
                for n in range(2):
                    t_.o(lambda e, d=dc, tt=t, nn=n, hh=sl:
                         e.matmul(psb[:, hh * 1024 + nn * 512:
                                       hh * 1024 + (nn + 1) * 512],
                                  xnT[:, d, tt * 128:(tt + 1) * 128],
                                  wvb[:, d, nn * 512:(nn + 1) * 512],
                                  start=(d == 0), stop=(d == NCH - 1)),
                         inc="te" if (dc == NCH - 1 and n == 1) else None)
            if L == 0:
                v.w("te", p.v("te"))
                vev[sl] = v.o(lambda e, o=vstg[sl], hh=sl: e.tensor_tensor(
                    o[:], psb[:, hh * 1024:hh * 1024 + 1024], bvt[:],
                    ALU.add), inc="ve")
                s.w("ve", vev[sl])
            else:
                c.w("te", p.v("te"))
                vev[sl] = c.o(lambda e, o=vstg[sl], hh=sl: e.activation(
                    o[:], psb[:, hh * 1024:hh * 1024 + 1024], AF.Identity),
                    inc="sc")
                s.w("sc", vev[sl])
            vfree[sl] = s.dma(
                kvi[VBASE + t * 128:VBASE + (t + 1) * 128, :], vstg[sl][:])
    ph.close()


def _k2_pair(p, ctx, t_, c, v, ktile, ready, first):
    G = ctx["G"]
    sq = G["sq_bf"]
    c.w("sc", ready)
    c.w("te", p.v("te"))
    c.o(lambda e: e.activation(sq[0:64, 0:SC], ktile[0:64, :], AF.Square),
        inc="sc")
    sq2 = c.o(lambda e: e.activation(sq[0:64, SC:2 * SC], ktile[64:128, :],
                                     AF.Square), inc="sc")
    last = 0
    for i in range(8):
        t_.w("sc", sq2)
        t_.w("ve", last or p.v("ve"))
        tq = t_.o(lambda e, ii=i: e.matmul(
            G["psD"][0:1, :], G["ones64"][:],
            sq[0:64, ii * 512:(ii + 1) * 512], start=True, stop=True),
            inc="te")
        v.w("te", tq)
        v.o(lambda e: e.reduce_max(G["krow"][0:1, 0:1], G["psD"][0:1, :],
                                   axis=AX.X))
        v.drain()
        if first and i == 0:
            v.o(lambda e: e.tensor_copy(G["kmax"][0:1, 0:1],
                                        G["krow"][0:1, 0:1]))
        else:
            v.o(lambda e: e.tensor_tensor(G["kmax"][0:1, 0:1],
                                          G["kmax"][0:1, 0:1],
                                          G["krow"][0:1, 0:1], ALU.max))
        v.drain()
        last = v.o(lambda e: e.drain(), inc="ve")
    return last


def _k2_finish(p, ctx, last):
    G, kvi = ctx["G"], ctx["kvi"]
    with Sec(p, "vector") as v, Sec(p, "sync") as s:
        v.w("ve", last)
        vq = v.o(lambda e: e.tensor_scalar(G["kmax_bf"][0:1, 0:1],
                                           G["kmax"][0:1, 0:1], 1.02, None,
                                           ALU.mult), inc="ve")
        s.w("ve", vq)
        s.dma(kvi[STATROW:STATROW + 1, 0:1], G["kmax_bf"][0:1, 0:1])



def _p2_rope(p, ctx, xnT):
    A, G, sb = ctx["A"], ctx["G"], ctx["sb"]
    kviK, kvi, qfm_dram = ctx["kviK"], ctx["kvi"], ctx["qfm_dram"]
    ph = contextlib.ExitStack()
    G["sq_bf"] = sb("r_sq", (64, SK), BF16, ph)
    qwb = sb("r_qw", (128, NCH, D), BF16, ph)
    kwb = sb("r_kw", (128, NCH, D), BF16, ph)
    fmb = sb("r_fm", (128, NCH, SC), BF16, ph)
    tb = [sb("r_tb%d" % i, (128, 1024), F32, ph) for i in range(2)]
    tmp = [sb("r_tmp%d" % i, (128, 512), F32, ph) for i in range(4)]
    rtm = [sb("r_tm%d" % i, (128, D), BF16, ph) for i in range(2)]

    with Sec(p, "gpsimd") as g:
        p.fence(g)
        qw_t = g.dma(qwb[:],
                     A["qw"].rearrange("(c p) d -> p c d", p=128), "dW")
        kw_t = g.dma(kwb[:],
                     A["kw"].rearrange("(c p) d -> p c d", p=128), "dW")

    qstg_done = 0
    for which in range(2):
        wb = qwb if which == 0 else kwb
        wt = qw_t if which == 0 else kw_t
        rope_q = [0, 0]
        cp_q = [0, 0]
        for t in range(NT):
            sl = t % 2
            with Sec(p, "sync") as s:
                if t >= 2:
                    s.w("ve", rope_q[sl])
                s.dma(tb[sl][:, 0:512], A["cos8"][t * 128:(t + 1) * 128, :])
                tbl_t = s.dma(tb[sl][:, 512:1024],
                              A["sin8"][t * 128:(t + 1) * 128, :])
            with Sec(p, "tensor") as t_:
                t_.w("dW", wt)
                if t + which > 0:
                    t_.w("ve", p.v("ve"))
                for dc in range(NCH):
                    for n in range(2):
                        t_.o(lambda e, d=dc, tt=t, nn=n, w2=wb:
                             e.matmul(G["psQ"][:, nn * 512:(nn + 1) * 512],
                                      xnT[:, d, tt * 128:(tt + 1) * 128],
                                      w2[:, d, nn * 512:(nn + 1) * 512],
                                      start=(d == 0), stop=(d == NCH - 1)),
                             inc="te" if (dc == NCH - 1 and n == 1) else None)
            with Sec(p, "vector") as v:
                v.w("te", p.v("te"))
                v.w("dS", tbl_t)
                if t >= 2:
                    v.w("te", cp_q[sl])
                pv = G["psQ"].ap()[:, 0:1024].rearrange(
                    "p (q two) -> p q two", two=2)
                ro = rtm[sl][:].rearrange("p (q two) -> p q two", two=2)
                ct, st_ = tb[sl][:, 0:512], tb[sl][:, 512:1024]
                v.o(lambda e, px=pv, cc=ct: e.tensor_tensor(
                    tmp[0][:], px[:, :, 0], cc, ALU.mult))
                v.o(lambda e, px=pv, cc=st_: e.tensor_tensor(
                    tmp[1][:], px[:, :, 1], cc, ALU.mult))
                v.o(lambda e, px=pv, cc=st_: e.tensor_tensor(
                    tmp[2][:], px[:, :, 0], cc, ALU.mult))
                v.o(lambda e, px=pv, cc=ct: e.tensor_tensor(
                    tmp[3][:], px[:, :, 1], cc, ALU.mult))
                v.drain()
                v.o(lambda e, o=ro: e.tensor_tensor(
                    o[:, :, 0], tmp[0][:], tmp[1][:], ALU.subtract))
                rope_q[sl] = v.o(lambda e, o=ro: e.tensor_tensor(
                    o[:, :, 1], tmp[2][:], tmp[3][:], ALU.add), inc="ve")
            with Sec(p, "tensor") as t_, Sec(p, "vector") as v:
                t_.w("ve", rope_q[sl])
                if which == 1 and t == 0:
                    v.w("dS", qstg_done)
                pbf = G["psC"].ap().bitcast(BF16)
                for c4 in range(2):
                    tq = 0
                    for cc in range(c4 * 4, c4 * 4 + 4):
                        r = t_.o(lambda e, c2=cc, s2=sl: e.transpose(
                            pbf[:, (c2 % 4) * 128:(c2 % 4 + 1) * 128],
                            rtm[s2][:, c2 * 128:(c2 + 1) * 128],
                            G["eye_t"][:]),
                            inc="te" if cc % 4 == 3 else None)
                        if r:
                            tq = r
                    v.w("te", tq)
                    cp_q[sl] = v.o(lambda e, b2=c4 * 4, tt=t: e.tensor_copy(
                        fmb[:, b2:b2 + 4, tt * 128:(tt + 1) * 128],
                        pbf[:, 0:512].rearrange("p (c s) -> p c s", s=128)),
                        inc="ve")
                    t_.w("ve", cp_q[sl])
        if which == 0:
            with Sec(p, "sync") as s:
                s.w("ve", p.v("ve"))
                for cc in range(NCH):
                    qstg_done = s.dma(qfm_dram[cc * 128:(cc + 1) * 128, :],
                                      fmb[:, cc, :])
        else:
            km_last = 0
            with Sec(p, "sync") as s, Sec(p, "tensor") as t_, \
                    Sec(p, "scalar") as c, Sec(p, "vector") as v:
                s.w("ve", p.v("ve"))
                vq = p.v("ve")
                for pr in range(NH // 2):
                    s.dma(kviK[65 * 2 * pr:65 * 2 * pr + 64, :],
                          fmb[0:64, pr, :])
                    s.dma(kviK[65 * (2 * pr + 1):65 * (2 * pr + 1) + 64, :],
                          fmb[64:128, pr, :])
                first = True
                for pr in range(NH // 2):
                    km_last = _k2_pair(p, ctx, t_, c, v, fmb[:, pr, :], vq,
                                       first)
                    first = False
            _k2_finish(p, ctx, km_last)
    ph.close()



def _p2_gate(p, ctx, xnT):
    A, G, sb = ctx["A"], ctx["G"], ctx["sb"]
    ph = contextlib.ExitStack()
    gwb = sb("p2gw", (128, NCH, 8), BF16, ph)
    ga, geq, m1 = G["gate_all"], G["gate_eq"], G["stats"]
    with Sec(p, "gpsimd") as g:
        p.fence(g)
        gw_t = g.dma(gwb[:],
                     A["gate_w"].rearrange("(c p) e -> p c e", p=128), "dW")
    with Sec(p, "tensor") as t_, Sec(p, "vector") as v:
        t_.w("dW", gw_t)
        for t in range(NT):
            if t >= 1:
                t_.w("ve", p.v("ve"))
            for dc in range(NCH):
                t_.o(lambda e, d=dc, tt=t: e.matmul(
                    G["psC"][:, 0:8], xnT[:, d, tt * 128:(tt + 1) * 128],
                    gwb[:, d, :], start=(d == 0), stop=(d == NCH - 1)),
                    inc="te" if dc == NCH - 1 else None)
            v.w("te", p.v("te"))
            v.o(lambda e, tt=t: e.tensor_tensor(
                ga[:, tt, :], G["psC"][:, 0:8], G["gateb_t"][:], ALU.add),
                inc="ve")
    with Sec(p, "vector") as v:
        v.w("ve", p.v("ve"))
        vq = v.o(lambda e: e.tensor_reduce(m1[:, 0:16], ga[:], AX.X, ALU.max,
                                           negate=True), inc="ve")
    with Sec(p, "scalar") as c:
        c.w("ve", vq)
        for t in range(NT):
            c.o(lambda e, tt=t: e.activation(ga[:, tt, :], ga[:, tt, :],
                                             AF.Exp, bias=m1[:, tt:tt + 1]),
                inc="sc" if t == NT - 1 else None)
    b16 = lambda ap: ap.unsqueeze(2).broadcast_to((128, 16, 8))
    with Sec(p, "vector") as v:
        v.w("sc", p.v("sc"))
        v.o(lambda e: e.tensor_reduce(m1[:, 16:32], ga[:], AX.X, ALU.add))
        v.drain()
        v.o(lambda e: e.reciprocal(m1[:, 32:48], m1[:, 16:32]))
        v.drain()
        v.o(lambda e: e.tensor_tensor(ga[:], ga[:], b16(m1[:, 32:48]),
                                      ALU.mult))
        v.drain()
        v.o(lambda e: e.tensor_reduce(m1[:, 0:16], ga[:], AX.X, ALU.max))
        v.drain()
        v.o(lambda e: e.tensor_tensor(geq[:], ga[:], b16(m1[:, 0:16]),
                                      ALU.is_equal))
        v.drain()
        v.o(lambda e: e.scalar_tensor_tensor(geq[:], geq[:], -1e30, ga[:],
                                             ALU.mult, ALU.add))
        v.drain()
        v.o(lambda e: e.tensor_reduce(m1[:, 16:32], geq[:], AX.X, ALU.max))
        v.drain()
        v.o(lambda e: e.tensor_tensor(geq[:], ga[:], b16(m1[:, 16:32]),
                                      ALU.is_ge))
        v.drain()
        v.o(lambda e: e.tensor_tensor(ga[:], ga[:], geq[:], ALU.mult))
        v.drain()
        vq = v.o(lambda e: e.tensor_copy(
            G["scr_bf"][:, 0:128], ga[:].rearrange("p t e -> p (t e)")),
            inc="ve")
    with Sec(p, "tensor") as t_, Sec(p, "vector") as v:
        t_.w("ve", vq)
        pbf = [G["psC"].ap().bitcast(BF16), G["psD"].ap().bitcast(BF16)]
        cpq = [0, 0]
        for t in range(NT):
            sl = t % 2
            if t >= 2:
                t_.w("ve", cpq[sl])
            tq = t_.o(lambda e, tt=t, pb=pbf[sl]: e.transpose(
                pb[0:8, 0:128], G["scr_bf"][:, tt * 8:tt * 8 + 8],
                G["eye_t"][:]), inc="te")
            v.w("te", tq)
            cpq[sl] = v.o(lambda e, tt=t, pb=pbf[sl]: e.tensor_copy(
                G["wT_sb"][0:8, tt * 128:(tt + 1) * 128],
                pb[0:8, 0:128]),
                inc="ve")
    with Sec(p, "sync") as s:
        if "gate" in ctx["dbg"]:
            s.w("ve", p.v("ve"))
            s.dma(ctx["dbg"]["gate"], ga[:].rearrange("p t e -> p (t e)"))
    ph.close()



def _mbuild(p, ctx):
    A, G, sb = ctx["A"], ctx["G"], ctx["sb"]
    ph = contextlib.ExitStack()
    wd = sb("mb_wd", (128, 12, 32), BF16, ph)
    wo = sb("mb_wo", (128, 12, 32), BF16, ph)
    wox = sb("mb_wox", (128, 12, 32), BF16, ph)
    mxr = sb("mb_mx", (128, 7, 128), BF16, ph)
    Sd = sb("mb_S", (128, 32, 128), BF16, ph)
    t2 = sb("mb_t2", (128, 32, 128), BF16, ph)
    Mc = [sb("mb_M%d" % i, (128, 32, 128), BF16, ph) for i in range(2)]

    with Sec(p, "gpsimd") as g0:
        p.fence(g0)
        mx_t = g0.dma(mxr[:], A["maskxor"].rearrange("s p n -> p s n"), "dG")

    b32 = lambda ap: ap.unsqueeze(1).broadcast_to((128, 32, 128))
    bj = lambda ap: ap.unsqueeze(2).broadcast_to((128, 32, 128))
    flat = lambda ap: ap.rearrange("p c n -> p (c n)")

    for pre, Mout in (("up", G["Mup"]), ("dn", G["Mdn"])):
        with Sec(p, "gpsimd") as g:
            g.w("ve", p.v("ve"))
            g.dma(wd[:], A[pre + "_wd"].rearrange("s p c -> p s c"), "dG")
            g.dma(wo[:], A[pre + "_wo"].rearrange("s p c -> p s c"), "dG")
            w_t = g.dma(wox[:], A[pre + "_wox"].rearrange("s p c -> p s c"),
                        "dG")

        def s_dense(v, stage, transposed):
            v.o(lambda e, st=stage: e.tensor_tensor(
                Sd[:], b32(G["eye_t"][:]), bj(wd[:, st, :]), ALU.mult))
            src = wox if transposed else wo
            v.o(lambda e, st=stage, s2=src: e.tensor_tensor(
                t2[:], b32(mxr[:, st, :]), bj(s2[:, st, :]), ALU.mult))
            v.drain()
            return v.o(lambda e: e.tensor_tensor(flat(Sd[:]), flat(Sd[:]),
                                                 flat(t2[:]), ALU.add),
                       inc="ve")

        with Sec(p, "vector") as v:
            v.w("dG", w_t)
            v.w("dG", mx_t)
            s_dense(v, 0, False)
            v.drain()
            v.o(lambda e: e.tensor_copy(flat(Mc[0][:]), flat(Sd[:])),
                inc="ve")
        cur = 0
        for stage in range(1, 7):
            with Sec(p, "vector") as v:
                v.w("ve", p.v("ve"))
                v.w("te", p.v("te"))
                sdq = s_dense(v, stage, True)
            with Sec(p, "tensor") as t_, Sec(p, "vector") as v:
                t_.w("ve", sdq)
                for c4 in range(8):
                    if c4 >= 2:
                        t_.w("ve", p.v("ve"))
                    tq = 0
                    for cc in range(c4 * 4, c4 * 4 + 4):
                        psx = [G["psC"], G["psD"]][c4 % 2]
                        r = t_.o(lambda e, c2=cc, cu=cur, ps=psx: e.matmul(
                            ps[:, (c2 % 4) * 128:(c2 % 4 + 1) * 128],
                            Sd[:, c2, :], Mc[cu][:, c2, :],
                            start=True, stop=True),
                            inc="te" if cc % 4 == 3 else None)
                        if r:
                            tq = r
                    v.w("te", tq)
                    v.o(lambda e, b2=c4 * 4, cu=cur, ps=[G["psC"],
                                                         G["psD"]][c4 % 2]:
                        e.tensor_copy(
                            flat(Mc[1 - cu][:, b2:b2 + 4, :]), ps[:]),
                        inc="ve")
            cur = 1 - cur
        with Sec(p, "tensor") as t_, Sec(p, "vector") as v:
            t_.w("ve", p.v("ve"))
            pbf = [G["psC"].ap().bitcast(BF16), G["psD"].ap().bitcast(BF16)]
            for c4 in range(8):
                if c4 >= 2:
                    t_.w("ve", p.v("ve"))
                tq = 0
                for cc in range(c4 * 4, c4 * 4 + 4):
                    r = t_.o(lambda e, c2=cc, cu=cur, pb=pbf[c4 % 2]:
                             e.transpose(
                                 pb[:, (c2 % 4) * 128:(c2 % 4 + 1) * 128],
                                 Mc[cu][:, c2, :], G["eye_t"][:]),
                             inc="te" if cc % 4 == 3 else None)
                    if r:
                        tq = r
                v.w("te", tq)
                v.o(lambda e, b2=c4 * 4, pb=pbf[c4 % 2]: e.tensor_copy(
                    Mout[:, b2 * 128:(b2 + 4) * 128], pb[:, 0:512]),
                    inc="ve")
    ph.close()


def _p2_collective(p, ctx):
    with Sec(p, "gpsimd") as g:
        g.w("dS", p.v("dS"))
        g.w("dG", p.v("dG"))
        g.o(lambda e: e.collective_compute(
            "AllGather", ALU.bypass,
            replica_groups=[[0, 1], [2, 3], [4, 5], [6, 7]],
            ins=[ctx["kv_in_t"].ap().opt()],
            outs=[ctx["kv_ga_t"].ap().opt()]), inc="cc")


def _p3_partner(p, ctx):
    G, sb = ctx["G"], ctx["sb"]
    kvga, kvp = ctx["kvgaflat"], ctx["kvp"]
    ph = contextlib.ExitStack()
    NB = 4
    SKEW = 3
    buf = [sb("pr_buf%d" % i, (128, 1024), BF16, ph) for i in range(NB)]
    NR = KVROWS // 128
    with Sec(p, "gpsimd") as g:
        g.w("cc", p.v("cc"))
        gq = {}
        done = {}
        for r in range(min(SKEW, NR)):
            gq[r] = g.o(lambda e, rr=r, b=buf[r % NB]: e.indirect_dma_start(
                out=b[:], out_offset=None, in_=kvga,
                in_offset=IndirectOffsetOnAxis(
                    ap=G["pidx_t"][:, rr:rr + 1], axis=0)), inc="dG", n=16)
        for r in range(NR):
            g.w("dG", gq[r])
            done[r] = g.dma(kvp[r * 128:(r + 1) * 128, :], buf[r % NB][:],
                            "dG")
            nr = r + SKEW
            if nr < NR:
                if nr >= NB:
                    g.w("dG", done[nr - NB])
                gq[nr] = g.o(lambda e, rr=nr, b=buf[nr % NB]:
                             e.indirect_dma_start(
                                 out=b[:], out_offset=None, in_=kvga,
                                 in_offset=IndirectOffsetOnAxis(
                                     ap=G["pidx_t"][:, rr:rr + 1], axis=0)),
                             inc="dG", n=16)
    ph.close()



class _Stream:

    def __init__(self, p, bufs, sched, sem="dW"):
        self.p = p
        self.bufs = bufs
        self.sched = sched
        self.sem = sem
        self.load_t = {}
        self.use_t = {}
        self.next_load = 0
        self.next_use = 0

    def _load(self, g):
        q = self.next_load
        if q >= len(self.sched):
            return
        self.next_load += 1
        if q >= 2:
            g.w("te", self.use_t[q - 2])
        self.load_t[q] = g.dma(self.bufs[q % 2][:], self.sched[q], self.sem)

    def prime(self, g):
        self._load(g)
        self._load(g)

    def acquire(self, t_):
        q = self.next_use
        t_.w(self.sem, self.load_t[q])
        return q, self.bufs[q % 2]

    def release(self, g):
        q = self.next_use
        self.next_use += 1
        self.use_t[q] = self.p.v("te")
        self._load(g)


def _p3_moe(p, ctx, xnT, partner_cb=None):
    A, G, sb = ctx["A"], ctx["G"], ctx["sb"]
    moe_dram = ctx["moe_dram"]
    ph = contextlib.ExitStack()
    w1b = [sb("m_w1%d" % i, (128, NCH, 512), BF16, ph) for i in range(2)]
    w2b = [sb("m_w2%d" % i, (128, 16, 256), BF16, ph) for i in range(2)]
    w3b = [sb("m_w3%d" % i, (128, 16, 512), BF16, ph) for i in range(2)]
    b1t = [sb("m_b1%d" % i, (128, 16), F32, ph) for i in range(2)]
    b2t = [sb("m_b2%d" % i, (128, 16), F32, ph) for i in range(2)]
    h1g = sb("m_h1", (128, 16, 1024), BF16, ph)
    h2g = sb("m_h2", (128, 16, 1024), BF16, ph)
    w3s = [sb("m_w3s%d" % i, (128, 1024), F32, ph) for i in range(2)]
    ga = G["gate_all"]
    psQ = G["psQ"]

    rr = lambda ap: ap.rearrange("(c p) h -> p c h", p=128)
    s1 = _Stream(p, w1b, [rr(A["w1"][e][:, i * 512:(i + 1) * 512])
                          for e in range(NE) for th in range(2)
                          for i in range(4)])
    s2 = _Stream(p, w2b, [rr(A["w2"][e][:, i * 256:(i + 1) * 256])
                          for e in range(NE) for th in range(2)
                          for i in range(8)])
    s3 = _Stream(p, w3b, [rr(A["w3"][e][:, i * 512:(i + 1) * 512])
                          for e in range(NE) for i in range(2)])
    b_t, b_use = {}, {}

    def load_b(g, e):
        g.w("sc", b_use.get(e - 2, 0))
        g.dma(b1t[e % 2][:], A["b1_r"][e], "dG")
        b_t[e] = g.dma(b2t[e % 2][:], A["b2_r"][e], "dG")

    with Sec(p, "gpsimd") as g:
        p.fence(g)
        load_b(g, 0)
        s1.prime(g)
        s2.prime(g)
        s3.prime(g)

    w3acc = [0, 0]
    h1ev = [0, 0]
    h2ev = [0, 0]
    w3ev = [0, 0]
    for e in range(NE):
        with Sec(p, "tensor") as t_, Sec(p, "scalar") as c, \
                Sec(p, "gpsimd") as g:
            if e + 1 < NE:
                load_b(g, e + 1)
            q3, _ = s3.acquire(t_)
            t_.w("dW", s3.load_t[q3 + 1])
            for th in range(2):
                for hw in range(4):
                    _, w1cur = s1.acquire(t_)
                    for hcs in range(4):
                        hc = hw * 4 + hcs
                        half = hc % 2
                        t_.w("sc", h1ev[half])
                        for dc in range(NCH):
                            for n in range(2):
                                t_.o(lambda e_, d=dc, nn=n, hh=half,
                                     wb=w1cur, h4=hcs, t2=th:
                                     e_.matmul(
                                         psQ[:, hh * 1024 + nn * 512:
                                             hh * 1024 + (nn + 1) * 512],
                                         wb[:, d, h4 * 128:(h4 + 1) * 128],
                                         xnT[:, d,
                                             t2 * 1024 + nn * 512:
                                             t2 * 1024 + (nn + 1) * 512],
                                         start=(d == 0),
                                         stop=(d == NCH - 1)),
                                     inc="te" if (dc == NCH - 1 and n == 1)
                                     else None)
                        c.w("te", p.v("te"))
                        c.w("dG", b_t[e])
                        h1ev[half] = c.o(
                            lambda e_, hh=half, hc2=hc, sl=e % 2:
                            e_.activation(
                                h1g[:, hc2, :],
                                psQ[:, hh * 1024:hh * 1024 + 1024],
                                AF.Gelu, bias=b1t[sl][:, hc2:hc2 + 1]),
                            inc="sc")
                    s1.release(g)
                b_use[e] = p.v("sc")
                for kw_ in range(8):
                    _, w2cur = s2.acquire(t_)
                    for k2 in range(2):
                        kc = kw_ * 2 + k2
                        half = kc % 2
                        t_.w("sc", h2ev[half])
                        if kc == 0:
                            t_.w("sc", h1ev[0])
                            t_.w("sc", h1ev[1])
                        for hc in range(16):
                            for n in range(2):
                                t_.o(lambda e_, h2=hc, nn=n, hh=half,
                                     wb=w2cur, kk=k2:
                                     e_.matmul(
                                         psQ[:, hh * 1024 + nn * 512:
                                             hh * 1024 + (nn + 1) * 512],
                                         wb[:, h2, kk * 128:(kk + 1) * 128],
                                         h1g[:, h2,
                                             nn * 512:(nn + 1) * 512],
                                         start=(h2 == 0), stop=(h2 == 15)),
                                     inc="te" if (hc == 15 and n == 1)
                                     else None)
                        c.w("te", p.v("te"))
                        h2ev[half] = c.o(
                            lambda e_, hh=half, k3=kc, sl=e % 2:
                            e_.activation(
                                h2g[:, k3, :],
                                psQ[:, hh * 1024:hh * 1024 + 1024],
                                AF.Gelu, bias=b2t[sl][:, k3:k3 + 1]),
                            inc="sc")
                    s2.release(g)
                for tt in range(8):
                    tg = th * 8 + tt
                    sl = tt % 2
                    if tt == 0:
                        t_.w("sc", h2ev[0])
                        t_.w("sc", h2ev[1])
                    for fh in range(2):
                        psx = [G["psA"], G["psB"]][fh]
                        wb3 = w3b[(s3.next_use + fh) % 2]
                        t_.w("sc", w3ev[fh])
                        for kc in range(16):
                            t_.o(lambda e_, k3=kc, ps=psx, wb=wb3, t2=tt:
                                 e_.matmul(ps[:],
                                           h2g[:, k3,
                                               t2 * 128:(t2 + 1) * 128],
                                           wb[:, k3, :],
                                           start=(k3 == 0), stop=(k3 == 15)),
                                 inc="te" if kc == 15 else None)
                        c.w("te", p.v("te"))
                        if w3acc[sl]:
                            c.w("dG", w3acc[sl])
                        w3ev[fh] = c.o(
                            lambda e_, f2=fh, s2=sl, tg2=tg, ee=e, ps=psx:
                            e_.activation(
                                w3s[s2][:, f2 * 512:(f2 + 1) * 512], ps[:],
                                AF.Identity, scale=ga[:, tg2, ee:ee + 1]),
                            inc="sc")
                    g.w("sc", w3ev[1])
                    w3acc[sl] = g.o(
                        lambda e_, s2=sl, tg2=tg, ee=e: e_.dma_start(
                            out=moe_dram[tg2 * 128:(tg2 + 1) * 128, :],
                            in_=w3s[s2][:],
                            accum_op=(ALU.bypass if ee == 0 else ALU.add)),
                        inc="dG", n=16)
            s3.release(g)
            s3.release(g)
        if e == 1 and partner_cb:
            partner_cb()
    ph.close()



def _p3_ffn(p, ctx, xnT, partner_cb=None):
    A, G, sb = ctx["A"], ctx["G"], ctx["sb"]
    moe_dram = ctx["moe_dram"]
    wexp_bf, wcon_bf = ctx["wexp_bf"], ctx["wcon_bf"]
    rp_sem = ctx["rp_sem"]
    ph = contextlib.ExitStack()
    TS = 256
    NS = SC // TS
    sl4 = [sb("f_sl%d" % i, (128, 32, TS), BF16, ph) for i in range(4)]
    wexpb = [sb("f_we%d" % i, (128, NCH, 512), BF16, ph) for i in range(2)]
    wconb = [sb("f_wc%d" % i, (128, 32, 256), BF16, ph) for i in range(2)]
    tmpb = [sb("f_tmp%d" % i, (128, TS), F32, ph) for i in range(4)]
    cstage = [sb("f_cs%d" % i, (128, D), F32, ph) for i in range(4)]
    bwd = sb("f_bwd", (128, 12, 32), F32, ph)
    bwo = sb("f_bwo", (128, 12, 32), F32, ph)
    dwd = sb("f_dwd", (128, 12, 32), F32, ph)
    dwo = sb("f_dwo", (128, 12, 32), F32, ph)
    bexp = sb("f_bexp", (128, 32), F32, ph)
    upb = sb("f_upb", (128, 32), F32, ph)

    with Sec(p, "sync") as s:
        p.fence(s)
        s.dma(bexp[:], A["bexp_r"])
        cb_t = s.dma(upb[:], A["upb_r"])
    with Sec(p, "sync") as s:
        s.dma(bwd[:], A["up_wd"].rearrange("s p c -> p s c"))
        s.dma(bwo[:], A["up_wo"].rearrange("s p c -> p s c"))
        s.dma(dwd[:], A["dn_wd"].rearrange("s p c -> p s c"))
        bw_t = s.dma(dwo[:], A["dn_wo"].rearrange("s p c -> p s c"))

    banks = [G["psA"], G["psB"], G["psC"], G["psD"]]
    we_t, wc_t = {}, {}
    we_use, wc_use = {}, {}
    weq = {}
    muq = {}
    mdq = {}
    wcq = {}
    mupd = {}
    gelq = {}
    stqfin = {}
    csfree = [0, 0, 0, 0]
    stq4 = [0, 0, 0, 0]

    def load_wexp(g, key):
        if key >= NS * 8:
            return
        g.w("te", we_use.get(key - 2, 0))
        we_t[key] = g.dma(
            wexpb[key % 2][:],
            wexp_bf[key % 8].rearrange("p (c h) -> p c h", c=NCH), "dW")

    def load_wcon(g, key):
        if key >= NS * 4:
            return
        g.w("te", wc_use.get(key - 2, 0))
        wc_t[key] = g.dma(
            wconb[key % 2][:],
            wcon_bf[key % 4].rearrange("p (c f) -> p c f", c=32), "dW")

    with Sec(p, "tensor") as t_, Sec(p, "scalar") as c, \
            Sec(p, "gpsimd") as g:
        g.ops.append(lambda e: e.wait_ge(rp_sem, 192))
        load_wexp(g, 0)
        load_wexp(g, 1)
        load_wcon(g, 0)
        load_wcon(g, 1)
        c.w("dS", cb_t)
        c.w("dS", bw_t)
        for hc in range(32):
            wi = hc // 4
            key = wi
            bk = hc % 4
            t_.w("dW", we_t[key])
            t_.w("sc", weq.get((0, hc - 4), 0))
            for dc in range(NCH):
                t_.o(lambda e, d=dc, h2=hc % 4, k2=key % 2, b2=bk:
                     e.matmul(banks[b2][:, 0:TS],
                              wexpb[k2][:, d, h2 * 128:(h2 + 1) * 128],
                              xnT[:, d, 0:TS],
                              start=(d == 0), stop=(d == NCH - 1)),
                     inc="te" if dc == NCH - 1 else None)
            we_use[key] = p.v("te")
            if hc % 4 == 3:
                load_wexp(g, key + 2)
            c.w("te", p.v("te"))
            weq[(0, hc)] = c.o(lambda e, h2=hc, b2=bk: e.activation(
                sl4[0][:, h2, :], banks[b2][:, 0:TS], AF.Identity,
                bias=bexp[:, h2:h2 + 1]), inc="sc")
        for cc in range(32):
            bk = cc % 4
            t_.w("sc", weq[(0, min(28 + bk, 31))])
            t_.w("sc", muq.get((0, cc - 4), 0))
            t_.o(lambda e, c2=cc, b2=bk: e.matmul(
                banks[b2][:, 0:TS], G["Mup"][:, c2 * 128:(c2 + 1) * 128],
                sl4[0][:, c2, :], start=True, stop=True), inc="te")
            c.w("te", p.v("te"))
            muq[(0, cc)] = c.o(lambda e, c2=cc, b2=bk: e.activation(
                sl4[1][:, c2, :], banks[b2][:, 0:TS], AF.Identity),
                inc="sc")
        mupd[0] = p.v("sc")

    for w in range(NS):
        P_ = sl4[2 * (w % 2)]
        Q_ = sl4[2 * (w % 2) + 1]
        Pn = sl4[2 * ((w + 1) % 2)]
        Qn = sl4[2 * ((w + 1) % 2) + 1]
        has_wc = w >= 1
        has_we = w + 1 < NS
        n_wc = 8 if has_wc else 0
        n_we = 32 if has_we else 0
        te0 = p.v("te")
        te_wc = [te0 + g2 + 1 for g2 in range(n_wc)]
        te_md = [te0 + n_wc + c2 + 1 for c2 in range(32)]
        te_we = [te0 + n_wc + 32 + h2 + 1 for h2 in range(n_we)]
        te_mu = [te0 + n_wc + 32 + n_we + c2 + 1 for c2 in range(n_we)]

        if has_we:
            for wi in range(8):
                we_use[(w + 1) * 8 + wi] = te_we[4 * wi + 3]
        if has_wc:
            for fq in range(4):
                wc_use[(w - 1) * 4 + fq] = te_wc[fq * 2 + 1]

        with Sec(p, "tensor") as t_, Sec(p, "scalar") as c, \
                Sec(p, "vector") as v, Sec(p, "sync") as s, \
                Sec(p, "gpsimd") as g:
            if w >= 1:
                for j in range(4):
                    load_wcon(g, (w - 1) * 4 + 2 + j)
            if has_we:
                for wi in range(8):
                    load_wexp(g, (w + 1) * 8 + wi + 2)
            wc_ev = []
            if has_wc:
                for g2 in range(8):
                    fq, tsub = g2 // 2, g2 % 2
                    tile = ((w - 1) % 2) * 2 + tsub
                    bk = 2 + (g2 % 2)

                    def _wcev(g3=g2, f3=fq, t3=tile, b3=bk, w3=w):
                        c.w("te", te_wc[g3])
                        if f3 == 0:
                            c.w("dS", csfree[t3])
                        wcq[(w3 - 1, g3)] = c.o(
                            lambda e, f4=f3, t4=t3, b4=b3: e.activation(
                                cstage[t4][:, f4 * 256:(f4 + 1) * 256],
                                banks[b4][:, 0:256], AF.Identity), inc="sc")
                    wc_ev.append(_wcev)
            we_ev = []
            if has_we:
                for hc in range(32):
                    bk = hc % 4

                    def _weev(h3=hc, b3=bk, w3=w):
                        c.w("te", te_we[h3])
                        if h3 == 0:
                            c.w("ve", stqfin.get(w3 - 1, 0))
                        weq[(w3 + 1, h3)] = c.o(
                            lambda e, h4=h3, b4=b3: e.activation(
                                Pn[:, h4, :], banks[b4][:, 0:TS],
                                AF.Identity,
                                bias=bexp[:, h4:h4 + 1]), inc="sc")
                    we_ev.append(_weev)

            opn = 0
            cur = [Q_, P_]
            for sg in range(5):
                k = 1 << sg
                stg = 7 + sg
                f_, d_ = cur
                for cc in range(32):
                    sl = cc % 4
                    c.w("ve", stq4[sl])
                    stq_sc = c.o(lambda e, c2=cc, s2=stg, f2=f_, sl2=sl:
                                 e.activation(tmpb[sl2][:], f2[:, c2, :],
                                              AF.Identity,
                                              scale=bwd[:, s2, c2:c2 + 1]),
                                 inc="sc")
                    v.w("sc", stq_sc)
                    if sg == 0 and cc == 0:
                        v.w("sc", mupd[w])
                    stq4[sl] = v.o(
                        lambda e, c2=cc, s2=stg, f2=f_, d2=d_, kk=k, sl2=sl:
                        e.scalar_tensor_tensor(
                            d2[:, c2, :], f2[:, c2 ^ kk, :],
                            bwo[:, s2, c2:c2 + 1], tmpb[sl2][:],
                            ALU.mult, ALU.add), inc="ve")
                    opn += 1
                    if opn % 24 == 0 and wc_ev:
                        wc_ev.pop(0)()
                cur = [d_, f_]
            c.w("ve", stq4[3])
            for cc in range(32):
                gelq[w] = c.o(lambda e, c2=cc: e.activation(
                    Q_[:, c2, :], P_[:, c2, :], AF.Gelu,
                    bias=upb[:, c2:c2 + 1]), inc="sc")
                opn += 1
                if opn % 24 == 0 and wc_ev:
                    wc_ev.pop(0)()
            while wc_ev:
                wc_ev.pop(0)()
            for cc in range(32):
                bk = cc % 4
                c.w("te", te_md[cc])
                mdq[(w, cc)] = c.o(lambda e, c2=cc, b2=bk: e.activation(
                    P_[:, c2, :], banks[b2][:, 0:TS], AF.Identity),
                    inc="sc")
            opn = 0
            cur = [P_, Q_]
            for sg in range(5):
                k = 1 << sg
                stg = 7 + sg
                f_, d_ = cur
                for cc in range(32):
                    sl = cc % 4
                    c.w("ve", stq4[sl])
                    stq_sc = c.o(lambda e, c2=cc, s2=stg, f2=f_, sl2=sl:
                                 e.activation(tmpb[sl2][:], f2[:, c2, :],
                                              AF.Identity,
                                              scale=dwd[:, s2, c2:c2 + 1]),
                                 inc="sc")
                    v.w("sc", stq_sc)
                    if sg == 0 and cc == 0:
                        v.w("sc", mdq[(w, 31)])
                    stq4[sl] = v.o(
                        lambda e, c2=cc, s2=stg, f2=f_, d2=d_, kk=k, sl2=sl:
                        e.scalar_tensor_tensor(
                            d2[:, c2, :], f2[:, c2 ^ kk, :],
                            dwo[:, s2, c2:c2 + 1], tmpb[sl2][:],
                            ALU.mult, ALU.add), inc="ve")
                    opn += 1
                    if opn % 5 == 0 and we_ev:
                        we_ev.pop(0)()
                cur = [d_, f_]
            while we_ev:
                we_ev.pop(0)()
            stqfin[w] = p.v("ve")
            if has_we:
                for cc in range(32):
                    bk = cc % 4
                    c.w("te", te_mu[cc])
                    muq[(w + 1, cc)] = c.o(
                        lambda e, c2=cc, b2=bk: e.activation(
                            Qn[:, c2, :], banks[b2][:, 0:TS], AF.Identity),
                        inc="sc")
                mupd[w + 1] = p.v("sc")

            Qp = sl4[2 * ((w - 1) % 2) + 1]
            if has_wc:
                t_.w("ve", stqfin[w - 1])
                for g2 in range(8):
                    fq, tsub = g2 // 2, g2 % 2
                    key = (w - 1) * 4 + fq
                    bk = 2 + (g2 % 2)
                    t_.w("dW", wc_t[key])
                    if g2 < 2:
                        t_.w("sc", muq[(w, 28 + bk)])
                    else:
                        t_.w("sc", wcq[(w - 1, g2 - 2)])
                    for hc in range(32):
                        t_.o(lambda e, h2=hc, k2=key % 2, b2=bk, t3=tsub:
                             e.matmul(banks[b2][:, 0:256],
                                      Qp[:, h2, t3 * 128:(t3 + 1) * 128],
                                      wconb[k2][:, h2, :],
                                      start=(h2 == 0), stop=(h2 == 31)),
                             inc="te" if hc == 31 else None)
                    assert p.v("te") == te_wc[g2]
            t_.w("sc", gelq[w])
            for cc in range(32):
                bk = cc % 4
                if cc < 4:
                    if has_wc and bk >= 2:
                        t_.w("sc", wcq[(w - 1, 6 + (bk - 2))])
                    else:
                        t_.w("sc", muq[(w, 28 + bk)])
                else:
                    t_.w("sc", mdq[(w, cc - 4)])
                t_.o(lambda e, c2=cc, b2=bk: e.matmul(
                    banks[b2][:, 0:TS], G["Mdn"][:, c2 * 128:(c2 + 1) * 128],
                    Q_[:, c2, :], start=True, stop=True), inc="te")
                assert p.v("te") == te_md[cc]
            if has_we:
                for hc in range(32):
                    wi = hc // 4
                    key = (w + 1) * 8 + wi
                    bk = hc % 4
                    t_.w("dW", we_t[key])
                    if hc < 4:
                        t_.w("sc", mdq[(w, 28 + bk)])
                    else:
                        t_.w("sc", weq[(w + 1, hc - 4)])
                    for dc in range(NCH):
                        t_.o(lambda e, d=dc, h2=hc % 4, k2=key % 2, b2=bk,
                             w3=w: e.matmul(
                                 banks[b2][:, 0:TS],
                                 wexpb[k2][:, d, h2 * 128:(h2 + 1) * 128],
                                 xnT[:, d, (w3 + 1) * TS:(w3 + 2) * TS],
                                 start=(d == 0), stop=(d == NCH - 1)),
                             inc="te" if dc == NCH - 1 else None)
                    assert p.v("te") == te_we[hc]
                for cc in range(32):
                    bk = cc % 4
                    t_.w("sc", weq[(w + 1, min(28 + bk, 31))])
                    t_.w("sc", weq[(w + 1, cc)])
                    t_.w("sc", muq.get((w + 1, cc - 4), 0))
                    t_.o(lambda e, c2=cc, b2=bk: e.matmul(
                        banks[b2][:, 0:TS],
                        G["Mup"][:, c2 * 128:(c2 + 1) * 128],
                        Pn[:, c2, :], start=True, stop=True), inc="te")
                    assert p.v("te") == te_mu[cc]

            if has_wc:
                for tsub in range(2):
                    tile = ((w - 1) % 2) * 2 + tsub
                    s.w("sc", wcq[(w - 1, 6 + tsub)])
                    row = (w - 1) * TS + tsub * 128
                    csfree[tile] = s.dma(moe_dram[row:row + 128, :],
                                         cstage[tile][:])
        if w == 0 and partner_cb:
            partner_cb()

    w = NS
    with Sec(p, "tensor") as t_, Sec(p, "scalar") as c, \
            Sec(p, "sync") as s, Sec(p, "gpsimd") as g:
        Qp = sl4[2 * ((w - 1) % 2) + 1]
        t_.w("ve", stqfin[w - 1])
        for g2 in range(8):
            fq, tsub = g2 // 2, g2 % 2
            key = (w - 1) * 4 + fq
            tile = ((w - 1) % 2) * 2 + tsub
            bk = 2 + (g2 % 2)
            t_.w("dW", wc_t[key])
            if g2 >= 2:
                t_.w("sc", wcq[(w - 1, g2 - 2)])
            else:
                t_.w("sc", mdq[(NS - 1, 28 + bk)])
            for hc in range(32):
                t_.o(lambda e, h2=hc, k2=key % 2, b2=bk, t3=tsub:
                     e.matmul(banks[b2][:, 0:256],
                              Qp[:, h2, t3 * 128:(t3 + 1) * 128],
                              wconb[k2][:, h2, :],
                              start=(h2 == 0), stop=(h2 == 31)),
                     inc="te" if hc == 31 else None)
            wc_use[key] = p.v("te")
            if g2 % 2 == 1 and key + 2 < NS * 4:
                load_wcon(g, key + 2)
            c.w("te", p.v("te"))
            if fq == 0:
                c.w("dS", csfree[tile])
            wcq[(w - 1, g2)] = c.o(
                lambda e, f4=fq, t4=tile, b4=bk: e.activation(
                    cstage[t4][:, f4 * 256:(f4 + 1) * 256],
                    banks[b4][:, 0:256], AF.Identity), inc="sc")
        for tsub in range(2):
            tile = ((w - 1) % 2) * 2 + tsub
            s.w("sc", wcq[(w - 1, 6 + tsub)])
            row = (w - 1) * TS + tsub * 128
            csfree[tile] = s.dma(moe_dram[row:row + 128, :],
                                 cstage[tile][:])
    ph.close()



def _p4_attn(p, ctx, L, xnT, attnout, pes):
    A, G, sb = ctx["A"], ctx["G"], ctx["sb"]
    kvpK, kviK, kvp, kvi = ctx["kvpK"], ctx["kviK"], ctx["kvp"], ctx["kvi"]
    qfm_dram = ctx["qfm_dram"]
    ph = contextlib.ExitStack()
    Kt = [sb("a_K%d_%d" % (L, i), (65, SK), BF16, ph) for i in range(4)]
    Vt = [sb("a_V%d_%d" % (L, i), (128, NSK, 130), BF16, ph)
          for i in range(2)]
    Qt = [sb("a_Q%d_%d" % (L, i), (65, SC), BF16, ph) for i in range(4)]
    PT = sb("a_PT%d" % L, (128, 2048), BF16, ph)
    G["sq_bf"] = sb("a_sq%d" % L, (64, SK), BF16, ph)
    diagm = sb("a_dm%d" % L, (128, 4, 512), BF16, ph)
    G["diagm"] = diagm
    with Sec(p, "gpsimd") as g0:
        g0.dma(diagm[:], A["diagmask"].rearrange("v p n -> p v n"), "dG")
    pvs = [sb("a_pvs%d_%d" % (L, i), (64, 512), F32, ph) for i in range(2)]
    rs = [sb("a_rs%d_%d" % (L, i), (1, 1024), F32, ph) for i in range(2)]
    rsb = [sb("a_rsb%d_%d" % (L, i), (1, 512), BF16, ph) for i in range(2)]
    if L == 0:
        wqb = sb("a_wq", (128, NCH, D), BF16, ph)
        bqt = sb("a_bq", (128, NCH), F32, ph)

    with Sec(p, "vector") as v:
        for i in range(2):
            v.o(lambda e, i2=i: e.memset(Vt[i2][:, :, 64:65], 1.0))
            v.o(lambda e, i2=i: e.memset(Vt[i2][:, :, 129:130], 1.0))

    with Sec(p, "sync") as s, Sec(p, "vector") as v:
        s.w("dG", p.v("dG"))
        s.dma(G["kmax_bf"][0:1, 1:2], kvp[STATROW:STATROW + 1, 0:1])
        kd2 = s.dma(G["kmax_bf"][0:1, 2:3], kvi[STATROW:STATROW + 1, 0:1])
        v.w("dS", kd2)
        v.o(lambda e: e.tensor_copy(G["kmax"][0:1, 1:2],
                                    G["kmax_bf"][0:1, 1:2]))
        v.o(lambda e: e.tensor_copy(G["kmax"][0:1, 2:3],
                                    G["kmax_bf"][0:1, 2:3]))
        v.drain()
        kmq = v.o(lambda e: e.tensor_tensor(
            G["kmax"][0:1, 1:2], G["kmax"][0:1, 1:2], G["kmax"][0:1, 2:3],
            ALU.max), inc="ve")

    wq_t = 0
    bq8 = 0
    if L == 0:
        with Sec(p, "gpsimd") as g, Sec(p, "sync") as s, \
                Sec(p, "scalar") as c:
            p.fence(g)
            wq_t = g.dma(wqb[:],
A["wqT"].rearrange("(c p) d -> p c d", p=128), "dW")
            bq_d = s.dma(bqt[:], A["bq_r"])
            c.w("dS", bq_d)
            bq8 = c.o(lambda e: e.activation(bqt[:], bqt[:], AF.Identity,
                                             scale=RSQ), inc="sc")

    kv_pull = {}
    q_done = {}
    bq_last = [0]

    def pulls(g_idx):
        sl = g_idx % 2
        with Sec(p, "sync") as s, Sec(p, "scalar") as c:
            s.w("te", p.v("te"))
            c.w("te", p.v("te"))
            for hp in range(2):
                h = 2 * g_idx + hp
                s.dma(Kt[2 * sl + hp][:, 0:SC], kvpK[65 * h:65 * h + 65, :])
                s.dma(Kt[2 * sl + hp][:, SC:SK], kviK[65 * h:65 * h + 65, :])
            last = 0
            for j in range(NSK):
                sec = s if j % 2 == 0 else c
                src = kvp if j < 16 else kvi
                row0 = VBASE + (j % 16) * 128
                last = sec.o(
                    lambda e, s2=sl, j2=j, r0=row0, sr=src, g2=g_idx:
                    e.dma_start(
                        out=Vt[s2][:, j2, 0:130].rearrange(
                            "p (h d) -> p h d", d=65)[:, :, 0:64],
                        in_=sr[r0:r0 + 128, g2 * 128:(g2 + 1) * 128]),
                    inc=("dS" if j % 2 == 0 else "dC"), n=16)
            kv_pull[g_idx] = (p.v("dS"), p.v("dC"))

    def build_q(g_idx):
        sl = g_idx % 2
        with Sec(p, "tensor") as t_, Sec(p, "scalar") as c, \
                Sec(p, "vector") as v, Sec(p, "sync") as s:
            for hp in range(2):
                h = 2 * g_idx + hp
                qtile = Qt[2 * sl + hp]
                t_.w("ve", bq_last[0])
                if L == 0:
                    t_.w("dW", wq_t)
                    for s4 in range(4):
                        t_.w("sc", p.v("sc"))
                        for dc in range(NCH):
                            t_.o(lambda e, d=dc, g2=g_idx, ss=s4: e.matmul(
                                G["psD"][:],
                                wqb[:, d, g2 * 128:(g2 + 1) * 128],
                                xnT[:, d, ss * 512:(ss + 1) * 512],
                                start=(d == 0), stop=(d == NCH - 1)),
                                inc="te" if dc == NCH - 1 else None)
                        c.w("te", p.v("te"))
                        c.w("sc", bq8)
                        c.o(lambda e, ss=s4, hp2=hp, q2=qtile, g2=g_idx:
                            e.activation(
                                q2[0:64, ss * 512:(ss + 1) * 512],
                                G["psD"][hp2 * 64:hp2 * 64 + 64, :],
                                AF.Identity, scale=RSQ,
                                bias=bqt[hp2 * 64:hp2 * 64 + 64,
                                         g2:g2 + 1]), inc="sc")
                else:
                    dq = s.dma(qtile[0:64, :],
                               qfm_dram[g_idx * 128 + hp * 64:
                                        g_idx * 128 + hp * 64 + 64, :])
                    c.w("dS", dq)
                c.w("sc", p.v("sc"))
                c.w("te", p.v("te"))
                sq_q = c.o(lambda e, q2=qtile: e.activation(
                    G["sq_bf"][0:64, 0:SC], q2[0:64, :], AF.Square),
                    inc="sc")
                for s4 in range(4):
                    t_.w("sc", sq_q)
                    t_.w("ve", p.v("ve"))
                    tq = t_.o(lambda e, ss=s4: e.matmul(
                        G["psD"][0:1, :], G["ones64"][:],
                        G["sq_bf"][0:64, ss * 512:(ss + 1) * 512],
                        start=True, stop=True), inc="te")
                    v.w("te", tq)
                    v.w("ve", kmq)
                    q_done[(g_idx, hp)] = v.o(
                        lambda e, ss=s4, q2=qtile: e.tensor_scalar(
                            q2[64:65, ss * 512:(ss + 1) * 512],
                            G["psD"][0:1, :], G["kmax"][0:1, 1:2], -0.5,
                            ALU.add, ALU.mult), inc="ve")
                bq_last[0] = q_done[(g_idx, hp)]

    pulls(0)
    build_q(0)

    epi = {"pend": None, "tt0": 0, "tt1": 0}

    def epi_rd(t_, c, v, ci):
        c0p, hpp, pvd, g2p = epi["pend"]
        pspv = [G["psA"], G["psB"]][ci]
        c.w("te", pvd[ci])
        if epi["tt%d" % ci]:
            c.w("ve", epi["tt%d" % ci])
        epi["pq%d" % ci] = c.o(lambda e, pv=pspv, c2=ci: e.activation(
            pvs[c2][:], pv[0:64, :], AF.Identity), inc="sc")
        v.w("te", pvd[ci])
        epi["rd%d" % ci] = v.o(lambda e, pv=pspv, c2=ci: e.tensor_copy(
            rs[c2][0:1, 0:512], pv[64:65, :]), inc="ve")

    def epi_rc(t_, c, v, ci):
        v.drain()
        v.o(lambda e, c2=ci: e.reciprocal(rs[c2][0:1, 512:1024],
                                          rs[c2][0:1, 0:512]))
        v.drain()
        epi["vq%d" % ci] = v.o(lambda e, c2=ci: e.tensor_copy(
            rsb[c2][0:1, :], rs[c2][0:1, 512:1024]), inc="ve")

    def epi_a(t_, c, v, ci):
        epi_rd(t_, c, v, ci)
        epi_rc(t_, c, v, ci)

    def epi_b(t_, c, v):
        c0p, hpp, pvd, g2p = epi["pend"]
        if epi["tt1"]:
            t_.w("ve", epi["tt1"])
        tqs = [0, 0]
        for ci in range(2):
            t_.w("ve", epi["vq%d" % ci])
            tqs[ci] = t_.o(lambda e, c2=ci: e.matmul(
                G["psC"][c2 * 64:c2 * 64 + 64, :], G["ones_col"][:],
                rsb[c2][0:1, :], start=True, stop=True), inc="te")
        for ci in range(2):
            s4 = c0p + ci
            v.w("te", tqs[1])
            v.w("sc", epi["pq%d" % ci])
            epi["tt%d" % ci] = v.o(
                lambda e, hp2=hpp, g2=g2p, ss=s4, c2=ci:
                e.tensor_tensor(
                    attnout[hp2 * 64:hp2 * 64 + 64, g2,
                            ss * 512:(ss + 1) * 512],
                    pvs[c2][:], G["psC"][c2 * 64:c2 * 64 + 64, :],
                    ALU.mult),
                inc="ve")

    for g_idx in range(NH // 2):
        sl = g_idx % 2
        if g_idx + 1 < NH // 2:
            pulls(g_idx + 1)
            build_q(g_idx + 1)
        with Sec(p, "tensor") as t_, Sec(p, "scalar") as c, \
                Sec(p, "vector") as v:
            t_.w("dS", kv_pull[g_idx][0])
            t_.w("dC", kv_pull[g_idx][1])
            exq = [0, 0]
            pvq = [0, 0]
            for hp in range(2):
                h = 2 * g_idx + hp
                ktile = Kt[2 * sl + hp]
                qtile = Qt[2 * sl + hp]
                t_.w("ve", q_done[(g_idx, hp)])
                for sw in range(2):
                    c0 = 2 * sw
                    jm = [20 + 4 * c0, 20 + 4 * c0 + 4]
                    pvps = [G["psA"], G["psB"]]
                    pvd = [0, 0]

                    def emit_scores(j):
                        ph2 = j % 2
                        valid = [ci for ci in range(2) if j < jm[ci]]
                        cstart = valid[0]
                        wid = len(valid) * 512
                        t_.w("sc", exq[ph2])
                        for ci in valid:
                            t_.o(lambda e, jj=j, q4=ci, ss=c0, kt=ktile,
                                 qt=qtile, hh=ph2: e.matmul(
                                     G["psQ"][:, hh * 1024 + q4 * 512:
                                              hh * 1024 + (q4 + 1) * 512],
                                     kt[:, jj * 128:(jj + 1) * 128],
                                     qt[:, (ss + q4) * 512:
                                        (ss + q4 + 1) * 512],
                                     start=True, stop=True),
                                 inc="te" if ci == valid[-1] else None)
                        c.w("te", p.v("te"))
                        if pvq[ph2]:
                            c.w("te", pvq[ph2])
                        kill = j < 16
                        exq[ph2] = c.o(
                            lambda e, hh=ph2, k2=kill, cs=cstart, w2=wid:
                            e.activation(
                                PT[:, hh * 1024 + cs * 512:
                                   hh * 1024 + cs * 512 + w2],
                                G["psQ"][:, hh * 1024 + cs * 512:
                                         hh * 1024 + cs * 512 + w2],
                                AF.Exp,
                                bias=(G["kill_t"][:, 0:1] if k2 else 0.0)),
                            inc="sc")
                        nm = 0
                        mq = 0
                        for ci in valid:
                            jl = j - 16 - 4 * (c0 + ci)
                            if 0 <= jl <= 3:
                                if nm == 0:
                                    v.w("sc", exq[ph2])
                                nm += 1
                                mq = v.o(lambda e, hh=ph2, q4=ci, vv=jl:
                                         e.tensor_tensor(
                                             PT[:, hh * 1024 + q4 * 512:
                                                hh * 1024 + (q4 + 1) * 512],
                                             PT[:, hh * 1024 + q4 * 512:
                                                hh * 1024 + (q4 + 1) * 512],
                                             G["diagm"][:, vv, :], ALU.mult),
                                         inc="ve")
                        return (exq[ph2], mq if nm else 0)

                    def emit_pv(j, gates):
                        ph2 = j % 2
                        valid = [ci for ci in range(2) if j < jm[ci]]
                        sq2, mq = gates
                        if mq:
                            t_.w("ve", mq)
                        else:
                            t_.w("sc", sq2)
                        for ci in valid:
                            t_.o(lambda e, jj=j, q4=ci, pv=pvps[ci], s2=sl,
                                 hp2=hp, hh=ph2, jmx=jm[ci]: e.matmul(
                                     pv[0:65, :],
                                     Vt[s2][:, jj, hp2 * 65:hp2 * 65 + 65],
                                     PT[:, hh * 1024 + q4 * 512:
                                        hh * 1024 + (q4 + 1) * 512],
                                     start=(jj == 0), stop=(jj == jmx - 1)),
                                 inc="te" if ci == valid[-1] else None)
                        pvq[ph2] = p.v("te")
                        for ci in valid:
                            if j == jm[ci] - 1:
                                pvd[ci] = p.v("te")

                    pend0 = epi["pend"] is not None
                    gates = {}
                    gates[0] = emit_scores(0)
                    if pend0:
                        epi_rd(t_, c, v, 0)
                        epi_rd(t_, c, v, 1)
                        epi_rc(t_, c, v, 0)
                        epi_rc(t_, c, v, 1)
                    gates[1] = emit_scores(1)
                    if pend0:
                        t_.w("sc", epi["pq0"])
                        t_.w("ve", epi["rd0"])
                        t_.w("sc", epi["pq1"])
                        t_.w("ve", epi["rd1"])
                    emit_pv(0, gates[0])
                    gates[2] = emit_scores(2)
                    emit_pv(1, gates[1])
                    gates[3] = emit_scores(3)
                    emit_pv(2, gates[2])
                    if pend0:
                        epi_b(t_, c, v)
                    for j in range(4, jm[1]):
                        gates[j] = emit_scores(j)
                        emit_pv(j - 1, gates[j - 1])
                    emit_pv(jm[1] - 1, gates[jm[1] - 1])
                    epi["pend"] = (c0, hp, pvd[:], g_idx)

    with Sec(p, "tensor") as t_, Sec(p, "scalar") as c, \
            Sec(p, "vector") as v:
        epi_a(t_, c, v, 0)
        epi_a(t_, c, v, 1)
        epi_b(t_, c, v)
        epi["pend"] = None
        epi["tt0"] = 0
        epi["tt1"] = 0
        t_.w("ve", p.v("ve"))
    if "attnout" in ctx["dbg"] and L == 0:
        with Sec(p, "sync") as s:
            s.w("ve", p.v("ve"))
            s.dma(ctx["dbg"]["attnout"],
                  attnout[:].rearrange("p c s -> p (c s)"))
    p.barrier()
    ph.close()



def _p5_residual(p, ctx, L, attnout, x_in, x_out):
    A, G, sb = ctx["A"], ctx["G"], ctx["sb"]
    moe_dram = ctx["moe_dram"]
    ph = contextlib.ExitStack()
    wob = sb("p5wo%d" % L, (128, NCH, D), BF16, ph)
    G["biasrep"] = sb("p5br%d" % L, (128, D), F32, ph)
    G["biasrep2"] = sb("p5br2%d" % L, (128, D), F32, ph)
    xb = [sb("p5x%d_%d" % (L, i), (128, D), F32, ph) for i in range(2)]
    mb = [sb("p5m%d_%d" % (L, i), (128, D), F32, ph) for i in range(2)]
    ob = [sb("p5o%d_%d" % (L, i), (128, D), F32, ph) for i in range(2)]

    with Sec(p, "gpsimd") as g:
        p.fence(g)
        wo_t = g.dma(wob[:],
                     A["woT" if L == 0 else "ow"]
                     .rearrange("(c p) d -> p c d", p=128), "dW")
    with Sec(p, "sync") as s, Sec(p, "vector") as v:
        b1 = s.dma(G["biasrep"][:], A["bo_rep" if L == 0 else "bcon_rep"])
        bias_q = 0
        if L == 1:
            b2 = s.dma(G["biasrep2"][:], A["dbias_rep"])
            v.w("dS", b2)
            bias_q = v.o(lambda e: e.tensor_tensor(
                G["biasrep"][:], G["biasrep"][:], G["biasrep2"][:], ALU.add),
                inc="ve")

    ob_q = [0, 0]
    with Sec(p, "sync") as s, Sec(p, "scalar") as c, Sec(p, "tensor") as t_, \
            Sec(p, "vector") as v:
        t_.w("dW", wo_t)
        for t in range(NT):
            sl = t % 2
            if t >= 2:
                s.w("ve", ob_q[sl])
                c.w("ve", ob_q[sl])
            dx = s.dma(xb[sl][:], x_in[t * 128:(t + 1) * 128, :])
            dm = c.dma(mb[sl][:], moe_dram[t * 128:(t + 1) * 128, :],
                       inc="dC")
            if t >= 2:
                t_.w("ve", ob_q[sl])
            for fh in range(2):
                for dc in range(NCH):
                    lastmm = (dc == NCH - 1) and L == 1
                    t_.o(lambda e, d=dc, f2=fh, tt=t, s2=sl, lm=lastmm:
                         e.matmul(
                             G["psQ"][:, s2 * 1024 + f2 * 512:
                                      s2 * 1024 + (f2 + 1) * 512],
                             attnout[:, d, tt * 128:(tt + 1) * 128],
                             wob[:, d, f2 * 512:(f2 + 1) * 512],
                             start=(d == 0), stop=lm),
                         inc="te" if (lastmm and fh == 1) else None)
                if L == 0:
                    t_.o(lambda e, f2=fh, tt=t, s2=sl: e.matmul(
                        G["psQ"][:, s2 * 1024 + f2 * 512:
                                 s2 * 1024 + (f2 + 1) * 512],
                        G["wT_sb"][0:8, tt * 128:(tt + 1) * 128],
                        G["b3_sb"][0:8, f2 * 512:(f2 + 1) * 512],
                        start=False, stop=True),
                        inc="te" if fh == 1 else None)
            v.w("te", p.v("te"))
            v.w("dS", dx)
            v.w("dC", dm)
            v.w("ve", bias_q)
            v.o(lambda e, s2=sl: e.tensor_tensor(
                ob[s2][:], G["psQ"][:, s2 * 1024:s2 * 1024 + 1024],
                xb[s2][:], ALU.add))
            v.drain()
            v.o(lambda e, s2=sl: e.tensor_tensor(ob[s2][:], ob[s2][:],
                                                 mb[s2][:], ALU.add))
            v.drain()
            ob_q[sl] = v.o(lambda e, s2=sl: e.tensor_tensor(
                ob[s2][:], ob[s2][:], G["biasrep"][:], ALU.add), inc="ve")
            s.w("ve", ob_q[sl])
            s.dma(x_out[t * 128:(t + 1) * 128, :], ob[sl][:])
    ph.close()



def kernel(**inputs):
    shared, per_core = _host_prepare(inputs)
    nc = _build(DEBUG)
    in_maps = []
    for c in range(8):
        m = dict(shared)
        m.update(per_core[c])
        in_maps.append(m)
    res = run_bass_kernel_spmd(nc, in_maps, core_ids=list(range(8)))
    out = np.empty((4, 4096, D), np.float32)
    for c in range(8):
        b, hf = c // 2, c % 2
        out[b, hf * SC:(hf + 1) * SC, :] = res.results[c]["out_x"]
    return out


# version 10
# speedup vs baseline: 1.0502x; 1.0120x over previous
import math
import os
import contextlib
import numpy as np
import ml_dtypes

import concourse.bass as bass
import concourse.mybir as mybir
from concourse.bass_utils import run_bass_kernel_spmd
from concourse.bass_core_rust import IndirectOffsetOnAxis

F32 = mybir.dt.float32
BF16 = mybir.dt.bfloat16
I32 = mybir.dt.int32
AF = mybir.ActivationFunctionType
ALU = mybir.AluOpType
AX = mybir.AxisListType

D = 1024
SC = 2048
NT = SC // 128
NH = 16
HD = 64
NE = 8
HM = 2048
DH = 4096
NCH = D // 128
KROWS = NH * 65
VBASE = 2 * KROWS
STATROW = VBASE + SC
KVROWS = 4224
SK = 2 * SC
NSK = SK // 128
RSQ = float((1.0 / 8.0) ** 0.5)
EPS = 1e-5

DEBUG = ()
KPH = int(os.environ.get("KPH", "99"))
_PHC = [0]


def _ph_on():
    _PHC[0] += 1
    return _PHC[0] <= KPH

DEBUG_SHAPES = {
    "xn": (SC, D),
    "gate": (128, NT * NE),
    "attnout": (128, NCH * SC),
}


class Plan:
    ENGS = ("sync", "tensor", "scalar", "vector", "gpsimd")
    QUEUES = {"dS": 6, "dC": 4, "dW": 4, "dG": 6}

    def __init__(self, nc, block, es):
        self.nc = nc
        self.block = block
        self.cnt = {}
        self.sems = {}
        for name in ("cc", "te", "sc", "ve", "gp"):
            self.sems[name] = es.enter_context(nc.semaphore(name))
            self.cnt[name] = 0
        for eng in self.ENGS:
            self.sems["bar_" + eng] = es.enter_context(
                nc.semaphore("bar_" + eng))
        self.dlane = {}
        self.dcnt = {}
        self.dmap = {}
        for q, nl in self.QUEUES.items():
            self.cnt[q] = 0
            self.dlane[q] = [es.enter_context(nc.semaphore("%s%d" % (q, i)))
                             for i in range(nl)]
            self.dcnt[q] = [0] * nl
            self.dmap[q] = {}
        self.nbar = 0

    def inc(self, name, n=1):
        self.cnt[name] += n
        return self.sems[name]

    def v(self, name):
        return self.cnt[name]

    def fence(self, sec):
        for q in self.QUEUES:
            sec.w(q, self.cnt[q])
        for s in ("te", "sc", "ve"):
            sec.w(s, self.cnt[s])

    def queue_wait_ops(self, q, pseudo):
        if pseudo <= 0:
            return []
        snap = self.dmap[q].get(pseudo)
        if snap is None:
            keys = [k for k in self.dmap[q] if k >= pseudo]
            if keys:
                snap = self.dmap[q][min(keys)]
            else:
                snap = tuple(self.dcnt[q])
        return [(self.dlane[q][i], v) for i, v in enumerate(snap) if v > 0]

    def barrier(self):
        self.nbar += 1
        tgt = self.nbar
        dwaits = []
        for q in self.QUEUES:
            dwaits.extend((h, v) for h, v in
                          [(self.dlane[q][i], self.dcnt[q][i])
                           for i in range(len(self.dlane[q]))] if v > 0)
        for eng in self.ENGS:
            own = self.sems["bar_" + eng]
            others = [self.sems["bar_" + x] for x in self.ENGS if x != eng]
            seq_only = eng == "sync"
            def _(e, _t=tgt, _own=own, _o=tuple(others), _dw=tuple(dwaits),
                  _sq=seq_only):
                for h, v in _dw:
                    e.wait_ge(h, v)
                if _sq:
                    e.sem_inc(_own, 1)
                else:
                    e.drain().then_inc(_own, 1)
                for h in _o:
                    e.wait_ge(h, _t)
            getattr(self.block, eng)(_)
class Sec:
    def __init__(self, plan, eng):
        self.p = plan
        self.eng = eng
        self.ops = []

    def __enter__(self):
        return self

    def __exit__(self, et, ev, tb):
        if et is not None:
            return False
        ops = self.ops
        getattr(self.p.block, self.eng)(lambda e: [f(e) for f in ops])
        return False

    def w(self, sem, val):
        if not val or val <= 0:
            return
        if sem in self.p.QUEUES:
            for h, v in self.p.queue_wait_ops(sem, val):
                self.ops.append(lambda e, s=h, vv=v: e.wait_ge(s, vv))
        else:
            self.ops.append(
                lambda e, s=self.p.sems[sem], v=val: e.wait_ge(s, v))

    def o(self, fn, inc=None, n=1):
        if inc is None:
            self.ops.append(fn)
            return None
        if inc in self.p.QUEUES:
            return self._dma_o(fn, inc)
        sem = self.p.inc(inc, n)
        tgt = self.p.v(inc)

        def g(e, _f=fn, _s=sem, _n=n):
            _f(e).then_inc(_s, _n)
        self.ops.append(g)
        return tgt

    def _dma_o(self, fn, q):
        p = self.p
        li = (p.cnt[q] // 16) % len(p.dlane[q])
        prev = p.dcnt[q][li]
        if prev > 0:
            self.ops.append(
                lambda e, s=p.dlane[q][li], v=prev: e.wait_ge(s, v))
        p.dcnt[q][li] += 16
        p.cnt[q] += 16
        p.dmap[q][p.cnt[q]] = tuple(p.dcnt[q])
        sem = p.dlane[q][li]

        def g(e, _f=fn, _s=sem):
            _f(e).then_inc(_s, 16)
        self.ops.append(g)
        return p.cnt[q]

    def dma(self, out, in_, inc="dS", **kw):
        return self.o(lambda e, o=out, i=in_, k=kw:
                      e.dma_start(out=o, in_=i, **k), inc=inc, n=16)

    def drain(self):
        self.ops.append(lambda e: e.drain())



def _butterfly_coefs(w):
    ns = w.shape[0]
    i = np.arange(DH)
    wdiag = np.empty((ns, DH), np.float32)
    woff = np.empty((ns, DH), np.float32)
    for s in range(ns):
        s2 = 1 << s
        k = (i // (2 * s2)) * s2 + (i % s2)
        is_b = ((i // s2) % 2).astype(bool)
        wdiag[s] = np.where(is_b, w[s, k, 1, 1], w[s, k, 0, 0])
        woff[s] = np.where(is_b, w[s, k, 1, 0], w[s, k, 0, 1])
    woffx = np.empty_like(woff)
    for s in range(ns):
        woffx[s] = woff[s][i ^ (1 << s)]

    def jc(a):
        return np.ascontiguousarray(
            a.reshape(ns, 32, 128).transpose(0, 2, 1)).astype(np.float32)
    return jc(wdiag), jc(woff), jc(woffx)


def _host_prepare(inp):
    T = lambda a: np.ascontiguousarray(np.asarray(a, np.float32).T)
    C = lambda a: np.ascontiguousarray(np.asarray(a, np.float32))
    chunks = lambda a, n: np.ascontiguousarray(C(a).reshape(n, 128).T)
    rep = lambda a: np.ascontiguousarray(
        np.broadcast_to(C(a)[None, :], (128, np.asarray(a).shape[-1])))

    sh = {}
    sh["wqT"] = T(inp["L0_attn_Wq"]); sh["wkT"] = T(inp["L0_attn_Wk"])
    sh["wvT"] = T(inp["L0_attn_Wv"]); sh["woT"] = T(inp["L0_attn_Wo"])
    sh["bq_r"] = chunks(inp["L0_attn_bq"], NCH)
    sh["bk_r"] = chunks(inp["L0_attn_bk"], NCH)
    sh["bv_rep"] = rep(inp["L0_attn_bv"])
    sh["bo_rep"] = rep(inp["L0_attn_bo"])
    sh["w1"] = C(inp["L0_ffn_w1"]); sh["w2"] = C(inp["L0_ffn_w2"])
    sh["w3"] = C(inp["L0_ffn_w3"])
    sh["b1_r"] = np.stack([chunks(inp["L0_ffn_b1"][e], HM // 128)
                           for e in range(NE)])
    sh["b2_r"] = np.stack([chunks(inp["L0_ffn_b2"][e], HM // 128)
                           for e in range(NE)])
    sh["b3"] = C(inp["L0_ffn_b3"])
    sh["gate_w"] = C(inp["L0_ffn_gate_w"])
    sh["gate_b_rep"] = rep(inp["L0_ffn_gate_b"])
    sh["qw"] = C(inp["L1_attn_qw"]); sh["kw"] = C(inp["L1_attn_kw"])
    sh["vw"] = C(inp["L1_attn_vw"]); sh["ow"] = C(inp["L1_attn_ow"])
    sh["wexp_sw"] = np.ascontiguousarray(
        C(inp["L1_ffn_W_exp"]).reshape(8, 128, 8, 512)
        .transpose(2, 1, 0, 3).reshape(8, 128, 4096))
    sh["bexp_r"] = chunks(inp["L1_ffn_b_exp"], DH // 128)
    sh["upb_r"] = chunks(inp["L1_ffn_up_bias"], DH // 128)
    sh["wcon_sw"] = np.ascontiguousarray(
        C(inp["L1_ffn_W_con"]).reshape(32, 128, 4, 256)
        .transpose(2, 1, 0, 3).reshape(4, 128, 8192))
    sh["bcon_rep"] = rep(inp["L1_ffn_b_con"])
    sh["dbias_rep"] = rep(inp["L1_ffn_down_bias"])
    for pre, w in (("up", inp["L1_ffn_up_weights"]),
                   ("dn", inp["L1_ffn_down_weights"])):
        wd, wo, wox = _butterfly_coefs(C(w))
        sh[pre + "_wd"], sh[pre + "_wo"], sh[pre + "_wox"] = wd, wo, wox

    sh["eye"] = np.eye(128, dtype=np.float32)
    mx = np.zeros((7, 128, 128), np.float32)
    j = np.arange(128)
    for s in range(7):
        mx[s, j, j ^ (1 << s)] = 1
    sh["maskxor"] = mx
    dm = np.zeros((4, 128, 512), np.float32)
    pc = np.arange(128)[:, None]
    qc = np.arange(512)[None, :]
    for v in range(4):
        dm[v] = ((v * 128 + pc) <= qc)
    sh["diagmask"] = dm

    x_full = np.asarray(inp["L0_attn_x"], np.float32)
    inv = (10000.0 ** (-np.arange(0, HD, 2, dtype=np.float32) / HD)
           ).astype(np.float32)
    per_core = []
    for c in range(8):
        b, hf = c // 2, c % 2
        d = {}
        d["x0"] = np.ascontiguousarray(x_full[b, hf * SC:(hf + 1) * SC, :])
        d["kill"] = np.full((128, 1), 0.0 if hf == 1 else -1e9, np.float32)
        d["pidx"] = np.ascontiguousarray(
            (((c ^ 1) % 2) * KVROWS + np.arange(KVROWS))
            .astype(np.int32).reshape(KVROWS // 128, 128).T)
        pos = np.arange(hf * SC, (hf + 1) * SC, dtype=np.float32)
        f = pos[:, None] * inv[None, :]
        d["cos8"] = np.ascontiguousarray(
            np.tile(np.cos(f) * RSQ, (1, NH)).astype(np.float32))
        d["sin8"] = np.ascontiguousarray(
            np.tile(np.sin(f) * RSQ, (1, NH)).astype(np.float32))
        per_core.append(d)
    return sh, per_core



def _build(debug=()):
    nc = bass.Bass("TRN2", target_bir_lowering=False)
    A = {}

    def di(name, shape, dtype=F32):
        A[name] = nc.dram_tensor(name, list(shape), dtype,
                                 kind="ExternalInput").ap()

    di("x0", (SC, D))
    for n in ("wqT", "wkT", "wvT", "woT", "qw", "kw", "vw", "ow"):
        di(n, (D, D))
    di("bq_r", (128, NCH)); di("bk_r", (128, NCH))
    di("bv_rep", (128, D)); di("bo_rep", (128, D))
    di("w1", (NE, D, HM)); di("w2", (NE, HM, HM)); di("w3", (NE, HM, D))
    di("b1_r", (NE, 128, HM // 128)); di("b2_r", (NE, 128, HM // 128))
    di("b3", (NE, D))
    di("gate_w", (D, NE)); di("gate_b_rep", (128, NE))
    di("wexp_sw", (8, 128, 4096)); di("bexp_r", (128, DH // 128))
    di("upb_r", (128, DH // 128))
    di("wcon_sw", (4, 128, 8192))
    di("bcon_rep", (128, D)); di("dbias_rep", (128, D))
    for pre in ("up", "dn"):
        di(pre + "_wd", (12, 128, 32)); di(pre + "_wo", (12, 128, 32))
        di(pre + "_wox", (12, 128, 32))
    di("eye", (128, 128))
    di("maskxor", (7, 128, 128))
    di("diagmask", (4, 128, 512))
    di("kill", (128, 1))
    di("pidx", (128, KVROWS // 128), I32)
    di("cos8", (SC, NH * 32)); di("sin8", (SC, NH * 32))

    out_x = nc.dram_tensor("out_x", [SC, D], F32, kind="ExternalOutput").ap()
    dbg = {}
    for name in debug:
        dbg[name] = nc.dram_tensor("dbg_" + name, list(DEBUG_SHAPES[name]),
                                   F32, kind="ExternalOutput").ap()

    _kd = os.environ.get("KDBG") == "1"
    _kind = dict(kind="ExternalOutput") if _kd else {}
    x1_dram = nc.dram_tensor("x1_dram", [SC, D], F32, **_kind).ap()
    moe_dram = nc.dram_tensor("moe_dram", [SC, D], F32, **_kind).ap()
    qfm_dram = nc.dram_tensor("qfm_dram", [D, SC], BF16).ap()
    wexp_bf = nc.dram_tensor("wexp_bf", [8, 128, 4096], BF16).ap()
    wcon_bf = nc.dram_tensor("wcon_bf", [4, 128, 8192], BF16).ap()
    kv_in_t = nc.dram_tensor("kv_in", [KVROWS, 1024], BF16)
    kv_ga_t = nc.dram_tensor("kv_gather", [2, KVROWS, 1024], BF16)
    kv_pa_t = nc.dram_tensor("kv_partner", [KVROWS, 1024], BF16)
    kvi = kv_in_t.ap()
    kvp = kv_pa_t.ap()
    kviK = kvi.rearrange("(r two) w -> r (two w)", two=2)[0:KROWS, :]
    kvpK = kvp.rearrange("(r two) w -> r (two w)", two=2)[0:KROWS, :]
    kvgaflat = kv_ga_t.ap().rearrange("s r w -> (s r) w")

    es = contextlib.ExitStack()
    _uid = [0]

    def sb(name, shape, dtype=F32, stack=None):
        _uid[0] += 1
        return (stack or es).enter_context(
            nc.sbuf_tensor("%s_u%d" % (name, _uid[0]), list(shape), dtype))

    with nc.Block() as block:
        p = Plan(nc, block, es)

        G = {}
        G["ones_col"] = sb("ones_col", (1, 64), BF16)
        G["ones64"] = sb("ones64", (64, 1), BF16)
        G["eye_t"] = sb("eye_t", (128, 128), BF16)
        G["stats"] = sb("stats", (128, 112), F32)
        G["scr_bf"] = sb("scr_bf", (128, 1024), BF16)
        G["kill_t"] = sb("kill_t", (128, 1), F32)
        G["gate_all"] = sb("gate_all", (128, NT, NE), F32)
        G["gate_eq"] = sb("gate_eq", (128, NT, NE), F32)
        G["gateb_t"] = sb("gateb_t", (128, NE), F32)
        G["wT_sb"] = sb("wT_sb", (8, NT * 128), BF16)
        G["b3_sb"] = sb("b3_sb", (8, D), BF16)
        G["pidx_t"] = sb("pidx_t", (128, KVROWS // 128), I32)
        G["Mup"] = sb("Mup", (128, 32 * 128), BF16)
        G["Mdn"] = sb("Mdn", (128, 32 * 128), BF16)
        G["kmax"] = sb("kmax", (2, 8), F32)
        G["kmax_bf"] = sb("kmax_bf", (2, 8), BF16)
        G["krow"] = sb("krow", (2, 8), F32)

        G["psA"] = es.enter_context(nc.psum_tensor("psA", [128, 512], F32))
        G["psB"] = es.enter_context(nc.psum_tensor("psB", [128, 512], F32))
        G["psC"] = es.enter_context(nc.psum_tensor("psC", [128, 512], F32))
        G["psD"] = es.enter_context(nc.psum_tensor("psD", [128, 512], F32))
        G["psQ"] = es.enter_context(nc.psum_tensor("psQ", [128, 2048], F32))

        with Sec(p, "sync") as s:
            s.dma(G["kill_t"][:], A["kill"])
            s.dma(G["gateb_t"][:], A["gate_b_rep"])
            s.dma(G["pidx_t"][:], A["pidx"])
        with Sec(p, "gpsimd") as g:
            g.dma(G["b3_sb"][:], A["b3"], "dG")
            g.dma(G["eye_t"][:], A["eye"], "dG")
        with Sec(p, "vector") as v:
            v.o(lambda e: e.memset(G["ones_col"][:], 1.0))
            v.o(lambda e: e.memset(G["ones64"][:], 1.0))
            zq = v.o(lambda e: e.memset(G["scr_bf"][:], 0.0), inc="ve")
        with Sec(p, "sync") as s:
            s.w("ve", zq)
            s.dma(kvi[STATROW:KVROWS, :], G["scr_bf"][0:KVROWS - STATROW, :])
        p.barrier()

        rp_sem = es.enter_context(nc.semaphore("rp"))
        with Sec(p, "gpsimd") as g:
            for i in range(8):
                g.ops.append(lambda e, i2=i: e.dma_start(
                    out=wexp_bf[i2], in_=A["wexp_sw"][i2]).then_inc(
                        rp_sem, 16))
            for i in range(4):
                g.ops.append(lambda e, i2=i: e.dma_start(
                    out=wcon_bf[i2], in_=A["wcon_sw"][i2]).then_inc(
                        rp_sem, 16))

        ctx = dict(A=A, G=G, kvi=kvi, kvp=kvp, kviK=kviK, kvpK=kvpK,
                   kvgaflat=kvgaflat, kv_in_t=kv_in_t, kv_ga_t=kv_ga_t,
                   moe_dram=moe_dram, qfm_dram=qfm_dram, sb=sb, dbg=dbg,
                   wexp_bf=wexp_bf, wcon_bf=wcon_bf, rp_sem=rp_sem)
        for L in range(2):
            _emit_layer(p, ctx, L,
                        x_in=(A["x0"] if L == 0 else x1_dram),
                        x_out=(x1_dram if L == 0 else out_x))

        with Sec(p, "sync") as s:
            s.w("dS", p.v("dS"))
            s.w("dG", p.v("dG"))
            s.w("dW", p.v("dW"))
            s.w("dC", p.v("dC"))

    es.close()
    return nc


def _emit_layer(p, ctx, L, x_in, x_out):
    sb = ctx["sb"]
    les = contextlib.ExitStack()
    xnT = sb("xnT%d" % L, (128, NCH, SC), BF16, les)

    if _ph_on():
        _p1_ln(p, ctx, L, x_in, xnT)
    p.barrier()
    if _ph_on():
        _p2_kv(p, ctx, L, xnT)
    if L == 1 and _ph_on():
        _p2_rope(p, ctx, xnT)
    if L == 0 and _ph_on():
        _p2_gate(p, ctx, xnT)
    if _ph_on():
        _p2_collective(p, ctx)
        ctx["cc_on"] = True
    else:
        ctx["cc_on"] = False
    if L == 0 and _ph_on():
        _mbuild(p, ctx)
    pcb = (lambda: _p3_partner(p, ctx)) if ctx["cc_on"] else None
    if _ph_on():
        if L == 0:
            _p3_moe(p, ctx, xnT, pcb)
        else:
            _p3_ffn(p, ctx, xnT, pcb)
    elif pcb:
        pcb()
    p.barrier()
    pes = contextlib.ExitStack()
    attnout = sb("attnout%d" % L, (128, NCH, SC), BF16, pes)
    if ctx["cc_on"] and _ph_on():
        _p4_attn(p, ctx, L, xnT, attnout, pes)
    p.barrier()
    if _ph_on():
        _p5_residual(p, ctx, L, attnout, x_in, x_out)
    p.barrier()
    pes.close()
    les.close()



def _p1_ln(p, ctx, L, x_in, xnT):
    A, G, sb = ctx["A"], ctx["G"], ctx["sb"]
    st = G["stats"]
    SUM, SQ, MEAN, VAR, SD, RSTD, NM = 0, 16, 32, 48, 64, 80, 96
    ph = contextlib.ExitStack()
    xb = [sb("p1x%d_%d" % (L, i), (128, D), F32, ph) for i in range(3)]
    xnb = [sb("p1n%d_%d" % (L, i), (128, D), BF16, ph) for i in range(2)]

    ve_done = [0] * NT
    sc_done = [0] * NT
    with Sec(p, "sync") as s, Sec(p, "vector") as v, Sec(p, "scalar") as c:
        for t in range(NT):
            if t >= 3:
                s.w("ve", ve_done[t - 3])
                s.w("sc", sc_done[t - 3])
            dt = s.dma(xb[t % 3][:], x_in[t * 128:(t + 1) * 128, :])
            v.w("dS", dt)
            ve_done[t] = v.o(
                lambda e, b=xb[t % 3], tt=t:
                e.reduce_sum(st[:, SUM + tt:SUM + tt + 1], b[:], axis=AX.X),
                inc="ve")
            c.w("dS", dt)
            if t >= 1:
                c.w("sc", sc_done[t - 1])
            sc_done[t] = c.o(
                lambda e, b=xb[t % 3], tt=t:
                e.activation(G["scr_bf"][:], b[:], AF.Square,
                             accum_out=st[:, SQ + tt:SQ + tt + 1]), inc="sc")
    with Sec(p, "vector") as v:
        v.w("ve", p.v("ve"))
        v.w("sc", p.v("sc"))
        v.o(lambda e: e.tensor_scalar(st[:, MEAN:MEAN + 16],
                                      st[:, SUM:SUM + 16], 1.0 / D, None,
                                      ALU.mult))
        v.o(lambda e: e.tensor_scalar(st[:, VAR:VAR + 16],
                                      st[:, SQ:SQ + 16], 1.0 / D, None,
                                      ALU.mult))
        v.drain()
        v.o(lambda e: e.tensor_tensor(st[:, SD:SD + 16], st[:, MEAN:MEAN + 16],
                                      st[:, MEAN:MEAN + 16], ALU.mult))
        v.drain()
        v.o(lambda e: e.tensor_tensor(st[:, VAR:VAR + 16], st[:, VAR:VAR + 16],
                                      st[:, SD:SD + 16], ALU.subtract))
        v.drain()
        veq = v.o(lambda e: e.tensor_scalar(st[:, VAR:VAR + 16],
                                            st[:, VAR:VAR + 16], EPS, None,
                                            ALU.add), inc="ve")
    with Sec(p, "scalar") as c:
        c.w("ve", veq)
        scq = c.o(lambda e: e.activation(st[:, SD:SD + 16],
                                         st[:, VAR:VAR + 16], AF.Sqrt),
                  inc="sc")
    with Sec(p, "vector") as v:
        v.w("sc", scq)
        v.o(lambda e: e.reciprocal(st[:, RSTD:RSTD + 16], st[:, SD:SD + 16]))
        v.drain()
        v.o(lambda e: e.tensor_tensor(st[:, NM:NM + 16], st[:, MEAN:MEAN + 16],
                                      st[:, RSTD:RSTD + 16], ALU.mult))
        v.drain()
        stat_q = v.o(lambda e: e.tensor_scalar(st[:, NM:NM + 16],
                                               st[:, NM:NM + 16], -1.0, None,
                                               ALU.mult), inc="ve")

    psv = [G["psA"].ap().bitcast(BF16), G["psB"].ap().bitcast(BF16)]
    tr_done = [0] * NT
    cp_done = [0] * NT
    sc2_done = [0] * NT
    with Sec(p, "sync") as s, Sec(p, "scalar") as c, Sec(p, "tensor") as t_, \
            Sec(p, "vector") as v:
        for t in range(NT):
            if t >= 3:
                s.w("sc", sc2_done[t - 3])
            else:
                s.w("ve", stat_q)
            dt = s.dma(xb[t % 3][:], x_in[t * 128:(t + 1) * 128, :])
            c.w("dS", dt)
            c.w("ve", stat_q)
            if t >= 2:
                c.w("te", tr_done[t - 2])
            sc2_done[t] = c.o(
                lambda e, b=xb[t % 3], n=xnb[t % 2], tt=t:
                e.activation(n[:], b[:], AF.Identity,
                             scale=st[:, RSTD + tt:RSTD + tt + 1],
                             bias=st[:, NM + tt:NM + tt + 1]), inc="sc")
            t_.w("sc", sc2_done[t])
            if t >= 2:
                t_.w("ve", cp_done[t - 2])
            for cc in range(NCH):
                r = t_.o(
                    lambda e, n=xnb[t % 2], c2=cc, pv=psv[t % 2]:
                    e.transpose(pv[:, c2 * 128:(c2 + 1) * 128],
                                n[:, c2 * 128:(c2 + 1) * 128], G["eye_t"][:]),
                    inc="te" if cc == NCH - 1 else None)
                if r:
                    tr_done[t] = r
            v.w("te", tr_done[t])
            cp_done[t] = v.o(
                lambda e, pv=psv[t % 2], tt=t: e.tensor_copy(
                    xnT[:, :, tt * 128:(tt + 1) * 128],
                    pv[:].rearrange("p (c s) -> p c s", s=128)), inc="ve")
            if "xn" in ctx["dbg"] and L == 0:
                s.w("sc", sc2_done[t])
                s.dma(ctx["dbg"]["xn"][t * 128:(t + 1) * 128, :],
                      xnb[t % 2][:])
    p.barrier()
    ph.close()



def _p2_kv(p, ctx, L, xnT):
    A, G, sb = ctx["A"], ctx["G"], ctx["sb"]
    kviK, kvi = ctx["kviK"], ctx["kvi"]
    ph = contextlib.ExitStack()
    wvb = sb("p2wv%d" % L, (128, NCH, D), BF16, ph)
    vstg = [sb("p2vs%d_%d" % (L, i), (128, D), BF16, ph) for i in range(2)]
    ones_row = sb("p2or%d" % L, (1, SC), BF16, ph)
    G["ones_row"] = ones_row
    if L == 0:
        G["sq_bf"] = sb("p2sq%d" % L, (64, SK), BF16, ph)
    with Sec(p, "vector") as v:
        vq_or = v.o(lambda e: e.memset(ones_row[:], 1.0), inc="ve")

    if L == 0:
        wkb = sb("p2wk", (128, NCH, D), BF16, ph)
        bkt = sb("p2bk", (128, NCH), F32, ph)
        bvt = sb("p2bvr", (128, D), F32, ph)
        kstg = [sb("p2ks%d" % i, (128, SC), BF16, ph) for i in range(2)]
        with Sec(p, "gpsimd") as g, Sec(p, "sync") as s:
            p.fence(g)
            wk_t = g.dma(wkb[:],
                         A["wkT"].rearrange("(c p) d -> p c d", p=128), "dW")
            wv_t = g.dma(wvb[:],
                         A["wvT"].rearrange("(c p) d -> p c d", p=128), "dW")
            s.dma(bkt[:], A["bk_r"])
            bv_t = s.dma(bvt[:], A["bv_rep"])
        with Sec(p, "scalar") as c:
            c.w("dS", bv_t)
            bk8 = c.o(lambda e: e.activation(bkt[:], bkt[:], AF.Identity,
                                             scale=RSQ), inc="sc")
        km_last = 0
        with Sec(p, "tensor") as t_, Sec(p, "scalar") as c, \
                Sec(p, "sync") as s, Sec(p, "vector") as v:
            t_.w("dW", wk_t)
            stg_free = [0, 0]
            evac = {}
            first = True
            for pr in range(NH // 2):
                sl = pr % 2
                if pr >= 2:
                    t_.w("dS", stg_free[sl])
                for s4 in range(4):
                    psl = s4 % 2
                    psx = [G["psA"], G["psB"]][psl]
                    t_.w("sc", evac.get((pr * 4 + s4) - 2, 0))
                    for dc in range(NCH):
                        t_.o(lambda e, ps=psx, d=dc, pp=pr, ss=s4:
                             e.matmul(ps[:],
                                      wkb[:, d, pp * 128:(pp + 1) * 128],
                                      xnT[:, d, ss * 512:(ss + 1) * 512],
                                      start=(d == 0), stop=(d == NCH - 1)),
                             inc="te" if dc == NCH - 1 else None)
                    c.w("te", p.v("te"))
                    c.w("sc", bk8)
                    evac[pr * 4 + s4] = c.o(
                        lambda e, ps=psx, pp=pr, ss=s4, k=kstg[sl]:
                        e.activation(k[:, ss * 512:(ss + 1) * 512], ps[:],
                                     AF.Identity, scale=RSQ,
                                     bias=bkt[:, pp:pp + 1]), inc="sc")
                s.w("sc", evac[pr * 4 + 3])
                s.dma(kviK[65 * 2 * pr:65 * 2 * pr + 64, :],
                      kstg[sl][0:64, :])
                stg_free[sl] = s.dma(
                    kviK[65 * (2 * pr + 1):65 * (2 * pr + 1) + 64, :],
                    kstg[sl][64:128, :])
                km_last = _k2_pair(p, ctx, t_, c, v, kstg[sl],
                                   evac[pr * 4 + 3], first)
                first = False
        _k2_finish(p, ctx, km_last)
    else:
        with Sec(p, "gpsimd") as g:
            p.fence(g)
            wv_t = g.dma(wvb[:],
                         A["vw"].rearrange("(c p) d -> p c d", p=128), "dW")

    with Sec(p, "sync") as s:
        s.w("ve", vq_or)
        for h in range(NH):
            s.dma(kviK[65 * h + 64:65 * h + 65, :], G["ones_row"][:])

    with Sec(p, "tensor") as t_, Sec(p, "vector") as v, \
            Sec(p, "scalar") as c, Sec(p, "sync") as s:
        t_.w("dW", wv_t)
        if L == 0:
            t_.w("dS", bv_t)
        vfree = [0, 0]
        vev = [0, 0]
        for t in range(NT):
            sl = t % 2
            psb = G["psQ"]
            if t >= 2:
                t_.w("dS", vfree[sl])
                t_.w("ve" if L == 0 else "sc", vev[sl])
            for dc in range(NCH):
                for n in range(2):
                    t_.o(lambda e, d=dc, tt=t, nn=n, hh=sl:
                         e.matmul(psb[:, hh * 1024 + nn * 512:
                                       hh * 1024 + (nn + 1) * 512],
                                  xnT[:, d, tt * 128:(tt + 1) * 128],
                                  wvb[:, d, nn * 512:(nn + 1) * 512],
                                  start=(d == 0), stop=(d == NCH - 1)),
                         inc="te" if (dc == NCH - 1 and n == 1) else None)
            if L == 0:
                v.w("te", p.v("te"))
                vev[sl] = v.o(lambda e, o=vstg[sl], hh=sl: e.tensor_tensor(
                    o[:], psb[:, hh * 1024:hh * 1024 + 1024], bvt[:],
                    ALU.add), inc="ve")
                s.w("ve", vev[sl])
            else:
                c.w("te", p.v("te"))
                vev[sl] = c.o(lambda e, o=vstg[sl], hh=sl: e.activation(
                    o[:], psb[:, hh * 1024:hh * 1024 + 1024], AF.Identity),
                    inc="sc")
                s.w("sc", vev[sl])
            vfree[sl] = s.dma(
                kvi[VBASE + t * 128:VBASE + (t + 1) * 128, :], vstg[sl][:])
    ph.close()


def _k2_pair(p, ctx, t_, c, v, ktile, ready, first):
    G = ctx["G"]
    sq = G["sq_bf"]
    c.w("sc", ready)
    c.w("te", p.v("te"))
    c.o(lambda e: e.activation(sq[0:64, 0:SC], ktile[0:64, :], AF.Square),
        inc="sc")
    sq2 = c.o(lambda e: e.activation(sq[0:64, SC:2 * SC], ktile[64:128, :],
                                     AF.Square), inc="sc")
    last = 0
    for i in range(8):
        t_.w("sc", sq2)
        t_.w("ve", last or p.v("ve"))
        tq = t_.o(lambda e, ii=i: e.matmul(
            G["psD"][0:1, :], G["ones64"][:],
            sq[0:64, ii * 512:(ii + 1) * 512], start=True, stop=True),
            inc="te")
        v.w("te", tq)
        v.o(lambda e: e.reduce_max(G["krow"][0:1, 0:1], G["psD"][0:1, :],
                                   axis=AX.X))
        v.drain()
        if first and i == 0:
            v.o(lambda e: e.tensor_copy(G["kmax"][0:1, 0:1],
                                        G["krow"][0:1, 0:1]))
        else:
            v.o(lambda e: e.tensor_tensor(G["kmax"][0:1, 0:1],
                                          G["kmax"][0:1, 0:1],
                                          G["krow"][0:1, 0:1], ALU.max))
        v.drain()
        last = v.o(lambda e: e.drain(), inc="ve")
    return last


def _k2_finish(p, ctx, last):
    G, kvi = ctx["G"], ctx["kvi"]
    with Sec(p, "vector") as v, Sec(p, "sync") as s:
        v.w("ve", last)
        vq = v.o(lambda e: e.tensor_scalar(G["kmax_bf"][0:1, 0:1],
                                           G["kmax"][0:1, 0:1], 1.02, None,
                                           ALU.mult), inc="ve")
        s.w("ve", vq)
        s.dma(kvi[STATROW:STATROW + 1, 0:1], G["kmax_bf"][0:1, 0:1])



def _p2_rope(p, ctx, xnT):
    A, G, sb = ctx["A"], ctx["G"], ctx["sb"]
    kviK, kvi, qfm_dram = ctx["kviK"], ctx["kvi"], ctx["qfm_dram"]
    ph = contextlib.ExitStack()
    G["sq_bf"] = sb("r_sq", (64, SK), BF16, ph)
    qwb = sb("r_qw", (128, NCH, D), BF16, ph)
    kwb = sb("r_kw", (128, NCH, D), BF16, ph)
    fmb = sb("r_fm", (128, NCH, SC), BF16, ph)
    tb = [sb("r_tb%d" % i, (128, 1024), F32, ph) for i in range(2)]
    tmp = [sb("r_tmp%d" % i, (128, 512), F32, ph) for i in range(4)]
    rtm = [sb("r_tm%d" % i, (128, D), BF16, ph) for i in range(2)]

    with Sec(p, "gpsimd") as g:
        p.fence(g)
        qw_t = g.dma(qwb[:],
                     A["qw"].rearrange("(c p) d -> p c d", p=128), "dW")
        kw_t = g.dma(kwb[:],
                     A["kw"].rearrange("(c p) d -> p c d", p=128), "dW")

    qstg_done = 0
    for which in range(2):
        wb = qwb if which == 0 else kwb
        wt = qw_t if which == 0 else kw_t
        rope_q = [0, 0]
        cp_q = [0, 0]
        for t in range(NT):
            sl = t % 2
            with Sec(p, "sync") as s:
                if t >= 2:
                    s.w("ve", rope_q[sl])
                s.dma(tb[sl][:, 0:512], A["cos8"][t * 128:(t + 1) * 128, :])
                tbl_t = s.dma(tb[sl][:, 512:1024],
                              A["sin8"][t * 128:(t + 1) * 128, :])
            with Sec(p, "tensor") as t_:
                t_.w("dW", wt)
                if t + which > 0:
                    t_.w("ve", p.v("ve"))
                for dc in range(NCH):
                    for n in range(2):
                        t_.o(lambda e, d=dc, tt=t, nn=n, w2=wb:
                             e.matmul(G["psQ"][:, nn * 512:(nn + 1) * 512],
                                      xnT[:, d, tt * 128:(tt + 1) * 128],
                                      w2[:, d, nn * 512:(nn + 1) * 512],
                                      start=(d == 0), stop=(d == NCH - 1)),
                             inc="te" if (dc == NCH - 1 and n == 1) else None)
            with Sec(p, "vector") as v:
                v.w("te", p.v("te"))
                v.w("dS", tbl_t)
                if t >= 2:
                    v.w("te", cp_q[sl])
                pv = G["psQ"].ap()[:, 0:1024].rearrange(
                    "p (q two) -> p q two", two=2)
                ro = rtm[sl][:].rearrange("p (q two) -> p q two", two=2)
                ct, st_ = tb[sl][:, 0:512], tb[sl][:, 512:1024]
                v.o(lambda e, px=pv, cc=ct: e.tensor_tensor(
                    tmp[0][:], px[:, :, 0], cc, ALU.mult))
                v.o(lambda e, px=pv, cc=st_: e.tensor_tensor(
                    tmp[1][:], px[:, :, 1], cc, ALU.mult))
                v.o(lambda e, px=pv, cc=st_: e.tensor_tensor(
                    tmp[2][:], px[:, :, 0], cc, ALU.mult))
                v.o(lambda e, px=pv, cc=ct: e.tensor_tensor(
                    tmp[3][:], px[:, :, 1], cc, ALU.mult))
                v.drain()
                v.o(lambda e, o=ro: e.tensor_tensor(
                    o[:, :, 0], tmp[0][:], tmp[1][:], ALU.subtract))
                rope_q[sl] = v.o(lambda e, o=ro: e.tensor_tensor(
                    o[:, :, 1], tmp[2][:], tmp[3][:], ALU.add), inc="ve")
            with Sec(p, "tensor") as t_, Sec(p, "vector") as v:
                t_.w("ve", rope_q[sl])
                if which == 1 and t == 0:
                    v.w("dS", qstg_done)
                pbf = G["psC"].ap().bitcast(BF16)
                for c4 in range(2):
                    tq = 0
                    for cc in range(c4 * 4, c4 * 4 + 4):
                        r = t_.o(lambda e, c2=cc, s2=sl: e.transpose(
                            pbf[:, (c2 % 4) * 128:(c2 % 4 + 1) * 128],
                            rtm[s2][:, c2 * 128:(c2 + 1) * 128],
                            G["eye_t"][:]),
                            inc="te" if cc % 4 == 3 else None)
                        if r:
                            tq = r
                    v.w("te", tq)
                    cp_q[sl] = v.o(lambda e, b2=c4 * 4, tt=t: e.tensor_copy(
                        fmb[:, b2:b2 + 4, tt * 128:(tt + 1) * 128],
                        pbf[:, 0:512].rearrange("p (c s) -> p c s", s=128)),
                        inc="ve")
                    t_.w("ve", cp_q[sl])
        if which == 0:
            with Sec(p, "sync") as s:
                s.w("ve", p.v("ve"))
                for cc in range(NCH):
                    qstg_done = s.dma(qfm_dram[cc * 128:(cc + 1) * 128, :],
                                      fmb[:, cc, :])
        else:
            km_last = 0
            with Sec(p, "sync") as s, Sec(p, "tensor") as t_, \
                    Sec(p, "scalar") as c, Sec(p, "vector") as v:
                s.w("ve", p.v("ve"))
                vq = p.v("ve")
                for pr in range(NH // 2):
                    s.dma(kviK[65 * 2 * pr:65 * 2 * pr + 64, :],
                          fmb[0:64, pr, :])
                    s.dma(kviK[65 * (2 * pr + 1):65 * (2 * pr + 1) + 64, :],
                          fmb[64:128, pr, :])
                first = True
                for pr in range(NH // 2):
                    km_last = _k2_pair(p, ctx, t_, c, v, fmb[:, pr, :], vq,
                                       first)
                    first = False
            _k2_finish(p, ctx, km_last)
    ph.close()



def _p2_gate(p, ctx, xnT):
    A, G, sb = ctx["A"], ctx["G"], ctx["sb"]
    ph = contextlib.ExitStack()
    gwb = sb("p2gw", (128, NCH, 8), BF16, ph)
    ga, geq, m1 = G["gate_all"], G["gate_eq"], G["stats"]
    with Sec(p, "gpsimd") as g:
        p.fence(g)
        gw_t = g.dma(gwb[:],
                     A["gate_w"].rearrange("(c p) e -> p c e", p=128), "dW")
    with Sec(p, "tensor") as t_, Sec(p, "vector") as v:
        t_.w("dW", gw_t)
        for t in range(NT):
            if t >= 1:
                t_.w("ve", p.v("ve"))
            for dc in range(NCH):
                t_.o(lambda e, d=dc, tt=t: e.matmul(
                    G["psC"][:, 0:8], xnT[:, d, tt * 128:(tt + 1) * 128],
                    gwb[:, d, :], start=(d == 0), stop=(d == NCH - 1)),
                    inc="te" if dc == NCH - 1 else None)
            v.w("te", p.v("te"))
            v.o(lambda e, tt=t: e.tensor_tensor(
                ga[:, tt, :], G["psC"][:, 0:8], G["gateb_t"][:], ALU.add),
                inc="ve")
    with Sec(p, "vector") as v:
        v.w("ve", p.v("ve"))
        vq = v.o(lambda e: e.tensor_reduce(m1[:, 0:16], ga[:], AX.X, ALU.max,
                                           negate=True), inc="ve")
    with Sec(p, "scalar") as c:
        c.w("ve", vq)
        for t in range(NT):
            c.o(lambda e, tt=t: e.activation(ga[:, tt, :], ga[:, tt, :],
                                             AF.Exp, bias=m1[:, tt:tt + 1]),
                inc="sc" if t == NT - 1 else None)
    b16 = lambda ap: ap.unsqueeze(2).broadcast_to((128, 16, 8))
    with Sec(p, "vector") as v:
        v.w("sc", p.v("sc"))
        v.o(lambda e: e.tensor_reduce(m1[:, 16:32], ga[:], AX.X, ALU.add))
        v.drain()
        v.o(lambda e: e.reciprocal(m1[:, 32:48], m1[:, 16:32]))
        v.drain()
        v.o(lambda e: e.tensor_tensor(ga[:], ga[:], b16(m1[:, 32:48]),
                                      ALU.mult))
        v.drain()
        v.o(lambda e: e.tensor_reduce(m1[:, 0:16], ga[:], AX.X, ALU.max))
        v.drain()
        v.o(lambda e: e.tensor_tensor(geq[:], ga[:], b16(m1[:, 0:16]),
                                      ALU.is_equal))
        v.drain()
        v.o(lambda e: e.scalar_tensor_tensor(geq[:], geq[:], -1e30, ga[:],
                                             ALU.mult, ALU.add))
        v.drain()
        v.o(lambda e: e.tensor_reduce(m1[:, 16:32], geq[:], AX.X, ALU.max))
        v.drain()
        v.o(lambda e: e.tensor_tensor(geq[:], ga[:], b16(m1[:, 16:32]),
                                      ALU.is_ge))
        v.drain()
        v.o(lambda e: e.tensor_tensor(ga[:], ga[:], geq[:], ALU.mult))
        v.drain()
        vq = v.o(lambda e: e.tensor_copy(
            G["scr_bf"][:, 0:128], ga[:].rearrange("p t e -> p (t e)")),
            inc="ve")
    with Sec(p, "tensor") as t_, Sec(p, "vector") as v:
        t_.w("ve", vq)
        pbf = [G["psC"].ap().bitcast(BF16), G["psD"].ap().bitcast(BF16)]
        cpq = [0, 0]
        for t in range(NT):
            sl = t % 2
            if t >= 2:
                t_.w("ve", cpq[sl])
            tq = t_.o(lambda e, tt=t, pb=pbf[sl]: e.transpose(
                pb[0:8, 0:128], G["scr_bf"][:, tt * 8:tt * 8 + 8],
                G["eye_t"][:]), inc="te")
            v.w("te", tq)
            cpq[sl] = v.o(lambda e, tt=t, pb=pbf[sl]: e.tensor_copy(
                G["wT_sb"][0:8, tt * 128:(tt + 1) * 128],
                pb[0:8, 0:128]),
                inc="ve")
    with Sec(p, "sync") as s:
        if "gate" in ctx["dbg"]:
            s.w("ve", p.v("ve"))
            s.dma(ctx["dbg"]["gate"], ga[:].rearrange("p t e -> p (t e)"))
    ph.close()



def _mbuild(p, ctx):
    A, G, sb = ctx["A"], ctx["G"], ctx["sb"]
    ph = contextlib.ExitStack()
    wd = sb("mb_wd", (128, 12, 32), BF16, ph)
    wo = sb("mb_wo", (128, 12, 32), BF16, ph)
    wox = sb("mb_wox", (128, 12, 32), BF16, ph)
    mxr = sb("mb_mx", (128, 7, 128), BF16, ph)
    Sd = sb("mb_S", (128, 32, 128), BF16, ph)
    t2 = sb("mb_t2", (128, 32, 128), BF16, ph)
    Mc = [sb("mb_M%d" % i, (128, 32, 128), BF16, ph) for i in range(2)]

    with Sec(p, "gpsimd") as g0:
        p.fence(g0)
        mx_t = g0.dma(mxr[:], A["maskxor"].rearrange("s p n -> p s n"), "dG")

    b32 = lambda ap: ap.unsqueeze(1).broadcast_to((128, 32, 128))
    bj = lambda ap: ap.unsqueeze(2).broadcast_to((128, 32, 128))
    flat = lambda ap: ap.rearrange("p c n -> p (c n)")

    for pre, Mout in (("up", G["Mup"]), ("dn", G["Mdn"])):
        with Sec(p, "gpsimd") as g:
            g.w("ve", p.v("ve"))
            g.dma(wd[:], A[pre + "_wd"].rearrange("s p c -> p s c"), "dG")
            g.dma(wo[:], A[pre + "_wo"].rearrange("s p c -> p s c"), "dG")
            w_t = g.dma(wox[:], A[pre + "_wox"].rearrange("s p c -> p s c"),
                        "dG")

        def s_dense(v, stage, transposed):
            v.o(lambda e, st=stage: e.tensor_tensor(
                Sd[:], b32(G["eye_t"][:]), bj(wd[:, st, :]), ALU.mult))
            src = wox if transposed else wo
            v.o(lambda e, st=stage, s2=src: e.tensor_tensor(
                t2[:], b32(mxr[:, st, :]), bj(s2[:, st, :]), ALU.mult))
            v.drain()
            return v.o(lambda e: e.tensor_tensor(flat(Sd[:]), flat(Sd[:]),
                                                 flat(t2[:]), ALU.add),
                       inc="ve")

        with Sec(p, "vector") as v:
            v.w("dG", w_t)
            v.w("dG", mx_t)
            s_dense(v, 0, False)
            v.drain()
            v.o(lambda e: e.tensor_copy(flat(Mc[0][:]), flat(Sd[:])),
                inc="ve")
        cur = 0
        for stage in range(1, 7):
            with Sec(p, "vector") as v:
                v.w("ve", p.v("ve"))
                v.w("te", p.v("te"))
                sdq = s_dense(v, stage, True)
            with Sec(p, "tensor") as t_, Sec(p, "vector") as v:
                t_.w("ve", sdq)
                for c4 in range(8):
                    if c4 >= 2:
                        t_.w("ve", p.v("ve"))
                    tq = 0
                    for cc in range(c4 * 4, c4 * 4 + 4):
                        psx = [G["psC"], G["psD"]][c4 % 2]
                        r = t_.o(lambda e, c2=cc, cu=cur, ps=psx: e.matmul(
                            ps[:, (c2 % 4) * 128:(c2 % 4 + 1) * 128],
                            Sd[:, c2, :], Mc[cu][:, c2, :],
                            start=True, stop=True),
                            inc="te" if cc % 4 == 3 else None)
                        if r:
                            tq = r
                    v.w("te", tq)
                    v.o(lambda e, b2=c4 * 4, cu=cur, ps=[G["psC"],
                                                         G["psD"]][c4 % 2]:
                        e.tensor_copy(
                            flat(Mc[1 - cu][:, b2:b2 + 4, :]), ps[:]),
                        inc="ve")
            cur = 1 - cur
        with Sec(p, "tensor") as t_, Sec(p, "vector") as v:
            t_.w("ve", p.v("ve"))
            pbf = [G["psC"].ap().bitcast(BF16), G["psD"].ap().bitcast(BF16)]
            for c4 in range(8):
                if c4 >= 2:
                    t_.w("ve", p.v("ve"))
                tq = 0
                for cc in range(c4 * 4, c4 * 4 + 4):
                    r = t_.o(lambda e, c2=cc, cu=cur, pb=pbf[c4 % 2]:
                             e.transpose(
                                 pb[:, (c2 % 4) * 128:(c2 % 4 + 1) * 128],
                                 Mc[cu][:, c2, :], G["eye_t"][:]),
                             inc="te" if cc % 4 == 3 else None)
                    if r:
                        tq = r
                v.w("te", tq)
                v.o(lambda e, b2=c4 * 4, pb=pbf[c4 % 2]: e.tensor_copy(
                    Mout[:, b2 * 128:(b2 + 4) * 128], pb[:, 0:512]),
                    inc="ve")
    ph.close()


def _p2_collective(p, ctx):
    with Sec(p, "gpsimd") as g:
        g.w("dS", p.v("dS"))
        g.w("dG", p.v("dG"))
        g.o(lambda e: e.collective_compute(
            "AllGather", ALU.bypass,
            replica_groups=[[0, 1], [2, 3], [4, 5], [6, 7]],
            ins=[ctx["kv_in_t"].ap().opt()],
            outs=[ctx["kv_ga_t"].ap().opt()]), inc="cc")


def _p3_partner(p, ctx):
    G, sb = ctx["G"], ctx["sb"]
    kvga, kvp = ctx["kvgaflat"], ctx["kvp"]
    ph = contextlib.ExitStack()
    NB = 4
    SKEW = 3
    buf = [sb("pr_buf%d" % i, (128, 1024), BF16, ph) for i in range(NB)]
    NR = KVROWS // 128
    with Sec(p, "gpsimd") as g:
        g.w("cc", p.v("cc"))
        gq = {}
        done = {}
        for r in range(min(SKEW, NR)):
            gq[r] = g.o(lambda e, rr=r, b=buf[r % NB]: e.indirect_dma_start(
                out=b[:], out_offset=None, in_=kvga,
                in_offset=IndirectOffsetOnAxis(
                    ap=G["pidx_t"][:, rr:rr + 1], axis=0)), inc="dG", n=16)
        for r in range(NR):
            g.w("dG", gq[r])
            done[r] = g.dma(kvp[r * 128:(r + 1) * 128, :], buf[r % NB][:],
                            "dG")
            nr = r + SKEW
            if nr < NR:
                if nr >= NB:
                    g.w("dG", done[nr - NB])
                gq[nr] = g.o(lambda e, rr=nr, b=buf[nr % NB]:
                             e.indirect_dma_start(
                                 out=b[:], out_offset=None, in_=kvga,
                                 in_offset=IndirectOffsetOnAxis(
                                     ap=G["pidx_t"][:, rr:rr + 1], axis=0)),
                             inc="dG", n=16)
    ph.close()



class _Stream:

    def __init__(self, p, bufs, sched, sem="dW"):
        self.p = p
        self.bufs = bufs
        self.sched = sched
        self.sem = sem
        self.load_t = {}
        self.use_t = {}
        self.next_load = 0
        self.next_use = 0

    def _load(self, g):
        q = self.next_load
        if q >= len(self.sched):
            return
        self.next_load += 1
        if q >= 2:
            g.w("te", self.use_t[q - 2])
        self.load_t[q] = g.dma(self.bufs[q % 2][:], self.sched[q], self.sem)

    def prime(self, g):
        self._load(g)
        self._load(g)

    def acquire(self, t_):
        q = self.next_use
        t_.w(self.sem, self.load_t[q])
        return q, self.bufs[q % 2]

    def release(self, g):
        q = self.next_use
        self.next_use += 1
        self.use_t[q] = self.p.v("te")
        self._load(g)


def _p3_moe(p, ctx, xnT, partner_cb=None):
    A, G, sb = ctx["A"], ctx["G"], ctx["sb"]
    moe_dram = ctx["moe_dram"]
    ph = contextlib.ExitStack()
    w1b = [sb("m_w1%d" % i, (128, NCH, 512), BF16, ph) for i in range(2)]
    w2b = [sb("m_w2%d" % i, (128, 16, 256), BF16, ph) for i in range(2)]
    w3b = [sb("m_w3%d" % i, (128, 16, 512), BF16, ph) for i in range(2)]
    b1t = [sb("m_b1%d" % i, (128, 16), F32, ph) for i in range(2)]
    b2t = [sb("m_b2%d" % i, (128, 16), F32, ph) for i in range(2)]
    h1g = sb("m_h1", (128, 16, 1024), BF16, ph)
    h2g = sb("m_h2", (128, 16, 1024), BF16, ph)
    w3s = [sb("m_w3s%d" % i, (128, 1024), F32, ph) for i in range(2)]
    ga = G["gate_all"]
    psQ = G["psQ"]

    rr = lambda ap: ap.rearrange("(c p) h -> p c h", p=128)
    s1 = _Stream(p, w1b, [rr(A["w1"][e][:, i * 512:(i + 1) * 512])
                          for e in range(NE) for th in range(2)
                          for i in range(4)])
    s2 = _Stream(p, w2b, [rr(A["w2"][e][:, i * 256:(i + 1) * 256])
                          for e in range(NE) for th in range(2)
                          for i in range(8)])
    s3 = _Stream(p, w3b, [rr(A["w3"][e][:, i * 512:(i + 1) * 512])
                          for e in range(NE) for i in range(2)])
    b_t, b_use = {}, {}

    def load_b(g, e):
        g.w("sc", b_use.get(e - 2, 0))
        g.dma(b1t[e % 2][:], A["b1_r"][e], "dG")
        b_t[e] = g.dma(b2t[e % 2][:], A["b2_r"][e], "dG")

    with Sec(p, "gpsimd") as g:
        p.fence(g)
        load_b(g, 0)
        s1.prime(g)
        s2.prime(g)
        s3.prime(g)

    w3acc = [0, 0]
    h1ev = [0, 0]
    h2ev = [0, 0]
    w3ev = [0, 0]
    for e in range(NE):
        with Sec(p, "tensor") as t_, Sec(p, "scalar") as c, \
                Sec(p, "gpsimd") as g:
            if e + 1 < NE:
                load_b(g, e + 1)
            q3, _ = s3.acquire(t_)
            t_.w("dW", s3.load_t[q3 + 1])
            for th in range(2):
                for hw in range(4):
                    _, w1cur = s1.acquire(t_)
                    for hcs in range(4):
                        hc = hw * 4 + hcs
                        half = hc % 2
                        t_.w("sc", h1ev[half])
                        for dc in range(NCH):
                            for n in range(2):
                                t_.o(lambda e_, d=dc, nn=n, hh=half,
                                     wb=w1cur, h4=hcs, t2=th:
                                     e_.matmul(
                                         psQ[:, hh * 1024 + nn * 512:
                                             hh * 1024 + (nn + 1) * 512],
                                         wb[:, d, h4 * 128:(h4 + 1) * 128],
                                         xnT[:, d,
                                             t2 * 1024 + nn * 512:
                                             t2 * 1024 + (nn + 1) * 512],
                                         start=(d == 0),
                                         stop=(d == NCH - 1)),
                                     inc="te" if (dc == NCH - 1 and n == 1)
                                     else None)
                        c.w("te", p.v("te"))
                        c.w("dG", b_t[e])
                        h1ev[half] = c.o(
                            lambda e_, hh=half, hc2=hc, sl=e % 2:
                            e_.activation(
                                h1g[:, hc2, :],
                                psQ[:, hh * 1024:hh * 1024 + 1024],
                                AF.Gelu, bias=b1t[sl][:, hc2:hc2 + 1]),
                            inc="sc")
                    s1.release(g)
                b_use[e] = p.v("sc")
                for kw_ in range(8):
                    _, w2cur = s2.acquire(t_)
                    for k2 in range(2):
                        kc = kw_ * 2 + k2
                        half = kc % 2
                        t_.w("sc", h2ev[half])
                        if kc == 0:
                            t_.w("sc", h1ev[0])
                            t_.w("sc", h1ev[1])
                        for hc in range(16):
                            for n in range(2):
                                t_.o(lambda e_, h2=hc, nn=n, hh=half,
                                     wb=w2cur, kk=k2:
                                     e_.matmul(
                                         psQ[:, hh * 1024 + nn * 512:
                                             hh * 1024 + (nn + 1) * 512],
                                         wb[:, h2, kk * 128:(kk + 1) * 128],
                                         h1g[:, h2,
                                             nn * 512:(nn + 1) * 512],
                                         start=(h2 == 0), stop=(h2 == 15)),
                                     inc="te" if (hc == 15 and n == 1)
                                     else None)
                        c.w("te", p.v("te"))
                        h2ev[half] = c.o(
                            lambda e_, hh=half, k3=kc, sl=e % 2:
                            e_.activation(
                                h2g[:, k3, :],
                                psQ[:, hh * 1024:hh * 1024 + 1024],
                                AF.Gelu, bias=b2t[sl][:, k3:k3 + 1]),
                            inc="sc")
                    s2.release(g)
                for tt in range(8):
                    tg = th * 8 + tt
                    sl = tt % 2
                    if tt == 0:
                        t_.w("sc", h2ev[0])
                        t_.w("sc", h2ev[1])
                    for fh in range(2):
                        psx = [G["psA"], G["psB"]][fh]
                        wb3 = w3b[(s3.next_use + fh) % 2]
                        t_.w("sc", w3ev[fh])
                        for kc in range(16):
                            t_.o(lambda e_, k3=kc, ps=psx, wb=wb3, t2=tt:
                                 e_.matmul(ps[:],
                                           h2g[:, k3,
                                               t2 * 128:(t2 + 1) * 128],
                                           wb[:, k3, :],
                                           start=(k3 == 0), stop=(k3 == 15)),
                                 inc="te" if kc == 15 else None)
                        c.w("te", p.v("te"))
                        if w3acc[sl]:
                            c.w("dG", w3acc[sl])
                        w3ev[fh] = c.o(
                            lambda e_, f2=fh, s2=sl, tg2=tg, ee=e, ps=psx:
                            e_.activation(
                                w3s[s2][:, f2 * 512:(f2 + 1) * 512], ps[:],
                                AF.Identity, scale=ga[:, tg2, ee:ee + 1]),
                            inc="sc")
                    g.w("sc", w3ev[1])
                    w3acc[sl] = g.o(
                        lambda e_, s2=sl, tg2=tg, ee=e: e_.dma_start(
                            out=moe_dram[tg2 * 128:(tg2 + 1) * 128, :],
                            in_=w3s[s2][:],
                            accum_op=(ALU.bypass if ee == 0 else ALU.add)),
                        inc="dG", n=16)
            s3.release(g)
            s3.release(g)
        if e == 1 and partner_cb:
            partner_cb()
    ph.close()



def _p3_ffn(p, ctx, xnT, partner_cb=None):
    A, G, sb = ctx["A"], ctx["G"], ctx["sb"]
    moe_dram = ctx["moe_dram"]
    wexp_bf, wcon_bf = ctx["wexp_bf"], ctx["wcon_bf"]
    rp_sem = ctx["rp_sem"]
    ph = contextlib.ExitStack()
    TS = 256
    NS = SC // TS
    sl4 = [sb("f_sl%d" % i, (128, 32, TS), BF16, ph) for i in range(4)]
    wexpb = [sb("f_we%d" % i, (128, NCH, 512), BF16, ph) for i in range(2)]
    wconb = [sb("f_wc%d" % i, (128, 32, 256), BF16, ph) for i in range(2)]
    tmpb = [sb("f_tmp%d" % i, (128, TS), F32, ph) for i in range(4)]
    cstage = [sb("f_cs%d" % i, (128, D), F32, ph) for i in range(4)]
    bwd = sb("f_bwd", (128, 12, 32), F32, ph)
    bwo = sb("f_bwo", (128, 12, 32), F32, ph)
    dwd = sb("f_dwd", (128, 12, 32), F32, ph)
    dwo = sb("f_dwo", (128, 12, 32), F32, ph)
    bexp = sb("f_bexp", (128, 32), F32, ph)
    upb = sb("f_upb", (128, 32), F32, ph)

    with Sec(p, "sync") as s:
        p.fence(s)
        s.dma(bexp[:], A["bexp_r"])
        cb_t = s.dma(upb[:], A["upb_r"])
    with Sec(p, "sync") as s:
        s.dma(bwd[:], A["up_wd"].rearrange("s p c -> p s c"))
        s.dma(bwo[:], A["up_wo"].rearrange("s p c -> p s c"))
        s.dma(dwd[:], A["dn_wd"].rearrange("s p c -> p s c"))
        bw_t = s.dma(dwo[:], A["dn_wo"].rearrange("s p c -> p s c"))

    banks = [G["psA"], G["psB"], G["psC"], G["psD"]]
    we_t, wc_t = {}, {}
    we_use, wc_use = {}, {}
    weq = {}
    muq = {}
    mdq = {}
    wcq = {}
    mupd = {}
    gelq = {}
    stqfin = {}
    csfree = [0, 0, 0, 0]
    stq4 = [0, 0, 0, 0]

    def load_wexp(g, key):
        if key >= NS * 8:
            return
        g.w("te", we_use.get(key - 2, 0))
        we_t[key] = g.dma(
            wexpb[key % 2][:],
            wexp_bf[key % 8].rearrange("p (c h) -> p c h", c=NCH), "dC")

    def load_wcon(g, key):
        if key >= NS * 4:
            return
        g.w("te", wc_use.get(key - 2, 0))
        wc_t[key] = g.dma(
            wconb[key % 2][:],
            wcon_bf[key % 4].rearrange("p (c f) -> p c f", c=32), "dC")

    with Sec(p, "tensor") as t_, Sec(p, "scalar") as c, \
            Sec(p, "sync") as g:
        g.ops.append(lambda e: e.wait_ge(rp_sem, 192))
        load_wexp(g, 0)
        load_wexp(g, 1)
        load_wcon(g, 0)
        load_wcon(g, 1)
        c.w("dS", cb_t)
        c.w("dS", bw_t)
        for hc in range(32):
            wi = hc // 4
            key = wi
            bk = hc % 4
            t_.w("dC", we_t[key])
            t_.w("sc", weq.get((0, hc - 4), 0))
            for dc in range(NCH):
                t_.o(lambda e, d=dc, h2=hc % 4, k2=key % 2, b2=bk:
                     e.matmul(banks[b2][:, 0:TS],
                              wexpb[k2][:, d, h2 * 128:(h2 + 1) * 128],
                              xnT[:, d, 0:TS],
                              start=(d == 0), stop=(d == NCH - 1)),
                     inc="te" if dc == NCH - 1 else None)
            we_use[key] = p.v("te")
            if hc % 4 == 3:
                load_wexp(g, key + 2)
            c.w("te", p.v("te"))
            weq[(0, hc)] = c.o(lambda e, h2=hc, b2=bk: e.activation(
                sl4[0][:, h2, :], banks[b2][:, 0:TS], AF.Identity,
                bias=bexp[:, h2:h2 + 1]), inc="sc")
        for cc in range(32):
            bk = cc % 4
            t_.w("sc", weq[(0, min(28 + bk, 31))])
            t_.w("sc", muq.get((0, cc - 4), 0))
            t_.o(lambda e, c2=cc, b2=bk: e.matmul(
                banks[b2][:, 0:TS], G["Mup"][:, c2 * 128:(c2 + 1) * 128],
                sl4[0][:, c2, :], start=True, stop=True), inc="te")
            c.w("te", p.v("te"))
            muq[(0, cc)] = c.o(lambda e, c2=cc, b2=bk: e.activation(
                sl4[1][:, c2, :], banks[b2][:, 0:TS], AF.Identity),
                inc="sc")
        mupd[0] = p.v("sc")

    for w in range(NS):
        P_ = sl4[2 * (w % 2)]
        Q_ = sl4[2 * (w % 2) + 1]
        Pn = sl4[2 * ((w + 1) % 2)]
        Qn = sl4[2 * ((w + 1) % 2) + 1]
        has_wc = w >= 1
        has_we = w + 1 < NS
        n_wc = 8 if has_wc else 0
        n_we = 32 if has_we else 0
        te0 = p.v("te")
        te_wc = [te0 + g2 + 1 for g2 in range(n_wc)]
        te_md = [te0 + n_wc + c2 + 1 for c2 in range(32)]
        te_we = [te0 + n_wc + 32 + h2 + 1 for h2 in range(n_we)]
        te_mu = [te0 + n_wc + 32 + n_we + c2 + 1 for c2 in range(n_we)]

        if has_we:
            for wi in range(8):
                we_use[(w + 1) * 8 + wi] = te_we[4 * wi + 3]
        if has_wc:
            for fq in range(4):
                wc_use[(w - 1) * 4 + fq] = te_wc[fq * 2 + 1]

        with Sec(p, "tensor") as t_, Sec(p, "scalar") as c, \
                Sec(p, "vector") as v, Sec(p, "sync") as s, \
                Sec(p, "gpsimd") as g:
            if w >= 1:
                for j in range(4):
                    load_wcon(s, (w - 1) * 4 + 2 + j)
            wc_ev = []
            if has_wc:
                for g2 in range(8):
                    fq, tsub = g2 // 2, g2 % 2
                    tile = ((w - 1) % 2) * 2 + tsub
                    bk = 2 + (g2 % 2)

                    def _wcev(g3=g2, f3=fq, t3=tile, b3=bk, w3=w):
                        c.w("te", te_wc[g3])
                        if f3 == 0:
                            c.w("dS", csfree[t3])
                        wcq[(w3 - 1, g3)] = c.o(
                            lambda e, f4=f3, t4=t3, b4=b3: e.activation(
                                cstage[t4][:, f4 * 256:(f4 + 1) * 256],
                                banks[b4][:, 0:256], AF.Identity), inc="sc")
                    wc_ev.append(_wcev)
            we_ev = []
            if has_we:
                for hc in range(32):
                    bk = hc % 4

                    def _weev(h3=hc, b3=bk, w3=w):
                        c.w("te", te_we[h3])
                        if h3 == 0:
                            c.w("ve", stqfin.get(w3 - 1, 0))
                        weq[(w3 + 1, h3)] = c.o(
                            lambda e, h4=h3, b4=b3: e.activation(
                                Pn[:, h4, :], banks[b4][:, 0:TS],
                                AF.Identity,
                                bias=bexp[:, h4:h4 + 1]), inc="sc")
                    we_ev.append(_weev)

            opn = 0
            cur = [Q_, P_]
            for sg in range(5):
                k = 1 << sg
                stg = 7 + sg
                f_, d_ = cur
                for cc in range(32):
                    sl = cc % 4
                    c.w("ve", stq4[sl])
                    stq_sc = c.o(lambda e, c2=cc, s2=stg, f2=f_, sl2=sl:
                                 e.activation(tmpb[sl2][:], f2[:, c2, :],
                                              AF.Identity,
                                              scale=bwd[:, s2, c2:c2 + 1]),
                                 inc="sc")
                    v.w("sc", stq_sc)
                    if sg == 0 and cc == 0:
                        v.w("sc", mupd[w])
                    stq4[sl] = v.o(
                        lambda e, c2=cc, s2=stg, f2=f_, d2=d_, kk=k, sl2=sl:
                        e.scalar_tensor_tensor(
                            d2[:, c2, :], f2[:, c2 ^ kk, :],
                            bwo[:, s2, c2:c2 + 1], tmpb[sl2][:],
                            ALU.mult, ALU.add), inc="ve")
                    opn += 1
                    if opn % 24 == 0 and wc_ev:
                        wc_ev.pop(0)()
                cur = [d_, f_]
            c.w("ve", stq4[3])
            for cc in range(32):
                gelq[w] = c.o(lambda e, c2=cc: e.activation(
                    Q_[:, c2, :], P_[:, c2, :], AF.Gelu,
                    bias=upb[:, c2:c2 + 1]), inc="sc")
                opn += 1
                if opn % 24 == 0 and wc_ev:
                    wc_ev.pop(0)()
            while wc_ev:
                wc_ev.pop(0)()
            for cc in range(32):
                bk = cc % 4
                c.w("te", te_md[cc])
                mdq[(w, cc)] = c.o(lambda e, c2=cc, b2=bk: e.activation(
                    P_[:, c2, :], banks[b2][:, 0:TS], AF.Identity),
                    inc="sc")
            cur = [P_, Q_]
            for sg in range(5):
                k = 1 << sg
                stg = 7 + sg
                f_, d_ = cur
                for cc in range(32):
                    sl = cc % 4
                    c.w("ve", stq4[sl])
                    stq_sc = c.o(lambda e, c2=cc, s2=stg, f2=f_, sl2=sl:
                                 e.activation(tmpb[sl2][:], f2[:, c2, :],
                                              AF.Identity,
                                              scale=dwd[:, s2, c2:c2 + 1]),
                                 inc="sc")
                    v.w("sc", stq_sc)
                    if sg == 0 and cc == 0:
                        v.w("sc", mdq[(w, 31)])
                    stq4[sl] = v.o(
                        lambda e, c2=cc, s2=stg, f2=f_, d2=d_, kk=k, sl2=sl:
                        e.scalar_tensor_tensor(
                            d2[:, c2, :], f2[:, c2 ^ kk, :],
                            dwo[:, s2, c2:c2 + 1], tmpb[sl2][:],
                            ALU.mult, ALU.add), inc="ve")
                cur = [d_, f_]
            while we_ev:
                we_ev.pop(0)()
            stqfin[w] = p.v("ve")
            if has_we:
                for cc in range(32):
                    bk = cc % 4
                    c.w("te", te_mu[cc])
                    muq[(w + 1, cc)] = c.o(
                        lambda e, c2=cc, b2=bk: e.activation(
                            Qn[:, c2, :], banks[b2][:, 0:TS], AF.Identity),
                        inc="sc")
                mupd[w + 1] = p.v("sc")

            if has_wc:
                for tsub in range(2):
                    tile = ((w - 1) % 2) * 2 + tsub
                    s.w("sc", wcq[(w - 1, 6 + tsub)])
                    row = (w - 1) * TS + tsub * 128
                    csfree[tile] = s.dma(moe_dram[row:row + 128, :],
                                         cstage[tile][:])
            if has_we:
                for wi in range(8):
                    load_wexp(s, (w + 1) * 8 + wi + 2)
            Qp = sl4[2 * ((w - 1) % 2) + 1]
            if has_wc:
                t_.w("ve", stqfin[w - 1])
                for g2 in range(8):
                    fq, tsub = g2 // 2, g2 % 2
                    key = (w - 1) * 4 + fq
                    bk = 2 + (g2 % 2)
                    t_.w("dC", wc_t[key])
                    if g2 < 2:
                        t_.w("sc", muq[(w, 28 + bk)])
                    else:
                        t_.w("sc", wcq[(w - 1, g2 - 2)])
                    for hc in range(32):
                        t_.o(lambda e, h2=hc, k2=key % 2, b2=bk, t3=tsub:
                             e.matmul(banks[b2][:, 0:256],
                                      Qp[:, h2, t3 * 128:(t3 + 1) * 128],
                                      wconb[k2][:, h2, :],
                                      start=(h2 == 0), stop=(h2 == 31)),
                             inc="te" if hc == 31 else None)
                    assert p.v("te") == te_wc[g2]
            t_.w("sc", gelq[w])
            for cc in range(32):
                bk = cc % 4
                if cc < 4:
                    if has_wc and bk >= 2:
                        t_.w("sc", wcq[(w - 1, 6 + (bk - 2))])
                    else:
                        t_.w("sc", muq[(w, 28 + bk)])
                else:
                    t_.w("sc", mdq[(w, cc - 4)])
                t_.o(lambda e, c2=cc, b2=bk: e.matmul(
                    banks[b2][:, 0:TS], G["Mdn"][:, c2 * 128:(c2 + 1) * 128],
                    Q_[:, c2, :], start=True, stop=True), inc="te")
                assert p.v("te") == te_md[cc]
            if has_we:
                for hc in range(32):
                    wi = hc // 4
                    key = (w + 1) * 8 + wi
                    bk = hc % 4
                    t_.w("dC", we_t[key])
                    if hc < 4:
                        t_.w("sc", mdq[(w, 28 + bk)])
                    else:
                        t_.w("sc", weq[(w + 1, hc - 4)])
                    for dc in range(NCH):
                        t_.o(lambda e, d=dc, h2=hc % 4, k2=key % 2, b2=bk,
                             w3=w: e.matmul(
                                 banks[b2][:, 0:TS],
                                 wexpb[k2][:, d, h2 * 128:(h2 + 1) * 128],
                                 xnT[:, d, (w3 + 1) * TS:(w3 + 2) * TS],
                                 start=(d == 0), stop=(d == NCH - 1)),
                             inc="te" if dc == NCH - 1 else None)
                    assert p.v("te") == te_we[hc]
                for cc in range(32):
                    bk = cc % 4
                    t_.w("sc", weq[(w + 1, min(28 + bk, 31))])
                    t_.w("sc", weq[(w + 1, cc)])
                    t_.w("sc", muq.get((w + 1, cc - 4), 0))
                    t_.o(lambda e, c2=cc, b2=bk: e.matmul(
                        banks[b2][:, 0:TS],
                        G["Mup"][:, c2 * 128:(c2 + 1) * 128],
                        Pn[:, c2, :], start=True, stop=True), inc="te")
                    assert p.v("te") == te_mu[cc]

        if w == 2 and partner_cb:
            partner_cb()

    w = NS
    with Sec(p, "tensor") as t_, Sec(p, "scalar") as c, \
            Sec(p, "sync") as s:
        Qp = sl4[2 * ((w - 1) % 2) + 1]
        t_.w("ve", stqfin[w - 1])
        for g2 in range(8):
            fq, tsub = g2 // 2, g2 % 2
            key = (w - 1) * 4 + fq
            tile = ((w - 1) % 2) * 2 + tsub
            bk = 2 + (g2 % 2)
            t_.w("dC", wc_t[key])
            if g2 >= 2:
                t_.w("sc", wcq[(w - 1, g2 - 2)])
            else:
                t_.w("sc", mdq[(NS - 1, 28 + bk)])
            for hc in range(32):
                t_.o(lambda e, h2=hc, k2=key % 2, b2=bk, t3=tsub:
                     e.matmul(banks[b2][:, 0:256],
                              Qp[:, h2, t3 * 128:(t3 + 1) * 128],
                              wconb[k2][:, h2, :],
                              start=(h2 == 0), stop=(h2 == 31)),
                     inc="te" if hc == 31 else None)
            wc_use[key] = p.v("te")
            if g2 % 2 == 1 and key + 2 < NS * 4:
                load_wcon(s, key + 2)
            c.w("te", p.v("te"))
            if fq == 0:
                c.w("dS", csfree[tile])
            wcq[(w - 1, g2)] = c.o(
                lambda e, f4=fq, t4=tile, b4=bk: e.activation(
                    cstage[t4][:, f4 * 256:(f4 + 1) * 256],
                    banks[b4][:, 0:256], AF.Identity), inc="sc")
        for tsub in range(2):
            tile = ((w - 1) % 2) * 2 + tsub
            s.w("sc", wcq[(w - 1, 6 + tsub)])
            row = (w - 1) * TS + tsub * 128
            csfree[tile] = s.dma(moe_dram[row:row + 128, :],
                                 cstage[tile][:])
    ph.close()



def _p4_attn(p, ctx, L, xnT, attnout, pes):
    A, G, sb = ctx["A"], ctx["G"], ctx["sb"]
    kvpK, kviK, kvp, kvi = ctx["kvpK"], ctx["kviK"], ctx["kvp"], ctx["kvi"]
    qfm_dram = ctx["qfm_dram"]
    ph = contextlib.ExitStack()
    Kt = [sb("a_K%d_%d" % (L, i), (65, SK), BF16, ph) for i in range(4)]
    Vt = [sb("a_V%d_%d" % (L, i), (128, NSK, 130), BF16, ph)
          for i in range(2)]
    Qt = [sb("a_Q%d_%d" % (L, i), (65, SC), BF16, ph) for i in range(4)]
    PT = sb("a_PT%d" % L, (128, 2048), BF16, ph)
    G["sq_bf"] = sb("a_sq%d" % L, (64, SK), BF16, ph)
    diagm = sb("a_dm%d" % L, (128, 4, 512), BF16, ph)
    G["diagm"] = diagm
    with Sec(p, "gpsimd") as g0:
        g0.dma(diagm[:], A["diagmask"].rearrange("v p n -> p v n"), "dG")
    pvs = [sb("a_pvs%d_%d" % (L, i), (64, 512), F32, ph) for i in range(2)]
    rs = [sb("a_rs%d_%d" % (L, i), (1, 1024), F32, ph) for i in range(2)]
    rsb = [sb("a_rsb%d_%d" % (L, i), (1, 512), BF16, ph) for i in range(2)]
    if L == 0:
        wqb = sb("a_wq", (128, NCH, D), BF16, ph)
        bqt = sb("a_bq", (128, NCH), F32, ph)

    with Sec(p, "vector") as v:
        for i in range(2):
            v.o(lambda e, i2=i: e.memset(Vt[i2][:, :, 64:65], 1.0))
            v.o(lambda e, i2=i: e.memset(Vt[i2][:, :, 129:130], 1.0))

    with Sec(p, "sync") as s, Sec(p, "vector") as v:
        s.w("dG", p.v("dG"))
        s.dma(G["kmax_bf"][0:1, 1:2], kvp[STATROW:STATROW + 1, 0:1])
        kd2 = s.dma(G["kmax_bf"][0:1, 2:3], kvi[STATROW:STATROW + 1, 0:1])
        v.w("dS", kd2)
        v.o(lambda e: e.tensor_copy(G["kmax"][0:1, 1:2],
                                    G["kmax_bf"][0:1, 1:2]))
        v.o(lambda e: e.tensor_copy(G["kmax"][0:1, 2:3],
                                    G["kmax_bf"][0:1, 2:3]))
        v.drain()
        kmq = v.o(lambda e: e.tensor_tensor(
            G["kmax"][0:1, 1:2], G["kmax"][0:1, 1:2], G["kmax"][0:1, 2:3],
            ALU.max), inc="ve")

    wq_t = 0
    bq8 = 0
    if L == 0:
        with Sec(p, "gpsimd") as g, Sec(p, "sync") as s, \
                Sec(p, "scalar") as c:
            p.fence(g)
            wq_t = g.dma(wqb[:],
A["wqT"].rearrange("(c p) d -> p c d", p=128), "dW")
            bq_d = s.dma(bqt[:], A["bq_r"])
            c.w("dS", bq_d)
            bq8 = c.o(lambda e: e.activation(bqt[:], bqt[:], AF.Identity,
                                             scale=RSQ), inc="sc")

    kv_pull = {}
    q_done = {}
    bq_last = [0]

    def pulls(g_idx):
        sl = g_idx % 2
        with Sec(p, "sync") as s, Sec(p, "scalar") as c:
            s.w("te", p.v("te"))
            c.w("te", p.v("te"))
            for hp in range(2):
                h = 2 * g_idx + hp
                s.dma(Kt[2 * sl + hp][:, 0:SC], kvpK[65 * h:65 * h + 65, :])
                s.dma(Kt[2 * sl + hp][:, SC:SK], kviK[65 * h:65 * h + 65, :])
            last = 0
            for j in range(NSK):
                sec = s if j % 2 == 0 else c
                src = kvp if j < 16 else kvi
                row0 = VBASE + (j % 16) * 128
                last = sec.o(
                    lambda e, s2=sl, j2=j, r0=row0, sr=src, g2=g_idx:
                    e.dma_start(
                        out=Vt[s2][:, j2, 0:130].rearrange(
                            "p (h d) -> p h d", d=65)[:, :, 0:64],
                        in_=sr[r0:r0 + 128, g2 * 128:(g2 + 1) * 128]),
                    inc=("dS" if j % 2 == 0 else "dC"), n=16)
            kv_pull[g_idx] = (p.v("dS"), p.v("dC"))

    def build_q(g_idx):
        sl = g_idx % 2
        with Sec(p, "tensor") as t_, Sec(p, "scalar") as c, \
                Sec(p, "vector") as v, Sec(p, "sync") as s:
            for hp in range(2):
                h = 2 * g_idx + hp
                qtile = Qt[2 * sl + hp]
                t_.w("ve", bq_last[0])
                if L == 0:
                    t_.w("dW", wq_t)
                    for s4 in range(4):
                        t_.w("sc", p.v("sc"))
                        for dc in range(NCH):
                            t_.o(lambda e, d=dc, g2=g_idx, ss=s4: e.matmul(
                                G["psD"][:],
                                wqb[:, d, g2 * 128:(g2 + 1) * 128],
                                xnT[:, d, ss * 512:(ss + 1) * 512],
                                start=(d == 0), stop=(d == NCH - 1)),
                                inc="te" if dc == NCH - 1 else None)
                        c.w("te", p.v("te"))
                        c.w("sc", bq8)
                        c.o(lambda e, ss=s4, hp2=hp, q2=qtile, g2=g_idx:
                            e.activation(
                                q2[0:64, ss * 512:(ss + 1) * 512],
                                G["psD"][hp2 * 64:hp2 * 64 + 64, :],
                                AF.Identity, scale=RSQ,
                                bias=bqt[hp2 * 64:hp2 * 64 + 64,
                                         g2:g2 + 1]), inc="sc")
                else:
                    dq = s.dma(qtile[0:64, :],
                               qfm_dram[g_idx * 128 + hp * 64:
                                        g_idx * 128 + hp * 64 + 64, :])
                    c.w("dS", dq)
                c.w("sc", p.v("sc"))
                c.w("te", p.v("te"))
                sq_q = c.o(lambda e, q2=qtile: e.activation(
                    G["sq_bf"][0:64, 0:SC], q2[0:64, :], AF.Square),
                    inc="sc")
                for s4 in range(4):
                    t_.w("sc", sq_q)
                    t_.w("ve", p.v("ve"))
                    tq = t_.o(lambda e, ss=s4: e.matmul(
                        G["psD"][0:1, :], G["ones64"][:],
                        G["sq_bf"][0:64, ss * 512:(ss + 1) * 512],
                        start=True, stop=True), inc="te")
                    v.w("te", tq)
                    v.w("ve", kmq)
                    q_done[(g_idx, hp)] = v.o(
                        lambda e, ss=s4, q2=qtile: e.tensor_scalar(
                            q2[64:65, ss * 512:(ss + 1) * 512],
                            G["psD"][0:1, :], G["kmax"][0:1, 1:2], -0.5,
                            ALU.add, ALU.mult), inc="ve")
                bq_last[0] = q_done[(g_idx, hp)]

    pulls(0)
    build_q(0)

    epi = {"pend": None, "tt0": 0, "tt1": 0}

    def epi_rd(t_, c, v, ci):
        c0p, hpp, pvd, g2p = epi["pend"]
        pspv = [G["psA"], G["psB"]][ci]
        c.w("te", pvd[ci])
        if epi["tt%d" % ci]:
            c.w("ve", epi["tt%d" % ci])
        epi["pq%d" % ci] = c.o(lambda e, pv=pspv, c2=ci: e.activation(
            pvs[c2][:], pv[0:64, :], AF.Identity), inc="sc")
        v.w("te", pvd[ci])
        epi["rd%d" % ci] = v.o(lambda e, pv=pspv, c2=ci: e.tensor_copy(
            rs[c2][0:1, 0:512], pv[64:65, :]), inc="ve")

    def epi_rc(t_, c, v, ci):
        v.drain()
        v.o(lambda e, c2=ci: e.reciprocal(rs[c2][0:1, 512:1024],
                                          rs[c2][0:1, 0:512]))
        v.drain()
        epi["vq%d" % ci] = v.o(lambda e, c2=ci: e.tensor_copy(
            rsb[c2][0:1, :], rs[c2][0:1, 512:1024]), inc="ve")

    def epi_a(t_, c, v, ci):
        epi_rd(t_, c, v, ci)
        epi_rc(t_, c, v, ci)

    def epi_b(t_, c, v):
        c0p, hpp, pvd, g2p = epi["pend"]
        if epi["tt1"]:
            t_.w("ve", epi["tt1"])
        tqs = [0, 0]
        for ci in range(2):
            t_.w("ve", epi["vq%d" % ci])
            tqs[ci] = t_.o(lambda e, c2=ci: e.matmul(
                G["psC"][c2 * 64:c2 * 64 + 64, :], G["ones_col"][:],
                rsb[c2][0:1, :], start=True, stop=True), inc="te")
        for ci in range(2):
            s4 = c0p + ci
            v.w("te", tqs[1])
            v.w("sc", epi["pq%d" % ci])
            epi["tt%d" % ci] = v.o(
                lambda e, hp2=hpp, g2=g2p, ss=s4, c2=ci:
                e.tensor_tensor(
                    attnout[hp2 * 64:hp2 * 64 + 64, g2,
                            ss * 512:(ss + 1) * 512],
                    pvs[c2][:], G["psC"][c2 * 64:c2 * 64 + 64, :],
                    ALU.mult),
                inc="ve")

    for g_idx in range(NH // 2):
        sl = g_idx % 2
        if g_idx + 1 < NH // 2:
            pulls(g_idx + 1)
            build_q(g_idx + 1)
        with Sec(p, "tensor") as t_, Sec(p, "scalar") as c, \
                Sec(p, "vector") as v:
            t_.w("dS", kv_pull[g_idx][0])
            t_.w("dC", kv_pull[g_idx][1])
            exq = [0, 0]
            pvq = [0, 0]
            for hp in range(2):
                h = 2 * g_idx + hp
                ktile = Kt[2 * sl + hp]
                qtile = Qt[2 * sl + hp]
                t_.w("ve", q_done[(g_idx, hp)])
                for sw in range(2):
                    c0 = 2 * sw
                    jm = [20 + 4 * c0, 20 + 4 * c0 + 4]
                    pvps = [G["psA"], G["psB"]]
                    pvd = [0, 0]

                    def emit_scores(j):
                        ph2 = j % 2
                        valid = [ci for ci in range(2) if j < jm[ci]]
                        cstart = valid[0]
                        wid = len(valid) * 512
                        t_.w("sc", exq[ph2])
                        for ci in valid:
                            t_.o(lambda e, jj=j, q4=ci, ss=c0, kt=ktile,
                                 qt=qtile, hh=ph2: e.matmul(
                                     G["psQ"][:, hh * 1024 + q4 * 512:
                                              hh * 1024 + (q4 + 1) * 512],
                                     kt[:, jj * 128:(jj + 1) * 128],
                                     qt[:, (ss + q4) * 512:
                                        (ss + q4 + 1) * 512],
                                     start=True, stop=True),
                                 inc="te" if ci == valid[-1] else None)
                        c.w("te", p.v("te"))
                        if pvq[ph2]:
                            c.w("te", pvq[ph2])
                        kill = j < 16
                        exq[ph2] = c.o(
                            lambda e, hh=ph2, k2=kill, cs=cstart, w2=wid:
                            e.activation(
                                PT[:, hh * 1024 + cs * 512:
                                   hh * 1024 + cs * 512 + w2],
                                G["psQ"][:, hh * 1024 + cs * 512:
                                         hh * 1024 + cs * 512 + w2],
                                AF.Exp,
                                bias=(G["kill_t"][:, 0:1] if k2 else 0.0)),
                            inc="sc")
                        nm = 0
                        mq = 0
                        for ci in valid:
                            jl = j - 16 - 4 * (c0 + ci)
                            if 0 <= jl <= 3:
                                if nm == 0:
                                    v.w("sc", exq[ph2])
                                nm += 1
                                mq = v.o(lambda e, hh=ph2, q4=ci, vv=jl:
                                         e.tensor_tensor(
                                             PT[:, hh * 1024 + q4 * 512:
                                                hh * 1024 + (q4 + 1) * 512],
                                             PT[:, hh * 1024 + q4 * 512:
                                                hh * 1024 + (q4 + 1) * 512],
                                             G["diagm"][:, vv, :], ALU.mult),
                                         inc="ve")
                        return (exq[ph2], mq if nm else 0)

                    def emit_pv(j, gates):
                        ph2 = j % 2
                        valid = [ci for ci in range(2) if j < jm[ci]]
                        sq2, mq = gates
                        if mq:
                            t_.w("ve", mq)
                        else:
                            t_.w("sc", sq2)
                        for ci in valid:
                            t_.o(lambda e, jj=j, q4=ci, pv=pvps[ci], s2=sl,
                                 hp2=hp, hh=ph2, jmx=jm[ci]: e.matmul(
                                     pv[0:65, :],
                                     Vt[s2][:, jj, hp2 * 65:hp2 * 65 + 65],
                                     PT[:, hh * 1024 + q4 * 512:
                                        hh * 1024 + (q4 + 1) * 512],
                                     start=(jj == 0), stop=(jj == jmx - 1)),
                                 inc="te" if ci == valid[-1] else None)
                        pvq[ph2] = p.v("te")
                        for ci in valid:
                            if j == jm[ci] - 1:
                                pvd[ci] = p.v("te")

                    pend0 = epi["pend"] is not None
                    gates = {}
                    gates[0] = emit_scores(0)
                    if pend0:
                        epi_rd(t_, c, v, 0)
                        epi_rd(t_, c, v, 1)
                        epi_rc(t_, c, v, 0)
                        epi_rc(t_, c, v, 1)
                    gates[1] = emit_scores(1)
                    if pend0:
                        t_.w("sc", epi["pq0"])
                        t_.w("ve", epi["rd0"])
                        t_.w("sc", epi["pq1"])
                        t_.w("ve", epi["rd1"])
                    emit_pv(0, gates[0])
                    gates[2] = emit_scores(2)
                    emit_pv(1, gates[1])
                    gates[3] = emit_scores(3)
                    emit_pv(2, gates[2])
                    if pend0:
                        epi_b(t_, c, v)
                    for j in range(4, jm[1]):
                        gates[j] = emit_scores(j)
                        emit_pv(j - 1, gates[j - 1])
                    emit_pv(jm[1] - 1, gates[jm[1] - 1])
                    epi["pend"] = (c0, hp, pvd[:], g_idx)

    with Sec(p, "tensor") as t_, Sec(p, "scalar") as c, \
            Sec(p, "vector") as v:
        epi_a(t_, c, v, 0)
        epi_a(t_, c, v, 1)
        epi_b(t_, c, v)
        epi["pend"] = None
        epi["tt0"] = 0
        epi["tt1"] = 0
        t_.w("ve", p.v("ve"))
    if "attnout" in ctx["dbg"] and L == 0:
        with Sec(p, "sync") as s:
            s.w("ve", p.v("ve"))
            s.dma(ctx["dbg"]["attnout"],
                  attnout[:].rearrange("p c s -> p (c s)"))
    p.barrier()
    ph.close()



def _p5_residual(p, ctx, L, attnout, x_in, x_out):
    A, G, sb = ctx["A"], ctx["G"], ctx["sb"]
    moe_dram = ctx["moe_dram"]
    ph = contextlib.ExitStack()
    wob = sb("p5wo%d" % L, (128, NCH, D), BF16, ph)
    G["biasrep"] = sb("p5br%d" % L, (128, D), F32, ph)
    G["biasrep2"] = sb("p5br2%d" % L, (128, D), F32, ph)
    xb = [sb("p5x%d_%d" % (L, i), (128, D), F32, ph) for i in range(2)]
    mb = [sb("p5m%d_%d" % (L, i), (128, D), F32, ph) for i in range(2)]
    ob = [sb("p5o%d_%d" % (L, i), (128, D), F32, ph) for i in range(2)]

    with Sec(p, "gpsimd") as g:
        p.fence(g)
        wo_t = g.dma(wob[:],
                     A["woT" if L == 0 else "ow"]
                     .rearrange("(c p) d -> p c d", p=128), "dW")
    with Sec(p, "sync") as s, Sec(p, "vector") as v:
        b1 = s.dma(G["biasrep"][:], A["bo_rep" if L == 0 else "bcon_rep"])
        bias_q = 0
        if L == 1:
            b2 = s.dma(G["biasrep2"][:], A["dbias_rep"])
            v.w("dS", b2)
            bias_q = v.o(lambda e: e.tensor_tensor(
                G["biasrep"][:], G["biasrep"][:], G["biasrep2"][:], ALU.add),
                inc="ve")

    ob_q = [0, 0]
    with Sec(p, "sync") as s, Sec(p, "scalar") as c, Sec(p, "tensor") as t_, \
            Sec(p, "vector") as v:
        t_.w("dW", wo_t)
        for t in range(NT):
            sl = t % 2
            if t >= 2:
                s.w("ve", ob_q[sl])
                c.w("ve", ob_q[sl])
            dx = s.dma(xb[sl][:], x_in[t * 128:(t + 1) * 128, :])
            dm = c.dma(mb[sl][:], moe_dram[t * 128:(t + 1) * 128, :],
                       inc="dC")
            if t >= 2:
                t_.w("ve", ob_q[sl])
            for fh in range(2):
                for dc in range(NCH):
                    lastmm = (dc == NCH - 1) and L == 1
                    t_.o(lambda e, d=dc, f2=fh, tt=t, s2=sl, lm=lastmm:
                         e.matmul(
                             G["psQ"][:, s2 * 1024 + f2 * 512:
                                      s2 * 1024 + (f2 + 1) * 512],
                             attnout[:, d, tt * 128:(tt + 1) * 128],
                             wob[:, d, f2 * 512:(f2 + 1) * 512],
                             start=(d == 0), stop=lm),
                         inc="te" if (lastmm and fh == 1) else None)
                if L == 0:
                    t_.o(lambda e, f2=fh, tt=t, s2=sl: e.matmul(
                        G["psQ"][:, s2 * 1024 + f2 * 512:
                                 s2 * 1024 + (f2 + 1) * 512],
                        G["wT_sb"][0:8, tt * 128:(tt + 1) * 128],
                        G["b3_sb"][0:8, f2 * 512:(f2 + 1) * 512],
                        start=False, stop=True),
                        inc="te" if fh == 1 else None)
            v.w("te", p.v("te"))
            v.w("dS", dx)
            v.w("dC", dm)
            v.w("ve", bias_q)
            v.o(lambda e, s2=sl: e.tensor_tensor(
                ob[s2][:], G["psQ"][:, s2 * 1024:s2 * 1024 + 1024],
                xb[s2][:], ALU.add))
            v.drain()
            v.o(lambda e, s2=sl: e.tensor_tensor(ob[s2][:], ob[s2][:],
                                                 mb[s2][:], ALU.add))
            v.drain()
            ob_q[sl] = v.o(lambda e, s2=sl: e.tensor_tensor(
                ob[s2][:], ob[s2][:], G["biasrep"][:], ALU.add), inc="ve")
            s.w("ve", ob_q[sl])
            s.dma(x_out[t * 128:(t + 1) * 128, :], ob[sl][:])
    ph.close()



def kernel(**inputs):
    shared, per_core = _host_prepare(inputs)
    nc = _build(DEBUG)
    in_maps = []
    for c in range(8):
        m = dict(shared)
        m.update(per_core[c])
        in_maps.append(m)
    res = run_bass_kernel_spmd(nc, in_maps, core_ids=list(range(8)))
    out = np.empty((4, 4096, D), np.float32)
    for c in range(8):
        b, hf = c // 2, c % 2
        out[b, hf * SC:(hf + 1) * SC, :] = res.results[c]["out_x"]
    return out
